# version 13
# baseline (speedup 1.0000x reference)
import numpy as np
import ml_dtypes
from contextlib import ExitStack
from collections import defaultdict

import concourse.bass as bass
import concourse.mybir as mybir
from concourse.alu_op_type import AluOpType as ALU
from concourse.bass_utils import run_bass_kernel_spmd

F32 = mybir.dt.float32
BF16 = mybir.dt.bfloat16
I32 = mybir.dt.int32
AF = mybir.ActivationFunctionType
AX = mybir.AxisListType
BF = ml_dtypes.bfloat16

NCORE = 8
B, S, D = 2, 4096, 2048
H, HD = 16, 128
TPC = 1024
NDC = D // 128
EPS = 1e-5
ISQD = float(1.0 / np.sqrt(128.0))

ENGINES = ("sync", "tensor", "vector", "scalar", "gpsimd")


class Prog:

    def __init__(self):
        self.nc = bass.Bass("TRN2", target_bir_lowering=False)
        self.items = {e: [] for e in ENGINES}
        self.counts = defaultdict(int)
        self.lastw = {e: defaultdict(int) for e in ENGINES}
        self.tok_engine = {}
        self.out_toks = []

    def _wait(self, eng, sem, val):
        if val > self.lastw[eng][sem]:
            self.items[eng].append(("w", sem, val))
            self.lastw[eng][sem] = val

    def emit(self, eng, fn, after=(), inc=1, sem=None):
        if inc == 1 and eng != "tensor":
            v = self.counts["e_" + eng]
            if v > 0:
                self._wait(eng, "e_" + eng, v)
        if inc == 0:
            for t in after:
                if t is None or self.tok_engine.get(t) == eng:
                    continue
                self._wait(eng, t[0], t[1])
            self.items[eng].append(("q", fn))
            return None
        for t in after:
            if t is None:
                continue
            self._wait(eng, t[0], t[1])
        if sem is None:
            sem = ("d_" if inc == 16 else "e_") + eng
        self.counts[sem] += inc
        tok = (sem, self.counts[sem])
        self.items[eng].append(("i", fn, sem, inc))
        self.tok_engine[tok] = eng
        return tok

    def dma(self, eng, out, in_, after=(), slot=None, **kw):
        sem = "ds_" + (slot if slot else "anon_" + eng)
        t = self.emit(eng, (lambda e, o=out, i=in_, kw=kw: e.dma_start(out=o, in_=i, **kw)),
                      after=after, inc=16, sem=sem)
        self.slot_last = getattr(self, "slot_last", {})
        self.slot_last[sem] = t
        return t

    def igather(self, dst, src_ap, idx_ap, after=(), slot="ig"):
        sem = "ds_" + slot
        t = self.emit("gpsimd",
                      (lambda e, o=dst, i=src_ap, x=idx_ap:
                       e.indirect_dma_start(out=o, out_offset=None, in_=i,
                                            in_offset=bass.IndirectOffsetOnAxis(x, 0))),
                      after=after, inc=16, sem=sem)
        self.slot_last = getattr(self, "slot_last", {})
        self.slot_last[sem] = t
        return t

    def out_dma(self, eng, out, in_, after=(), slot=None, **kw):
        t = self.dma(eng, out, in_, after=after, slot=(slot or "out_" + eng), **kw)
        self.out_toks.append(t)
        return t

    def barrier(self, exclude=("cc",)):
        snap = dict(self.counts)
        for eng in ENGINES:
            for sem, val in snap.items():
                if sem in exclude:
                    continue
                self._wait(eng, sem, val)

    def check(self):
        pos = {e: 0 for e in ENGINES}
        cnt = defaultdict(int)
        progressed = True
        while progressed:
            progressed = False
            for e in ENGINES:
                while pos[e] < len(self.items[e]):
                    it = self.items[e][pos[e]]
                    if it[0] == "w" and cnt[it[1]] < it[2]:
                        break
                    if it[0] == "i":
                        cnt[it[2]] += it[3]
                    pos[e] += 1
                    progressed = True
        stuck = {e: (pos[e], len(self.items[e])) for e in ENGINES if pos[e] < len(self.items[e])}
        if stuck:
            raise RuntimeError(f"build-time deadlock at {stuck}")

    def render(self, ctx):
        fin = {}
        for t in self.out_toks:
            fin[t[0]] = max(fin.get(t[0], 0), t[1])
        for s, v in sorted(fin.items()):
            self._wait("gpsimd", s, v)
        self.check()
        sems = {s: ctx.enter_context(self.nc.semaphore(s)) for s in sorted(self.counts)}
        block = ctx.enter_context(self.nc.Block())

        def body(eng_name):
            def _(eng):
                for it in self.items[eng_name]:
                    if it[0] == "w":
                        eng.wait_ge(sems[it[1]], it[2])
                    elif it[0] == "q":
                        it[1](eng)
                    else:
                        it[1](eng).then_inc(sems[it[2]], it[3])
            return _

        for e in ENGINES:
            getattr(block, e)(body(e))


class Guard:

    def __init__(self):
        self.readers = []
        self.writer = None

    def rd(self, tok):
        self.readers.append(tok)

    def war(self):
        return list(self.readers) + ([self.writer] if self.writer else [])

    def wrote(self, tok):
        self.writer = tok
        self.readers = []


class Banks:

    def __init__(self, p, ctx, nf32=6, nbf=2):
        self.p = p
        self.t = {}
        self.wr = {}
        self.rd = {}
        self.pools = {"f": [], "b": []}
        self.i = {"f": 0, "b": 0}
        for k in range(nf32):
            n = f"pbf{k}"
            self.t[n] = ctx.enter_context(p.nc.psum_tensor(n, [128, 512], F32))
            self.wr[n] = None
            self.rd[n] = None
            self.pools["f"].append(n)
        for k in range(nbf):
            n = f"pbb{k}"
            self.t[n] = ctx.enter_context(p.nc.psum_tensor(n, [128, 1024], BF16))
            self.wr[n] = None
            self.rd[n] = None
            self.pools["b"].append(n)

    def take(self, kind="f"):
        pool = self.pools[kind]
        n = pool[self.i[kind]]
        self.i[kind] = (self.i[kind] + 1) % len(pool)
        return n

    def matmuls(self, n, out_ap, mms, extra_after=()):
        p = self.p
        after = [self.rd[n]] + list(extra_after)
        tok = None
        for j, (lhsT, rhs) in enumerate(mms):
            last = (j == len(mms) - 1)
            tok = p.emit("tensor",
                         (lambda e, o=out_ap, l=lhsT, r=rhs, st=(j == 0), sp=last:
                          e.matmul(o, l, r, start=st, stop=sp)),
                         after=after if j == 0 else (), inc=1 if last else 0)
        self.wr[n] = tok
        return tok

    def transpose(self, n, out_ap, src, ident, extra_after=()):
        p = self.p
        tok = p.emit("tensor",
                     (lambda e, o=out_ap, s=src, idn=ident: e.transpose(o, s, idn)),
                     after=[self.rd[n]] + list(extra_after))
        self.wr[n] = tok
        return tok

    def evac(self, n, eng, fn, after=()):
        tok = self.p.emit(eng, fn, after=[self.wr[n]] + list(after))
        self.rd[n] = tok
        return tok



def ts(out, in0, s1, op0, s2=None, op1=None):
    kw = {} if op1 is None else {"op1": op1}
    return (lambda e, o=out, i=in0, a=s1, b=s2, x=op0, kw=kw:
            e.tensor_scalar(out=o, in0=i, scalar1=a, scalar2=b, op0=x, **kw))


def tt(out, in0, in1, op):
    return (lambda e, o=out, a=in0, b=in1, x=op: e.tensor_tensor(out=o, in0=a, in1=b, op=x))


def stt(out, in0, scalar, in1, op0, op1):
    return (lambda e, o=out, i=in0, s=scalar, j=in1, x=op0, y=op1:
            e.scalar_tensor_tensor(out=o, in0=i, scalar=s, in1=j, op0=x, op1=y))


def act(out, in_, func, bias=None, scale=None, accum=None):
    kw = {}
    if bias is not None:
        kw["bias"] = bias
    if scale is not None:
        kw["scale"] = scale
    if accum is not None:
        kw["accum_out"] = accum
    return (lambda e, o=out, i=in_, f=func, kw=kw: e.activation(o, i, f, **kw))


def scp(out, in_):
    return (lambda e, o=out, i=in_: e.copy(o, i))


def cp(out, in_):
    return (lambda e, o=out, i=in_: e.tensor_copy(o, i))


def red(out, in_, op, absval=None, negate=None):
    kw = {}
    if absval:
        kw["apply_absolute_value"] = True
    if negate:
        kw["negate"] = True
    return (lambda e, o=out, i=in_, op=op, kw=kw:
            e.tensor_reduce(out=o, in_=i, axis=AX.X, op=op, **kw))


def recip(out, in_):
    return (lambda e, o=out, i=in_: e.reciprocal(o, i))


def mset(ap, v):
    return (lambda e, a=ap, v=v: e.memset(a, v))


def bnstats(out, in_):
    return (lambda e, o=out, i=in_: e.bn_stats(o, i))


def bnaggr(out, in_):
    return (lambda e, o=out, i=in_: e.bn_aggr(o, i))



def pre0_dram(nc, inter=False):
    io = {}
    io["x"] = nc.dram_tensor("x", [TPC, D], F32, kind="ExternalInput").ap()
    for w in ("wqt", "wkt", "wvt"):
        io[w] = nc.dram_tensor(w, [D, D], F32, kind="ExternalInput").ap()
    io["a_dt"] = nc.dram_tensor("a_dt", [D, 512], F32, kind="ExternalInput").ap()
    io["b_allt"] = nc.dram_tensor("b_allt", [512, D], F32, kind="ExternalInput").ap()
    io["wrt"] = nc.dram_tensor("wrt", [D, 8], F32, kind="ExternalInput").ap()
    io["br"] = nc.dram_tensor("br", [1, 8], F32, kind="ExternalInput").ap()
    io["esel"] = nc.dram_tensor("esel", [8, 512], BF16, kind="ExternalInput").ap()
    io["idf"] = nc.dram_tensor("idf", [128, 128], F32, kind="ExternalInput").ap()
    io["idb"] = nc.dram_tensor("idb", [128, 128], BF16, kind="ExternalInput").ap()
    dt = BF16 if inter else F32
    kind = {} if inter else {"kind": "ExternalOutput"}
    io["qk_out"] = nc.dram_tensor("qk_out", [4096, TPC], dt, **kind).ap()
    io["v_out"] = nc.dram_tensor("v_out", [4096, 512], dt, **kind).ap()
    if not inter:
        io["m1_out"] = nc.dram_tensor("m1_out", [D, TPC], F32, kind="ExternalOutput").ap()
    return io


def build_pre0(phases="ABC"):
    p = Prog()
    io = pre0_dram(p.nc)
    with ExitStack() as ctx:
        resid = ctx.enter_context(p.nc.sbuf_tensor("resid", [128, NDC * TPC], F32))
        bk = Banks(p, ctx)
        emit_pre0(p, ctx, bk, io, resid, fused=False)
        p.render(ctx)
    return p.nc


def emit_pre0(p, ctx, bk, io, resid, fused, qkv_done_cb=None):
    nc = p.nc
    x_in = io["x"]
    wts = {w: io[w] for w in ("wqt", "wkt", "wvt")}
    a_dt, b_allt, wrt, br, esel = io["a_dt"], io["b_allt"], io["wrt"], io["br"], io["esel"]
    idf, idb = io["idf"], io["idb"]
    qk_out, v_out = io["qk_out"], io["v_out"]
    m1_out = io.get("m1_out")
    EVDT = BF16 if fused else F32
    if True:
        sb = lambda n, s, d: ctx.enter_context(nc.sbuf_tensor(n, s, d, allow_name_mangling=True))
        xq_fm = sb("xq_fm", [128, NDC * TPC], BF16)
        invsx_c = sb("invsx_c", [128, 8], F32)
        invsx_b = sb("invsx_b", [128, TPC], F32)
        invsx_r = sb("invsx_r", [1, TPC], F32)
        c_idf = sb("c_idf", [128, 128], F32)
        c_idb = sb("c_idb", [128, 128], BF16)
        ones_cf = sb("ones_cf", [128, 1], F32)
        ones_rf = sb("ones_rf", [1, 128], F32)
        ones_rb = sb("ones_rb", [1, 128], BF16)

        cAB = ExitStack()
        xn_fm = cAB.enter_context(nc.sbuf_tensor("xn_fm", [128, NDC * TPC], BF16))

        p.dma("sync", c_idf[:, :], idf, slot="init")
        t_init = p.dma("sync", c_idb[:, :], idb, slot="init")
        t_idf = t_idb = t_init
        t_ocf = p.emit("gpsimd", mset(ones_cf[:, :], 1.0))
        t_orf = p.emit("gpsimd", mset(ones_rf[:, :], 1.0))
        t_orb = p.emit("gpsimd", mset(ones_rb[:, :], 1.0))

        with ExitStack() as c2:
            sb2 = lambda n, s, d: c2.enter_context(nc.sbuf_tensor(n, s, d, allow_name_mangling=True))
            t_x = [sb2(f"t_x{i}", [128, D], F32) for i in range(2)]
            t_xn = sb2("t_xn", [128, D], F32)
            t_v = sb2("t_v", [128, D], F32)
            t_i = sb2("t_i", [128, D], I32)
            t_kf = sb2("t_kf", [128, D], BF16)
            t_ca = sb2("t_ca", [128, D], BF16)
            t_cb = sb2("t_cb", [128, D], BF16)
            t_xq = sb2("t_xq", [128, D], BF16)
            t_xnb = sb2("t_xnb", [128, D], BF16)
            t_bn = sb2("t_bn", [128, 24], F32)
            t_mv = sb2("t_mv", [128, 2], F32)
            t_s = [sb2(f"t_s{i}", [128, 1], F32) for i in range(6)]
            g_x = [Guard(), Guard()]
            g_xq, g_xnb, g_s1, g_s2 = Guard(), Guard(), Guard(), Guard()
            ivr_toks = []

            for tt_ in range(8):
                xb = t_x[tt_ % 2]
                gx = g_x[tt_ % 2]
                tok_x = p.dma("sync", xb[:, :], x_in[tt_ * 128:(tt_ + 1) * 128, :],
                              after=gx.war(), slot=f"x{tt_ % 2}")
                gx.wrote(tok_x)
                for dc in range(NDC):
                    n = bk.take("f")
                    w = bk.transpose(n, bk.t[n][:, 0:128].bitcast(F32),
                                     xb[:, dc * 128:(dc + 1) * 128], c_idf[:, :],
                                     extra_after=[tok_x, t_idf])
                    gx.rd(w)
                    e = "scalar" if dc % 2 else "vector"
                    dst = resid[:, dc * TPC + tt_ * 128: dc * TPC + (tt_ + 1) * 128]
                    fn = scp(dst, bk.t[n][:, 0:128]) if e == "scalar" else cp(dst, bk.t[n][:, 0:128])
                    bk.evac(n, e, fn)
                for qq in range(4):
                    a1 = p.emit("vector", bnstats(t_bn[:, qq * 6:(qq + 1) * 6],
                                                  xb[:, qq * 512:(qq + 1) * 512]),
                                after=[tok_x])
                    gx.rd(a1)
                p.emit("vector", bnaggr(t_mv[:, :], t_bn[:, :]))
                a3 = p.emit("vector", ts(t_s[0][:, :], t_mv[:, 1:2], EPS, ALU.add))
                a4 = p.emit("vector", recip(t_s[1][:, :], t_s[0][:, :]), after=g_s1.war())
                g_s1.wrote(a4)
                a5 = p.emit("scalar", act(t_s[2][:, :], t_s[1][:, :], AF.Sqrt),
                            after=[a4] + g_s2.war())
                g_s1.rd(a5)
                g_s2.wrote(a5)
                a6 = p.emit("vector", ts(t_xn[:, :], xb[:, :], t_mv[:, 0:1], ALU.subtract,
                                         t_s[2][:, :], ALU.mult), after=[a5])
                g_s2.rd(a6)
                gx.rd(a6)
                a7 = p.emit("scalar", scp(t_xnb[:, :], t_xn[:, :]),
                            after=[a6] + g_xnb.war())
                g_xnb.wrote(a7)
                p.emit("vector", red(t_s[3][:, :], t_xn[:, :], ALU.max, absval=True),
                       after=[a6])
                p.emit("vector", ts(t_s[3][:, :], t_s[3][:, :], 1e-5, ALU.max))
                p.emit("vector", recip(t_s[4][:, :], t_s[3][:, :]))
                p.emit("vector", ts(t_s[5][:, :], t_s[4][:, :], 127.0, ALU.mult))
                ivc = p.emit("vector", ts(invsx_c[:, tt_:tt_ + 1], t_s[3][:, :],
                                          1.0 / 127.0, ALU.mult))
                p.emit("vector", ts(t_v[:, :], t_xn[:, :], t_s[5][:, :], ALU.mult),
                       after=[a7])
                p.emit("vector", cp(t_i[:, :], t_v[:, :]))
                p.emit("vector", cp(t_kf[:, :], t_i[:, :]))
                p.emit("vector", tt(t_v[:, :], t_v[:, :], t_kf[:, :], ALU.subtract))
                p.emit("vector", stt(t_ca[:, :], t_v[:, :], 0.5, t_kf[:, :],
                                     ALU.is_gt, ALU.add))
                p.emit("vector", ts(t_cb[:, :], t_v[:, :], -0.5, ALU.is_lt))
                p.emit("vector", tt(t_xq[:, :], t_ca[:, :], t_cb[:, :], ALU.subtract),
                       after=g_xq.war())
                p.emit("vector", ts(t_i[:, :], t_i[:, :], 1, ALU.bitwise_and))
                p.emit("vector", stt(t_ca[:, :], t_v[:, :], 0.5, t_i[:, :],
                                     ALU.is_equal, ALU.mult))
                p.emit("vector", stt(t_cb[:, :], t_v[:, :], -0.5, t_i[:, :],
                                     ALU.is_equal, ALU.mult))
                p.emit("vector", tt(t_xq[:, :], t_xq[:, :], t_ca[:, :], ALU.add))
                aq = p.emit("vector", tt(t_xq[:, :], t_xq[:, :], t_cb[:, :], ALU.subtract))
                g_xq.wrote(aq)
                for dc in range(NDC):
                    n = bk.take("b")
                    w = bk.transpose(n, bk.t[n][:, 0:128],
                                     t_xq[:, dc * 128:(dc + 1) * 128], c_idb[:, :],
                                     extra_after=[aq, t_idb])
                    g_xq.rd(w)
                    e = "scalar" if dc % 2 else "vector"
                    dst = xq_fm[:, dc * TPC + tt_ * 128: dc * TPC + (tt_ + 1) * 128]
                    fn = scp(dst, bk.t[n][:, 0:128]) if e == "scalar" else cp(dst, bk.t[n][:, 0:128])
                    bk.evac(n, e, fn)
                    n = bk.take("b")
                    w = bk.transpose(n, bk.t[n][:, 0:128],
                                     t_xnb[:, dc * 128:(dc + 1) * 128], c_idb[:, :],
                                     extra_after=[a7, t_idb])
                    g_xnb.rd(w)
                    dst = xn_fm[:, dc * TPC + tt_ * 128: dc * TPC + (tt_ + 1) * 128]
                    fn = scp(dst, bk.t[n][:, 0:128]) if e == "scalar" else cp(dst, bk.t[n][:, 0:128])
                    bk.evac(n, e, fn)
                n = bk.take("f")
                w = bk.transpose(n, bk.t[n][0:1, 0:128].bitcast(F32),
                                 invsx_c[:, tt_:tt_ + 1], c_idf[:, :],
                                 extra_after=[ivc, t_idf])
                ivr_toks.append(bk.evac(n, "vector",
                                        cp(invsx_r[0:1, tt_ * 128:(tt_ + 1) * 128],
                                           bk.t[n][0:1, 0:128])))
            for cc in range(2):
                n = bk.take("f")
                mm = [(ones_rf[0:1, :], invsx_r[0:1, cc * 512:(cc + 1) * 512])]
                bk.matmuls(n, bk.t[n][:, 0:512], mm, extra_after=ivr_toks + [t_orf])
                bk.evac(n, "vector", cp(invsx_b[:, cc * 512:(cc + 1) * 512],
                                        bk.t[n][:, 0:512]))

        p.barrier()
        NBLK, BW = 8, 256
        with ExitStack() as c2:
            sb2 = lambda n, s, d: c2.enter_context(nc.sbuf_tensor(n, s, d, allow_name_mangling=True))
            t_w = [sb2(f"t_w{i}", [128, NDC * BW], F32) for i in range(2)]
            t_b1 = sb2("t_b1", [128, NDC * BW], BF16)
            t_tern2 = [sb2(f"t_tern{i}", [128, NDC * BW], BF16) for i in range(2)]
            t_acc = sb2("t_acc", [128, NBLK], F32)
            t_g = [sb2(f"t_g{i}", [1, 1], F32) for i in range(3)]
            t_thr = sb2("t_thr", [128, 1], F32)
            t_nthr = sb2("t_nthr", [128, 1], F32)
            t_sw = sb2("t_sw", [128, 1], F32)
            t_ev = [sb2(f"t_ev{i}", [128, 512], EVDT) for i in range(4)]
            g_w = [Guard(), Guard()]
            g_tern2 = [Guard(), Guard()]
            g_acc, g_b1 = Guard(), Guard()
            g_ev = [Guard() for _ in range(4)]

            for wi, wname in enumerate(("wqt", "wkt", "wvt")):
                wT = wts[wname]
                acc_toks = []
                for blk in range(NBLK):
                    wb = t_w[blk % 2]
                    gw = g_w[blk % 2]
                    tok = p.dma("sync", wb[:, :].rearrange("p (ec o) -> p ec o", ec=NDC),
                                wT[:, blk * BW:(blk + 1) * BW]
                                .rearrange("(ec p) o -> p ec o", p=128),
                                after=gw.war(), slot=f"w{blk % 2}")
                    gw.wrote(tok)
                    a = p.emit("scalar", act(t_b1[:, :], wb[:, :], AF.Abs,
                                             accum=t_acc[:, blk:blk + 1]),
                               after=[tok] + (g_acc.war() + g_b1.war() if blk == 0 else []))
                    gw.rd(a)
                    if blk == 0:
                        g_b1.wrote(a)
                    acc_toks.append(a)
                g_acc.wrote(acc_toks[-1])
                g_b1.wrote(acc_toks[-1])
                s1 = p.emit("vector", red(t_thr[:, :], t_acc[:, :], ALU.add),
                            after=acc_toks)
                g_acc.rd(s1)
                n = bk.take("f")
                bk.matmuls(n, bk.t[n][0:1, 0:1], [(t_thr[:, :], ones_cf[:, :])],
                           extra_after=[s1, t_ocf])
                bk.evac(n, "vector", ts(t_g[0][0:1, :], bk.t[n][0:1, 0:1],
                                        1.0 / (D * D), ALU.mult))
                p.emit("vector", ts(t_g[1][0:1, :], t_g[0][0:1, :], 1e-5, ALU.max))
                g3 = p.emit("vector", ts(t_g[2][0:1, :], t_g[1][0:1, :], 0.5, ALU.mult))
                n = bk.take("f")
                bk.matmuls(n, bk.t[n][:, 0:1], [(ones_rf[0:1, :], t_g[1][0:1, :])],
                           extra_after=[g3, t_orf])
                scale_const = ISQD if wi == 0 else 1.0
                bk.evac(n, "vector", ts(t_sw[:, :], bk.t[n][:, 0:1], scale_const, ALU.mult))
                n = bk.take("f")
                bk.matmuls(n, bk.t[n][:, 0:1], [(ones_rf[0:1, :], t_g[2][0:1, :])],
                           extra_after=[g3, t_orf])
                bk.evac(n, "vector", cp(t_thr[:, :], bk.t[n][:, 0:1]))
                p.emit("vector", ts(t_nthr[:, :], t_thr[:, :], -1.0, ALU.mult))

                for blk in range(NBLK):
                    wb = t_w[blk % 2]
                    gw = g_w[blk % 2]
                    tok = p.dma("sync", wb[:, :].rearrange("p (ec o) -> p ec o", ec=NDC),
                                wT[:, blk * BW:(blk + 1) * BW]
                                .rearrange("(ec p) o -> p ec o", p=128),
                                after=gw.war(), slot=f"w{blk % 2}")
                    gw.wrote(tok)
                    t_tern = t_tern2[blk % 2]
                    g_tern = g_tern2[blk % 2]
                    b1 = p.emit("vector", ts(t_b1[:, :], wb[:, :], t_nthr[:, :], ALU.is_lt),
                                after=[tok] + g_b1.war())
                    g_b1.wrote(b1)
                    gw.rd(b1)
                    tern = p.emit("vector", stt(t_tern[:, :], wb[:, :], t_thr[:, :],
                                                t_b1[:, :], ALU.is_gt, ALU.subtract),
                                  after=g_tern.war())
                    gw.rd(tern)
                    g_b1.rd(tern)
                    g_tern.wrote(tern)
                    if wi < 2:
                        for ob in range(2):
                            for cc_ in range(2):
                                n = bk.take("f")
                                mms = [(t_tern[:, dc * BW + ob * 128:dc * BW + (ob + 1) * 128],
                                        xq_fm[:, dc * TPC + cc_ * 512:dc * TPC + cc_ * 512 + 512])
                                       for dc in range(NDC)]
                                m = bk.matmuls(n, bk.t[n][:, 0:512], mms,
                                               extra_after=[tern])
                                g_tern.rd(m)
                                k = (ob * 2 + cc_) % 4
                                ev = t_ev[k]
                                evt = bk.evac(n, "vector",
                                              stt(ev[:, :], bk.t[n][:, 0:512], t_sw[:, :],
                                                  invsx_b[:, cc_ * 512:cc_ * 512 + 512],
                                                  ALU.mult, ALU.mult),
                                              after=g_ev[k].war())
                                g_ev[k].wrote(evt)
                                h = blk * 2 + ob
                                rbase = (h // 4) * 1024 + (0 if wi == 0 else 512) + (h % 4) * 128
                                dt_ = p.out_dma("sync",
                                                qk_out[rbase:rbase + 128,
                                                       cc_ * 512:cc_ * 512 + 512],
                                                ev[:, :], after=[evt], slot=f"ev{k}")
                                g_ev[k].rd(dt_)
                    else:
                        for tt_ in range(8):
                            n = bk.take("f")
                            mms = [(xq_fm[:, dc * TPC + tt_ * 128:dc * TPC + (tt_ + 1) * 128],
                                    t_tern[:, dc * BW:(dc + 1) * BW])
                                   for dc in range(NDC)]
                            m = bk.matmuls(n, bk.t[n][:, 0:BW], mms, extra_after=[tern])
                            g_tern.rd(m)
                            k = tt_ % 4
                            ev = t_ev[k]
                            evt = bk.evac(n, "vector",
                                          ts(ev[:, 0:BW], bk.t[n][:, 0:BW],
                                             invsx_c[:, tt_:tt_ + 1], ALU.mult,
                                             t_sw[:, :], ALU.mult),
                                          after=g_ev[k].war())
                            g_ev[k].wrote(evt)
                            rbase = (blk // 2) * 1024 + tt_ * 128
                            cbase = (blk % 2) * 256
                            dt_ = p.out_dma("sync",
                                            v_out[rbase:rbase + 128, cbase:cbase + BW],
                                            ev[:, 0:BW], after=[evt], slot=f"ev{k}")
                            g_ev[k].rd(dt_)

        if qkv_done_cb is not None:
            qkv_done_cb(p)
        p.barrier()
        resid_toks = {}
        with ExitStack() as c2:
            sb2 = lambda n, s, d: c2.enter_context(nc.sbuf_tensor(n, s, d, allow_name_mangling=True))
            t_A = sb2("t_A", [128, NDC * 512], BF16)
            t_B = sb2("t_B", [128, 4 * D], BF16)
            t_W = sb2("t_W", [128, NDC * 8], BF16)
            t_br = sb2("t_br", [1, 8], BF16)
            t_E = sb2("t_E", [8, 512], BF16)
            t_mwT = sb2("t_mwT", [8, TPC], BF16)
            t_hm = sb2("t_hm", [128, 4 * TPC], BF16)
            t_bc = [sb2(f"t_bc{i}", [128, 512], F32) for i in range(2)]
            t_e8 = sb2("t_e8", [128, 8], F32)
            t_mw = sb2("t_mw", [128, 8], F32)
            t_u = [sb2(f"t_u{i}", [128, 1], F32) for i in range(4)]
            g_e8, g_mw, g_u0 = Guard(), Guard(), Guard()
            g_bc = [Guard(), Guard()]

            t_fst = sb2("t_fst", [128, NDC * 256], F32)
            prev = None
            for half in range(2):
                dd = p.dma("sync",
                           t_fst[:, :].rearrange("p (dc r) -> p dc r", dc=NDC),
                           a_dt[:, half * 256:(half + 1) * 256]
                           .rearrange("(dc p) r -> p dc r", p=128),
                           after=[prev], slot="binit")
                prev = p.emit("vector",
                              cp(t_A[:, :].rearrange("p (dc r) -> p dc r", dc=NDC)
                                 [:, :, half * 256:(half + 1) * 256],
                                 t_fst[:, :].rearrange("p (dc r) -> p dc r", dc=NDC)),
                              after=[dd])
            for half in range(2):
                dd = p.dma("sync",
                           t_fst[:, :].rearrange("p (ec o) -> p ec o", ec=2),
                           b_allt[half * 256:(half + 1) * 256, :]
                           .rearrange("(ec p) o -> p ec o", p=128),
                           after=[prev], slot="binit")
                prev = p.emit("vector",
                              cp(t_B[:, half * 2 * D:(half + 1) * 2 * D], t_fst[:, :]),
                              after=[dd])
            d3 = p.dma("sync", t_fst[:, 0:NDC * 8].rearrange("p (dc e) -> p dc e", dc=NDC),
                       wrt.rearrange("(dc p) e -> p dc e", p=128),
                       after=[prev], slot="binit")
            c3 = p.emit("vector", cp(t_W[:, :], t_fst[:, 0:NDC * 8]), after=[d3])
            d4 = p.dma("sync", t_fst[0:1, 0:8], br, after=[c3], slot="binit")
            c4 = p.emit("vector", cp(t_br[:, :], t_fst[0:1, 0:8]), after=[d4])
            t_ld = p.dma("sync", t_E[:, :], esel, slot="binit2")
            t_binit = p.emit("vector", cp(t_E[0:1, 0:1], t_E[0:1, 0:1]), after=[c4, t_ld])
            tA = tB = tW = tbr = tE = t_binit

            mwT_toks = []
            for tt_ in range(8):
                n = bk.take("f")
                mms = [(xn_fm[:, dc * TPC + tt_ * 128:dc * TPC + (tt_ + 1) * 128],
                        t_W[:, dc * 8:(dc + 1) * 8]) for dc in range(NDC)]
                mms.append((ones_rb[0:1, :], t_br[0:1, :]))
                mtok = bk.matmuls(n, bk.t[n][:, 0:8], mms,
                                  extra_after=[tW, tbr, t_orb])
                nm = p.emit("vector", red(t_u[0][:, :], bk.t[n][:, 0:8], ALU.max,
                                          negate=True), after=[mtok] + g_u0.war())
                g_u0.wrote(nm)
                ex = p.emit("scalar", act(t_e8[:, :], bk.t[n][:, 0:8], AF.Exp,
                                          bias=t_u[0][:, :]),
                            after=[mtok, nm] + g_e8.war())
                g_u0.rd(ex)
                g_e8.wrote(ex)
                bk.rd[n] = ex
                p.emit("vector", red(t_u[1][:, :], t_e8[:, :], ALU.add), after=[ex])
                mx = p.emit("vector", red(t_u[2][:, :], t_e8[:, :], ALU.max), after=[ex])
                p.emit("vector", recip(t_u[3][:, :], t_u[1][:, :]))
                p.emit("vector", stt(t_u[3][:, :], t_u[3][:, :], 1.0 / 64.0, t_u[2][:, :],
                                     ALU.mult, ALU.mult))
                mw = p.emit("vector", ts(t_mw[:, :], t_e8[:, :], t_u[2][:, :], ALU.is_ge,
                                         t_u[3][:, :], ALU.mult), after=g_mw.war())
                g_e8.rd(mw)
                g_mw.wrote(mw)
                n3 = bk.take("f")
                w = bk.transpose(n3, bk.t[n3][0:8, 0:128].bitcast(F32), t_mw[:, :],
                                 c_idf[:, :], extra_after=[mw, t_idf])
                g_mw.rd(w)
                mwT_toks.append(bk.evac(n3, "vector",
                                        cp(t_mwT[0:8, tt_ * 128:(tt_ + 1) * 128],
                                           bk.t[n3][0:8, 0:128])))
            hm_toks = []
            for ec in range(4):
                for cc_ in range(2):
                    j = (ec * 2 + cc_) % 2
                    n = bk.take("f")
                    mms = [(t_A[:, dc * 512 + ec * 128:dc * 512 + (ec + 1) * 128],
                            xn_fm[:, dc * TPC + cc_ * 512:dc * TPC + cc_ * 512 + 512])
                           for dc in range(NDC)]
                    bk.matmuls(n, bk.t[n][:, 0:512], mms, extra_after=[tA])
                    n3 = bk.take("f")
                    bk.matmuls(n3, bk.t[n3][:, 0:512],
                               [(t_E[0:8, ec * 128:(ec + 1) * 128],
                                 t_mwT[0:8, cc_ * 512:cc_ * 512 + 512])],
                               extra_after=[tE] + mwT_toks)
                    bc = bk.evac(n3, "scalar", scp(t_bc[j][:, :], bk.t[n3][:, 0:512]),
                                 after=g_bc[j].war())
                    g_bc[j].wrote(bc)
                    hm = bk.evac(n, "vector",
                                 tt(t_hm[:, ec * TPC + cc_ * 512:ec * TPC + cc_ * 512 + 512],
                                    bk.t[n][:, 0:512], t_bc[j][:, :], ALU.mult),
                                 after=[bc])
                    g_bc[j].rd(hm)
                    hm_toks.append(hm)
            for oc in range(NDC):
                for cc_ in range(2):
                    n = bk.take("f")
                    mms = [(t_B[:, ec * D + oc * 128:ec * D + (oc + 1) * 128],
                            t_hm[:, ec * TPC + cc_ * 512:ec * TPC + cc_ * 512 + 512])
                           for ec in range(4)]
                    bk.matmuls(n, bk.t[n][:, 0:512], mms, extra_after=[tB] + hm_toks)
                    sl = resid[:, oc * TPC + cc_ * 512: oc * TPC + cc_ * 512 + 512]
                    resid_toks[(oc, cc_)] = bk.evac(
                        n, "vector", tt(sl, bk.t[n][:, 0:512], sl, ALU.add))

        cAB.close()
        if not fused:
            for dc in range(NDC):
                p.out_dma("sync", m1_out[dc * 128:(dc + 1) * 128, :],
                          resid[:, dc * TPC:(dc + 1) * TPC],
                          after=[resid_toks[(dc, 0)], resid_toks[(dc, 1)]])





def build_attn(l0: bool):
    p = Prog()
    nc = p.nc
    io = {}
    io["qin"] = nc.dram_tensor("qin", [512, S], F32, kind="ExternalInput").ap()
    io["kin"] = nc.dram_tensor("kin", [512, S], F32, kind="ExternalInput").ap()
    io["vin"] = nc.dram_tensor("vin", [S, 512], F32, kind="ExternalInput").ap()
    io["maskin"] = nc.dram_tensor("maskin", [512, 512], BF16, kind="ExternalInput").ap()
    io["idf"] = nc.dram_tensor("idf", [128, 128], F32, kind="ExternalInput").ap()
    if l0:
        io["o_out"] = nc.dram_tensor("o_out", [S, 512], F32, kind="ExternalOutput").ap()
    else:
        io["o_out"] = nc.dram_tensor("o_out", [512, S], F32, kind="ExternalOutput").ap()
    with ExitStack() as ctx:
        bk = Banks(p, ctx, nf32=6, nbf=0)
        emit_attn(p, ctx, bk, io, l0, loader=None, o_dt=F32)
        p.render(ctx)
    return p.nc


def emit_attn(p, ctx, bk, io, l0, loader, o_dt):
    nc = p.nc
    maskin, idf = io["maskin"], io["idf"]
    o_out = io["o_out"]
    if True:
        sb = lambda n, s, d: ctx.enter_context(nc.sbuf_tensor(n, s, d, allow_name_mangling=True))
        all_f = list(bk.pools["f"])
        bk.pools["f"] = all_f[:4]
        bk.i["f"] = 0
        sb_q = sb("sb_q", [128, 4 * S], BF16)
        sb_k = sb("sb_k", [128, 4 * S], BF16)
        sb_v = sb("sb_v", [128, 32 * 512], BF16)
        sb_m = sb("sb_m", [128, 4 * 512], BF16)
        sb_sq = sb("sb_sq", [128, 512], BF16)
        sb_P = sb("sb_P", [128, 4 * 512], BF16)
        sb_li = sb("sb_li", [1, 512], F32)
        sb_negc = sb("sb_negc", [128, 4], F32)
        sb_u = [sb(f"sb_u{i}", [1, 1], F32) for i in range(3)]
        sb_of = sb("sb_of", [128, 512], F32)
        sb_otm = sb("sb_otm", [128, 4 * 128], F32)
        sb_liT = sb("sb_liT", [128, 4], F32)
        sb_ob = sb("sb_ob", [128, 512], F32)
        sb_obf = sb("sb_obf", [128, 512], o_dt)
        c_idf = sb("c_idf", [128, 128], F32)
        ones_cb = sb("ones_cb", [128, 1], BF16)
        ones_rf = sb("ones_rf", [1, 128], F32)

        if loader is None:
            p.dma("gpsimd", sb_q[:, :].rearrange("p (ph t) -> p ph t", ph=4),
                  io["qin"].rearrange("(ph p) t -> p ph t", p=128), slot="init")
            p.dma("gpsimd", sb_k[:, :].rearrange("p (ph t) -> p ph t", ph=4),
                  io["kin"].rearrange("(ph p) t -> p ph t", p=128), slot="init")
            p.dma("gpsimd", sb_v[:, :].rearrange("p (tc d) -> p tc d", tc=32),
                  io["vin"].rearrange("(tc p) d -> p tc d", p=128), slot="init")
            p.dma("gpsimd", sb_m[:, :].rearrange("p (ms q) -> p ms q", ms=4),
                  maskin.rearrange("(ms p) q -> p ms q", p=128), slot="init")
            t_init = p.dma("gpsimd", c_idf[:, :], idf, slot="init")
            tq = tk = tv = tm_ = t_init
        else:
            t_qkv = loader(p, sb_q, sb_k, sb_v, sb_m, c_idf)
            t_ldm = p.slot_last.get("ds_ldm")
            tq = tk = tv = t_qkv
            tm_ = t_init = t_ldm
        p.emit("gpsimd", mset(ones_cb[:, :], 1.0))
        t_ones = p.emit("gpsimd", mset(ones_rf[:, :], 1.0))

        g_sq, g_n, g_u = Guard(), Guard(), Guard()
        g_P = [Guard() for _ in range(4)]
        g_li, g_of, g_otm, g_liT, g_ob, g_obf = (Guard(), Guard(), Guard(),
                                                 Guard(), Guard(), Guard())

        for pr in range(4):
            last_acc = None
            for which, buf in ((0, sb_q), (1, sb_k)):
                for cc in range(8):
                    sq = p.emit("scalar", act(sb_sq[:, :],
                                              buf[:, pr * S + cc * 512: pr * S + (cc + 1) * 512],
                                              AF.Square),
                                after=[t_init, tq, tk] + g_sq.war())
                    g_sq.wrote(sq)
                    n = bk.take("f")
                    mm = [(ones_cb[:, :], sb_sq[:, :])]
                    mt = bk.matmuls(n, bk.t[n][0:1, 0:512], mm,
                                    extra_after=[sq, t_ones])
                    g_sq.rd(mt)
                    if cc == 0:
                        ev = bk.evac(n, "vector",
                                     red(sb_u[which][0:1, :], bk.t[n][0:1, 0:512], ALU.max),
                                     after=g_u.war() if which == 0 else [])
                    else:
                        ev = bk.evac(n, "vector",
                                     red(sb_u[2][0:1, :], bk.t[n][0:1, 0:512], ALU.max))
                        ev = p.emit("vector", tt(sb_u[which][0:1, :], sb_u[which][0:1, :],
                                                 sb_u[2][0:1, :], ALU.max))
                    last_acc = ev
            r3 = p.emit("vector", tt(sb_u[2][0:1, :], sb_u[0][0:1, :], sb_u[1][0:1, :],
                                     ALU.mult), after=[last_acc])
            r4 = p.emit("scalar", act(sb_u[2][0:1, :], sb_u[2][0:1, :], AF.Sqrt),
                        after=[r3])
            g_u.wrote(r4)
            n = bk.take("f")
            bk.matmuls(n, bk.t[n][:, 0:1], [(ones_rf[0:1, :], sb_u[2][0:1, :])],
                       extra_after=[r4, t_ones])
            ng = bk.evac(n, "vector", ts(sb_negc[:, pr:pr + 1], bk.t[n][:, 0:1],
                                         -1.0, ALU.mult))
            g_u.rd(ng)

            bpool = bk.pools["b"]
            for qs in range(8):
                if qs % 2 == 0:
                    nO = all_f[4]
                    nL = all_f[5]
                else:
                    nO = bpool[0]
                    nL = bpool[1]
                units = [(tc, ts_) for tc in range(qs + 1) for ts_ in range(4)]
                nsub = len(units)
                AH = 4
                pend = {}
                for i in range(nsub + AH):
                    j = i - AH
                    if j >= 0:
                        tcc, ts_, ptok = pend.pop(j)
                        first = (j == 0)
                        last = (j == nsub - 1)
                        lt = p.emit("tensor",
                                    (lambda e, o=(bk.t[nL][0:1, 0:1024].bitcast(F32) if nL.startswith('pbb') else bk.t[nL][0:1, 0:512]),
                                     l=ones_cb[:, :], r=sb_P[:, ts_ * 512:(ts_ + 1) * 512],
                                     st=first, sp=last:
                                     e.matmul(o, l, r, start=st, stop=sp)),
                                    after=[ptok, t_ones, bk.rd[nL] if first else None],
                                    inc=0 if not last else 1)
                        if lt is not None:
                            bk.wr[nL] = lt
                        pv = p.emit("tensor",
                                    (lambda e, o=(bk.t[nO][:, 0:1024].bitcast(F32) if nO.startswith('pbb') else bk.t[nO][:, 0:512]),
                                     l=sb_v[:, tcc * 512 + pr * 128: tcc * 512 + (pr + 1) * 128],
                                     r=sb_P[:, ts_ * 512:(ts_ + 1) * 512],
                                     st=first, sp=last:
                                     e.matmul(o, l, r, start=st, stop=sp)),
                                    after=[ptok, tv, bk.rd[nO] if first else None])
                        g_P[ts_].rd(pv)
                        bk.wr[nO] = pv
                    if i < nsub:
                        tc, ts_ = units[i]
                        n = bk.take("f")
                        mm = [(sb_k[:, pr * S + tc * 512 + ts_ * 128:
                                    pr * S + tc * 512 + (ts_ + 1) * 128],
                               sb_q[:, pr * S + qs * 512: pr * S + (qs + 1) * 512])]
                        mt = bk.matmuls(n, bk.t[n][:, 0:512], mm, extra_after=[tq, tk])
                        ex = bk.evac(n, "scalar",
                                     act(sb_P[:, ts_ * 512:(ts_ + 1) * 512],
                                         bk.t[n][:, 0:512], AF.Exp,
                                         bias=sb_negc[:, pr:pr + 1]),
                                     after=[ng] + g_P[ts_].war())
                        g_P[ts_].wrote(ex)
                        ptok = ex
                        if tc == qs:
                            mk = p.emit("vector",
                                        tt(sb_P[:, ts_ * 512:(ts_ + 1) * 512],
                                           sb_P[:, ts_ * 512:(ts_ + 1) * 512],
                                           sb_m[:, ts_ * 512:(ts_ + 1) * 512],
                                           ALU.mult), after=[ex, tm_])
                            g_P[ts_].wrote(mk)
                            ptok = mk
                        pend[i] = (tc * 4 + ts_, ts_, ptok)

                li = p.emit("vector", recip(sb_li[0:1, :], (bk.t[nL][0:1, 0:1024].bitcast(F32) if nL.startswith('pbb') else bk.t[nL][0:1, 0:512])),
                            after=[bk.wr[nL]] + g_li.war())
                bk.rd[nL] = li
                g_li.wrote(li)
                if l0:
                    ev = p.emit("vector", cp(sb_of[:, :], (bk.t[nO][:, 0:1024].bitcast(F32) if nO.startswith('pbb') else bk.t[nO][:, 0:512])),
                                after=[bk.wr[nO]] + g_of.war())
                    bk.rd[nO] = ev
                    g_of.wrote(ev)
                    for qsub in range(4):
                        n = bk.take("f")
                        w = bk.transpose(n, bk.t[n][:, 0:1].bitcast(F32),
                                         sb_li[0:1, qsub * 128:(qsub + 1) * 128],
                                         c_idf[0:1, 0:1], extra_after=[li, t_init])
                        g_li.rd(w)
                        lt_ = bk.evac(n, "vector", cp(sb_liT[:, qsub:qsub + 1],
                                                      bk.t[n][:, 0:1]),
                                      after=g_liT.war() if qsub == 0 else ())
                        if qsub == 0:
                            g_liT.wrote(lt_)
                    g_liT.wrote(lt_)
                    ot_toks = []
                    for qsub in range(4):
                        n = bk.take("f")
                        w = bk.transpose(n, bk.t[n][:, 0:128],
                                         sb_of[:, qsub * 128:(qsub + 1) * 128],
                                         c_idf[:, :], extra_after=[ev, t_init])
                        g_of.rd(w)
                        ot = bk.evac(n, "vector",
                                     ts(sb_otm[:, qsub * 128:(qsub + 1) * 128],
                                        bk.t[n][:, 0:128], sb_liT[:, qsub:qsub + 1],
                                        ALU.mult),
                                     after=[lt_] + (g_otm.war() if qsub == 0 else []))
                        g_liT.rd(ot)
                        if qsub == 0:
                            g_otm.wrote(ot)
                        ot_toks.append(ot)
                    g_otm.wrote(ot_toks[-1])
                    od = p.out_dma("sync",
                                   o_out[qs * 512:(qs + 1) * 512, pr * 128:(pr + 1) * 128]
                                   .rearrange("(qsub r) d -> r qsub d", r=128),
                                   sb_otm[:, :].rearrange("r (qsub d) -> r qsub d", qsub=4),
                                   after=ot_toks, slot="od")
                    g_otm.rd(od)
                else:
                    n = bk.take("f")
                    bk.matmuls(n, bk.t[n][:, 0:512], [(ones_rf[0:1, :], sb_li[0:1, :])],
                               extra_after=[li, t_ones])
                    g_li.rd(bk.wr[n])
                    bo = bk.evac(n, "vector", cp(sb_ob[:, :], bk.t[n][:, 0:512]),
                                 after=g_ob.war())
                    g_ob.wrote(bo)
                    ev = p.emit("vector", tt(sb_obf[:, :], (bk.t[nO][:, 0:1024].bitcast(F32) if nO.startswith('pbb') else bk.t[nO][:, 0:512]),
                                             sb_ob[:, :], ALU.mult),
                                after=[bk.wr[nO], bo] + g_obf.war())
                    bk.rd[nO] = ev
                    g_ob.rd(ev)
                    g_obf.wrote(ev)
                    od = p.out_dma("sync",
                                   o_out[pr * 128:(pr + 1) * 128, qs * 512:(qs + 1) * 512],
                                   sb_obf[:, :], after=[ev], slot="od")
                    g_obf.rd(od)
        bk.pools["f"] = all_f
        bk.i["f"] = 0



HID = 5632
NHC = HID // 128
ZP = 128.0


def mid_dram(nc, inter=False):
    io = {}
    if not inter:
        io["o_as"] = nc.dram_tensor("o_as", [TPC, D], F32, kind="ExternalInput").ap()
        io["m1_in"] = nc.dram_tensor("m1_in", [D, TPC], F32, kind="ExternalInput").ap()
    io["wot"] = nc.dram_tensor("wot", [D, D], F32, kind="ExternalInput").ap()
    io["wqkv"] = nc.dram_tensor("wqkv", [D, 3 * D], F32, kind="ExternalInput").ap()
    io["wqg_t"] = nc.dram_tensor("wqg_t", [D, HID], I32, kind="ExternalInput").ap()
    io["wqu_t"] = nc.dram_tensor("wqu_t", [D, HID], I32, kind="ExternalInput").ap()
    io["wqd_t"] = nc.dram_tensor("wqd_t", [HID, D], I32, kind="ExternalInput").ap()
    io["sg"] = nc.dram_tensor("sg", [128, NHC], F32, kind="ExternalInput").ap()
    io["su"] = nc.dram_tensor("su", [128, NHC], F32, kind="ExternalInput").ap()
    io["sd"] = nc.dram_tensor("sd", [128, NDC], F32, kind="ExternalInput").ap()
    if inter:
        io["qk1_out"] = nc.dram_tensor("qk1_out", [4096, TPC], BF16).ap()
        io["v1_out"] = nc.dram_tensor("v1_out", [4096, 512], BF16).ap()
    else:
        io["idf"] = nc.dram_tensor("idf", [128, 128], F32, kind="ExternalInput").ap()
        io["idb"] = nc.dram_tensor("idb", [128, 128], BF16, kind="ExternalInput").ap()
        io["qk1_out"] = nc.dram_tensor("qk1_out", [4096, TPC], F32, kind="ExternalOutput").ap()
        io["v1_out"] = nc.dram_tensor("v1_out", [4096, 512], F32, kind="ExternalOutput").ap()
        io["m2_out"] = nc.dram_tensor("m2_out", [D, TPC], F32, kind="ExternalOutput").ap()
    return io


def build_mid(phases="ABCDE"):
    p = Prog()
    io = mid_dram(p.nc)
    with ExitStack() as ctx:
        resid = ctx.enter_context(p.nc.sbuf_tensor("resid", [128, NDC * TPC], F32))
        bk = Banks(p, ctx, nf32=6, nbf=2)
        emit_mid(p, ctx, bk, io, resid, fused=False, o_loader=None, ev_dt=F32)
        p.render(ctx)
    return p.nc


def emit_mid(p, ctx, bk, io, resid, fused, o_loader, ev_dt, qkv_done_cb=None):
    nc = p.nc
    wot, wqkv = io["wot"], io["wqkv"]
    wqg_t, wqu_t, wqd_t = io["wqg_t"], io["wqu_t"], io["wqd_t"]
    sg_in, su_in, sd_in = io["sg"], io["su"], io["sd"]
    idf, idb = io["idf"], io["idb"]
    qk1_out, v1_out = io["qk1_out"], io["v1_out"]
    m2_out = io.get("m2_out")
    if True:
        sb = lambda n, s, d: ctx.enter_context(nc.sbuf_tensor(n, s, d, allow_name_mangling=True))
        xn1_fm = sb("xn1_fm", [128, NDC * TPC], BF16)
        invso_c = sb("invso_c", [128, 8], F32)
        invso_b = sb("invso_b", [128, TPC], F32)
        invso_r = sb("invso_r", [1, TPC], F32)
        c_idf = sb("c_idf", [128, 128], F32)
        c_idb = sb("c_idb", [128, 128], BF16)
        ones_cf = sb("ones_cf", [128, 1], F32)
        ones_cb = sb("ones_cb", [128, 1], BF16)
        ones_rf = sb("ones_rf", [1, 128], F32)
        m128_r = sb("m128_r", [1, 128], BF16)
        sg = sb("sgv", [128, NHC], F32)
        su = sb("suv", [128, NHC], F32)
        sd = sb("sdv", [128, NDC], F32)

        p.dma("sync", c_idf[:, :], idf, slot="init")
        p.dma("sync", c_idb[:, :], idb, slot="init")
        p.dma("sync", sg[:, :], sg_in, slot="init")
        p.dma("sync", su[:, :], su_in, slot="init")
        t_init = p.dma("sync", sd[:, :], sd_in, slot="init")
        t_idf = t_idb = t_scl = t_init
        p.emit("gpsimd", mset(ones_cf[:, :], 1.0))
        p.emit("gpsimd", mset(ones_cb[:, :], 1.0))
        p.emit("gpsimd", mset(ones_rf[:, :], 1.0))
        t_ones = p.emit("gpsimd", mset(m128_r[:, :], -128.0))

        resid_guard = Guard()
        if not fused:
            t_m1 = p.dma("gpsimd", resid[:, :].rearrange("p (dc t) -> p dc t", dc=NDC),
                         io["m1_in"].rearrange("(dc p) t -> p dc t", p=128), slot="m1")
            resid_guard.wrote(t_m1)
        t_m1 = resid_guard.writer

        t_thr = ctx.enter_context(nc.sbuf_tensor("t_thr", [128, 1], F32, allow_name_mangling=True))
        t_nthr = ctx.enter_context(nc.sbuf_tensor("t_nthr", [128, 1], F32, allow_name_mangling=True))
        t_sw = ctx.enter_context(nc.sbuf_tensor("t_sw", [128, 1], F32, allow_name_mangling=True))
        NBLK, BW = 8, 256
        with ExitStack() as cW:
            sbw = lambda n, s, d: cW.enter_context(nc.sbuf_tensor(n, s, d, allow_name_mangling=True))
            t_w = [sbw(f"t_w{i}", [128, NDC * BW], F32) for i in range(2)]
            t_ab = sbw("t_ab", [128, NDC * BW], BF16)
            t_acc = sbw("t_acc", [128, NBLK], F32)
            t_g = [sbw(f"t_g{i}", [1, 1], F32) for i in range(3)]
            g_w = [Guard(), Guard()]
            acc_toks = []
            for blk in range(NBLK):
                wb = t_w[blk % 2]
                gw = g_w[blk % 2]
                tok = p.dma("sync", wb[:, :].rearrange("p (ec o) -> p ec o", ec=NDC),
                            wot[:, blk * BW:(blk + 1) * BW]
                            .rearrange("(ec p) o -> p ec o", p=128),
                            after=gw.war(), slot=f"wm{blk % 2}")
                gw.wrote(tok)
                a = p.emit("scalar", act(t_ab[:, :], wb[:, :], AF.Abs,
                                         accum=t_acc[:, blk:blk + 1]), after=[tok])
                gw.rd(a)
                acc_toks.append(a)
            s1 = p.emit("vector", red(t_thr[:, :], t_acc[:, :], ALU.add), after=acc_toks)
            n = bk.take("f")
            bk.matmuls(n, bk.t[n][0:1, 0:1], [(t_thr[:, :], ones_cf[:, :])],
                       extra_after=[s1, t_ones])
            bk.evac(n, "vector", ts(t_g[0][0:1, :], bk.t[n][0:1, 0:1],
                                    1.0 / (D * D), ALU.mult))
            p.emit("vector", ts(t_g[1][0:1, :], t_g[0][0:1, :], 1e-5, ALU.max))
            g3 = p.emit("vector", ts(t_g[2][0:1, :], t_g[1][0:1, :], 0.5, ALU.mult))
            n = bk.take("f")
            bk.matmuls(n, bk.t[n][:, 0:1], [(ones_rf[0:1, :], t_g[1][0:1, :])],
                       extra_after=[g3, t_ones])
            bk.evac(n, "vector", cp(t_sw[:, :], bk.t[n][:, 0:1]))
            n = bk.take("f")
            bk.matmuls(n, bk.t[n][:, 0:1], [(ones_rf[0:1, :], t_g[2][0:1, :])],
                       extra_after=[g3, t_ones])
            bk.evac(n, "vector", cp(t_thr[:, :], bk.t[n][:, 0:1]))
            t_nth = p.emit("vector", ts(t_nthr[:, :], t_thr[:, :], -1.0, ALU.mult))

        oq_fm = None
        cA = ExitStack()
        oq_fm = cA.enter_context(nc.sbuf_tensor("oq_fm", [128, NDC * TPC], BF16))
        with ExitStack() as c2:
            sb2 = lambda n, s, d: c2.enter_context(nc.sbuf_tensor(n, s, d, allow_name_mangling=True))
            t_x = [sb2(f"t_x{i}", [128, D], F32) for i in range(2)]
            t_v = sb2("t_v", [128, D], F32)
            t_i = sb2("t_i", [128, D], I32)
            t_kf = sb2("t_kf", [128, D], BF16)
            t_ca = sb2("t_ca", [128, D], BF16)
            t_cb = sb2("t_cb", [128, D], BF16)
            t_xq = sb2("t_xq", [128, D], BF16)
            t_s = [sb2(f"t_s{i}", [128, 1], F32) for i in range(6)]
            g_x = [Guard(), Guard()]
            g_xq = Guard()
            ivr_toks = []
            for tt_ in range(8):
                xb = t_x[tt_ % 2]
                gx = g_x[tt_ % 2]
                if o_loader is None:
                    tok_x = p.dma("sync", xb[:, :], io["o_as"][tt_ * 128:(tt_ + 1) * 128, :],
                                  after=gx.war(), slot=f"x{tt_ % 2}")
                else:
                    tok_x = o_loader(p, xb, tt_, gx.war())
                gx.wrote(tok_x)
                a6 = tok_x
                r0 = p.emit("vector", red(t_s[3][:, :], xb[:, :], ALU.max, absval=True),
                            after=[tok_x])
                gx.rd(r0)
                p.emit("vector", ts(t_s[3][:, :], t_s[3][:, :], 1e-5, ALU.max))
                p.emit("vector", recip(t_s[4][:, :], t_s[3][:, :]))
                p.emit("vector", ts(t_s[5][:, :], t_s[4][:, :], 127.0, ALU.mult))
                ivc = p.emit("vector", ts(invso_c[:, tt_:tt_ + 1], t_s[3][:, :],
                                          1.0 / 127.0, ALU.mult))
                vv = p.emit("vector", ts(t_v[:, :], xb[:, :], t_s[5][:, :], ALU.mult))
                gx.rd(vv)
                p.emit("vector", cp(t_i[:, :], t_v[:, :]))
                p.emit("vector", cp(t_kf[:, :], t_i[:, :]))
                p.emit("vector", tt(t_v[:, :], t_v[:, :], t_kf[:, :], ALU.subtract))
                p.emit("vector", stt(t_ca[:, :], t_v[:, :], 0.5, t_kf[:, :],
                                     ALU.is_gt, ALU.add))
                p.emit("vector", ts(t_cb[:, :], t_v[:, :], -0.5, ALU.is_lt))
                p.emit("vector", tt(t_xq[:, :], t_ca[:, :], t_cb[:, :], ALU.subtract),
                       after=g_xq.war())
                p.emit("vector", ts(t_i[:, :], t_i[:, :], 1, ALU.bitwise_and))
                p.emit("vector", stt(t_ca[:, :], t_v[:, :], 0.5, t_i[:, :],
                                     ALU.is_equal, ALU.mult))
                p.emit("vector", stt(t_cb[:, :], t_v[:, :], -0.5, t_i[:, :],
                                     ALU.is_equal, ALU.mult))
                p.emit("vector", tt(t_xq[:, :], t_xq[:, :], t_ca[:, :], ALU.add))
                aq = p.emit("vector", tt(t_xq[:, :], t_xq[:, :], t_cb[:, :], ALU.subtract))
                g_xq.wrote(aq)
                for dc in range(NDC):
                    n = bk.take("b")
                    w = bk.transpose(n, bk.t[n][:, 0:128],
                                     t_xq[:, dc * 128:(dc + 1) * 128], c_idb[:, :],
                                     extra_after=[aq, t_idb])
                    g_xq.rd(w)
                    e = "scalar" if dc % 2 else "vector"
                    dst = oq_fm[:, dc * TPC + tt_ * 128: dc * TPC + (tt_ + 1) * 128]
                    fn = scp(dst, bk.t[n][:, 0:128]) if e == "scalar" else cp(dst, bk.t[n][:, 0:128])
                    bk.evac(n, e, fn)
                n = bk.take("f")
                w = bk.transpose(n, bk.t[n][0:1, 0:128],
                                 invso_c[:, tt_:tt_ + 1], c_idf[:, :],
                                 extra_after=[ivc, t_idf])
                ivr_toks.append(bk.evac(n, "vector",
                                        cp(invso_r[0:1, tt_ * 128:(tt_ + 1) * 128],
                                           bk.t[n][0:1, 0:128])))
            for cc in range(2):
                n = bk.take("f")
                mm = [(ones_rf[0:1, :], invso_r[0:1, cc * 512:(cc + 1) * 512])]
                bk.matmuls(n, bk.t[n][:, 0:512], mm, extra_after=ivr_toks + [t_ones])
                bk.evac(n, "vector", cp(invso_b[:, cc * 512:(cc + 1) * 512],
                                        bk.t[n][:, 0:512]))

        p.barrier()
        with ExitStack() as c2:
            sb2 = lambda n, s, d: c2.enter_context(nc.sbuf_tensor(n, s, d, allow_name_mangling=True))
            t_w = [sb2(f"t_w{i}", [128, NDC * BW], F32) for i in range(2)]
            t_b1 = sb2("t_b1", [128, NDC * BW], BF16)
            t_tern2 = [sb2(f"t_tern{i}", [128, NDC * BW], BF16) for i in range(2)]
            t_y = [sb2(f"t_y{i}", [128, 512], F32) for i in range(2)]
            g_w = [Guard(), Guard()]
            g_tern2 = [Guard(), Guard()]
            g_b1 = Guard()
            g_y = [Guard(), Guard()]
            isv = p.emit("vector", ts(invso_b[:, :], invso_b[:, :], t_sw[:, :], ALU.mult),
                         after=[t_nth])

            for blk in range(NBLK):
                wb = t_w[blk % 2]
                gw = g_w[blk % 2]
                tok = p.dma("sync", wb[:, :].rearrange("p (ec o) -> p ec o", ec=NDC),
                            wot[:, blk * BW:(blk + 1) * BW]
                            .rearrange("(ec p) o -> p ec o", p=128),
                            after=gw.war(), slot=f"w{blk % 2}")
                gw.wrote(tok)
                t_tern = t_tern2[blk % 2]
                g_tern = g_tern2[blk % 2]
                b1 = p.emit("vector", ts(t_b1[:, :], wb[:, :], t_nthr[:, :], ALU.is_lt),
                            after=[tok] + g_b1.war())
                g_b1.wrote(b1)
                gw.rd(b1)
                tern = p.emit("vector", stt(t_tern[:, :], wb[:, :], t_thr[:, :],
                                            t_b1[:, :], ALU.is_gt, ALU.subtract),
                              after=g_tern.war())
                gw.rd(tern)
                g_b1.rd(tern)
                g_tern.wrote(tern)
                for ob in range(2):
                    for cc_ in range(2):
                        n = bk.take("f")
                        mms = [(t_tern[:, dc * BW + ob * 128:dc * BW + (ob + 1) * 128],
                                oq_fm[:, dc * TPC + cc_ * 512:dc * TPC + cc_ * 512 + 512])
                               for dc in range(NDC)]
                        m = bk.matmuls(n, bk.t[n][:, 0:512], mms, extra_after=[tern])
                        g_tern.rd(m)
                        k = (ob * 2 + cc_) % 2
                        yt = bk.evac(n, "vector",
                                     tt(t_y[k][:, :], bk.t[n][:, 0:512],
                                        invso_b[:, cc_ * 512:cc_ * 512 + 512], ALU.mult),
                                     after=[isv] + g_y[k].war())
                        g_y[k].wrote(yt)
                        oc = blk * 2 + ob
                        sl = resid[:, oc * TPC + cc_ * 512: oc * TPC + cc_ * 512 + 512]
                        ra = p.emit("vector", tt(sl, sl, t_y[k][:, :], ALU.add),
                                    after=[yt, t_m1])
                        g_y[k].rd(ra)
                        resid_guard.rd(ra)
        cA.close()

        p.barrier()
        with ExitStack() as c2:
            sb2 = lambda n, s, d: c2.enter_context(nc.sbuf_tensor(n, s, d, allow_name_mangling=True))
            rb = sb2("rb", [128, NDC * TPC], BF16)
            rbsq = sb2("rbsq", [128, NDC * TPC], BF16)
            row = [sb2(f"row{i}", [1, TPC], F32) for i in range(3)]
            mb = sb2("mb", [128, TPC], F32)
            rsb = sb2("rsb", [128, TPC], F32)
            t_t1 = sb2("t_t1", [128, TPC], F32)
            g_t1 = Guard()

            cb0 = p.emit("scalar", scp(rb[:, :], resid[:, :]))
            sq0 = p.emit("scalar", act(rbsq[:, :], rb[:, :], AF.Square), after=[cb0])
            r0_toks, r1_toks = [], []
            for cc in range(2):
                n = bk.take("f")
                mms = [(ones_cb[:, :], rb[:, dc * TPC + cc * 512:dc * TPC + cc * 512 + 512])
                       for dc in range(NDC)]
                bk.matmuls(n, bk.t[n][0:1, 0:512], mms, extra_after=[cb0, t_ones])
                r0_toks.append(bk.evac(n, "vector", ts(row[0][0:1, cc * 512:(cc + 1) * 512],
                                                       bk.t[n][0:1, 0:512], 1.0 / D, ALU.mult)))
                n = bk.take("f")
                mms = [(ones_cb[:, :], rbsq[:, dc * TPC + cc * 512:dc * TPC + cc * 512 + 512])
                       for dc in range(NDC)]
                bk.matmuls(n, bk.t[n][0:1, 0:512], mms, extra_after=[sq0, t_ones])
                r1_toks.append(bk.evac(n, "vector", ts(row[1][0:1, cc * 512:(cc + 1) * 512],
                                                       bk.t[n][0:1, 0:512], 1.0 / D, ALU.mult)))
            sqm = p.emit("scalar", act(row[2][0:1, :], row[0][0:1, :], AF.Square),
                         after=r0_toks)
            vr = p.emit("vector", tt(row[1][0:1, :], row[1][0:1, :], row[2][0:1, :],
                                     ALU.subtract), after=[sqm] + r1_toks)
            p.emit("vector", ts(row[1][0:1, :], row[1][0:1, :], EPS, ALU.add))
            rc = p.emit("vector", recip(row[1][0:1, :], row[1][0:1, :]))
            rs = p.emit("scalar", act(row[1][0:1, :], row[1][0:1, :], AF.Sqrt), after=[rc])
            for cc in range(2):
                n = bk.take("f")
                bk.matmuls(n, bk.t[n][:, 0:512],
                           [(ones_rf[0:1, :], row[0][0:1, cc * 512:(cc + 1) * 512])],
                           extra_after=[rs, t_ones] + r0_toks)
                bk.evac(n, "vector", cp(mb[:, cc * 512:(cc + 1) * 512], bk.t[n][:, 0:512]))
                n = bk.take("f")
                bk.matmuls(n, bk.t[n][:, 0:512],
                           [(ones_rf[0:1, :], row[1][0:1, cc * 512:(cc + 1) * 512])],
                           extra_after=[rs, t_ones])
                mbt = bk.evac(n, "vector", cp(rsb[:, cc * 512:(cc + 1) * 512],
                                              bk.t[n][:, 0:512]))
            for dc in range(NDC):
                d1 = p.emit("vector", tt(t_t1[:, :], resid[:, dc * TPC:(dc + 1) * TPC],
                                         mb[:, :], ALU.subtract),
                            after=[mbt] + g_t1.war())
                g_t1.wrote(d1)
                d2 = p.emit("vector", tt(xn1_fm[:, dc * TPC:(dc + 1) * TPC],
                                         t_t1[:, :], rsb[:, :], ALU.mult))
                g_t1.rd(d2)

        p.barrier()
        with ExitStack() as c2:
            sb2 = lambda n, s, d: c2.enter_context(nc.sbuf_tensor(n, s, d, allow_name_mangling=True))
            t_wb = [sb2(f"t_wb{i}", [128, NDC * 512], BF16) for i in range(2)]
            t_wf = sb2("t_wf", [128, NDC * 512], F32)
            t_ev = [sb2(f"t_ev{i}", [128, 512], ev_dt) for i in range(4)]
            g_wb = [Guard(), Guard()]
            g_wf = Guard()
            g_ev = [Guard() for _ in range(4)]
            for blk in range(12):
                wb = t_wb[blk % 2]
                gw = g_wb[blk % 2]
                tokf = p.dma("sync", t_wf[:, :].rearrange("p (dc o) -> p dc o", dc=NDC),
                             wqkv[:, blk * 512:(blk + 1) * 512]
                             .rearrange("(dc p) o -> p dc o", p=128),
                             after=g_wf.war(), slot="wf")
                g_wf.wrote(tokf)
                tok = p.emit("vector", cp(wb[:, :], t_wf[:, :]),
                             after=[tokf] + gw.war())
                g_wf.rd(tok)
                gw.wrote(tok)
                if blk < 8:
                    for ob in range(4):
                        for cc_ in range(2):
                            n = bk.take("f")
                            mms = [(wb[:, dc * 512 + ob * 128:dc * 512 + (ob + 1) * 128],
                                    xn1_fm[:, dc * TPC + cc_ * 512:dc * TPC + cc_ * 512 + 512])
                                   for dc in range(NDC)]
                            m = bk.matmuls(n, bk.t[n][:, 0:512], mms, extra_after=[tok])
                            gw.rd(m)
                            k = (ob * 2 + cc_) % 4
                            ev = t_ev[k]
                            sc_ = ISQD if blk < 4 else 1.0
                            evt = bk.evac(n, "scalar",
                                          act(ev[:, :], bk.t[n][:, 0:512], AF.Copy,
                                              bias=0.0, scale=sc_),
                                          after=g_ev[k].war())
                            g_ev[k].wrote(evt)
                            if blk < 4:
                                h = blk * 4 + ob
                                rbase = (h // 4) * 1024 + (h % 4) * 128
                            else:
                                h = (blk - 4) * 4 + ob
                                rbase = (h // 4) * 1024 + 512 + (h % 4) * 128
                            dt_ = p.out_dma("sync",
                                            qk1_out[rbase:rbase + 128,
                                                    cc_ * 512:cc_ * 512 + 512],
                                            ev[:, :], after=[evt], slot=f"ev{k}")
                            g_ev[k].rd(dt_)
                else:
                    for tt_ in range(8):
                        n = bk.take("f")
                        mms = [(xn1_fm[:, dc * TPC + tt_ * 128:dc * TPC + (tt_ + 1) * 128],
                                wb[:, dc * 512:(dc + 1) * 512])
                               for dc in range(NDC)]
                        m = bk.matmuls(n, bk.t[n][:, 0:512], mms, extra_after=[tok])
                        gw.rd(m)
                        k = tt_ % 4
                        ev = t_ev[k]
                        evt = bk.evac(n, "scalar", scp(ev[:, :], bk.t[n][:, 0:512]),
                                      after=g_ev[k].war())
                        g_ev[k].wrote(evt)
                        rbase = (blk - 8) * 1024 + tt_ * 128
                        dt_ = p.out_dma("sync", v1_out[rbase:rbase + 128, :],
                                        ev[:, :], after=[evt], slot=f"ev{k}")
                        g_ev[k].rd(dt_)

        if qkv_done_cb is not None:
            qkv_done_cb(p)
        p.barrier()
        with ExitStack() as c2:
            sb2 = lambda n, s, d: c2.enter_context(nc.sbuf_tensor(n, s, d, allow_name_mangling=True))
            gu_sb = sb2("gu_sb", [128, NHC * 512], BF16)
            t_gs = sb2("t_gs", [128, 512], BF16)
            trow = sb2("trow", [1, 512], BF16)
            t2row = sb2("t2row", [1, 512], BF16)
            g_gs, g_trow, g_t2, g_gu = Guard(), Guard(), Guard(), Guard()
            down_drain = []

            for th in range(2):
                n = bk.take("f")
                mms = [(ones_cb[:, :], xn1_fm[:, dc * TPC + th * 512:dc * TPC + th * 512 + 512])
                       for dc in range(NDC)]
                bk.matmuls(n, bk.t[n][0:1, 0:512], mms, extra_after=[t_ones] + g_trow.war())
                ttr = bk.evac(n, "vector", cp(trow[0:1, :], bk.t[n][0:1, 0:512]))
                g_trow.wrote(ttr)
                gu_toks = []
                with ExitStack() as c3:
                    sb3 = lambda n_, s_, d_: c3.enter_context(
                        nc.sbuf_tensor(n_, s_, d_, allow_name_mangling=True))
                    stg = sb3("stg", [128, 4 * 2048], BF16)
                    stu = sb3("stu", [128, 4 * 2048], BF16)
                    t_wi = [sb3(f"t_wi{i}", [128, 2048], I32) for i in range(2)]
                    g_wi = [Guard(), Guard()]
                    g_q = [Guard() for _ in range(8)]
                    witer = 0
                    for hb in range(11):
                        cvt = {}
                        for wsel, (wt, stage) in enumerate(((wqg_t, stg), (wqu_t, stu))):
                            for q4 in range(4):
                                wi_ = t_wi[witer % 2]
                                gwi = g_wi[witer % 2]
                                witer += 1
                                tok = p.dma("sync",
                                            wi_[:, :].rearrange("p (dc o) -> p dc o", dc=NDC),
                                            wt[:, hb * 512 + q4 * 128: hb * 512 + (q4 + 1) * 128]
                                            .rearrange("(dc p) o -> p dc o", p=128),
                                            after=gwi.war() + (down_drain if hb == 0 else []),
                                            slot=f"wi{witer % 2}")
                                gwi.wrote(tok)
                                eng = "scalar" if wsel == 0 else "vector"
                                cv = p.emit(eng,
                                            (scp if eng == "scalar" else cp)(
                                                stage[:, q4 * 2048:(q4 + 1) * 2048], wi_[:, :]),
                                            after=[tok] + g_q[wsel * 4 + q4].war())
                                gwi.rd(cv)
                                g_q[wsel * 4 + q4].wrote(cv)
                                cvt[(wsel, q4)] = cv
                        for hc4 in range(4):
                            hc = hb * 4 + hc4
                            n = bk.take("f")
                            mms = [(stg[:, hc4 * 2048 + dc * 128: hc4 * 2048 + (dc + 1) * 128],
                                    xn1_fm[:, dc * TPC + th * 512:dc * TPC + th * 512 + 512])
                                   for dc in range(NDC)]
                            mms.append((m128_r[0:1, :], trow[0:1, :]))
                            mg = bk.matmuls(n, bk.t[n][:, 0:512], mms,
                                            extra_after=[cvt[(0, hc4)], ttr, t_ones])
                            g_q[hc4].rd(mg)
                            g_trow.rd(mg)
                            gs = bk.evac(n, "scalar",
                                         act(t_gs[:, :], bk.t[n][:, 0:512], AF.Silu,
                                             scale=sg[:, hc:hc + 1]),
                                         after=[t_scl] + g_gs.war())
                            g_gs.wrote(gs)
                            n2 = bk.take("f")
                            mms = [(stu[:, hc4 * 2048 + dc * 128: hc4 * 2048 + (dc + 1) * 128],
                                    xn1_fm[:, dc * TPC + th * 512:dc * TPC + th * 512 + 512])
                                   for dc in range(NDC)]
                            mms.append((m128_r[0:1, :], trow[0:1, :]))
                            mu = bk.matmuls(n2, bk.t[n2][:, 0:512], mms,
                                            extra_after=[cvt[(1, hc4)], ttr, t_ones])
                            g_q[4 + hc4].rd(mu)
                            g_trow.rd(mu)
                            gut = bk.evac(n2, "vector",
                                          stt(gu_sb[:, hc * 512:(hc + 1) * 512],
                                              bk.t[n2][:, 0:512], su[:, hc:hc + 1],
                                              t_gs[:, :], ALU.mult, ALU.mult),
                                          after=[gs, t_scl] + g_gu.war())
                            g_gs.rd(gut)
                            gu_toks.append(gut)
                g_gu.wrote(gu_toks[-1])
                stage_drain = []
                for gq in g_q:
                    stage_drain += gq.readers + ([gq.writer] if gq.writer else [])
                n = bk.take("f")
                mms = [(ones_cb[:, :], gu_sb[:, hc * 512:(hc + 1) * 512])
                       for hc in range(NHC)]
                bk.matmuls(n, bk.t[n][0:1, 0:512], mms,
                           extra_after=gu_toks + [t_ones] + g_t2.war())
                tt2 = bk.evac(n, "vector", cp(t2row[0:1, :], bk.t[n][0:1, 0:512]))
                g_t2.wrote(tt2)
                with ExitStack() as c3:
                    sb3 = lambda n_, s_, d_: c3.enter_context(
                        nc.sbuf_tensor(n_, s_, d_, allow_name_mangling=True))
                    std2 = [sb3(f"std{i}", [128, NHC * 128], BF16) for i in range(2)]
                    wdi = sb3("wdi", [128, NHC * 128], I32)
                    g_sd = [Guard(), Guard()]
                    g_di = Guard()
                    for oc in range(NDC):
                        std_ = std2[oc % 2]
                        td = p.dma("sync", wdi[:, :].rearrange("p (hc o) -> p hc o", hc=NHC),
                                   wqd_t[:, oc * 128:(oc + 1) * 128]
                                   .rearrange("(hc p) o -> p hc o", p=128),
                                   after=g_di.war() + (stage_drain if oc == 0 else []),
                                   slot="wd0")
                        g_di.wrote(td)
                        cv = p.emit("scalar" if oc % 2 else "vector",
                                    (scp if oc % 2 else cp)(std_[:, :], wdi[:, :]),
                                    after=[td] + g_sd[oc % 2].war())
                        g_di.rd(cv)
                        g_sd[oc % 2].wrote(cv)
                        n = bk.take("f")
                        mms = [(std_[:, hc * 128:(hc + 1) * 128],
                                gu_sb[:, hc * 512:(hc + 1) * 512]) for hc in range(NHC)]
                        mms.append((m128_r[0:1, :], t2row[0:1, :]))
                        md = bk.matmuls(n, bk.t[n][:, 0:512], mms,
                                        extra_after=[cv, tt2, t_ones] + gu_toks)
                        g_sd[oc % 2].rd(md)
                        g_t2.rd(md)
                        g_gu.rd(md)
                        sl = resid[:, oc * TPC + th * 512: oc * TPC + th * 512 + 512]
                        ra = bk.evac(n, "vector",
                                     stt(sl, bk.t[n][:, 0:512], sd[:, oc:oc + 1],
                                         sl, ALU.mult, ALU.add), after=[t_scl])
                        resid_guard.rd(ra)
                        down_drain.append(md)

        if not fused:
            for dc in range(NDC):
                p.out_dma("sync", m2_out[dc * 128:(dc + 1) * 128, :],
                          resid[:, dc * TPC:(dc + 1) * TPC],
                          after=resid_guard.readers[-40:])



def build_fin():
    p = Prog()
    nc = p.nc
    io = {}
    io["o1"] = nc.dram_tensor("o1", [D, TPC], F32, kind="ExternalInput").ap()
    io["m2_in"] = nc.dram_tensor("m2_in", [D, TPC], F32, kind="ExternalInput").ap()
    io["w_out"] = nc.dram_tensor("w_out", [D, D], F32, kind="ExternalInput").ap()
    io["y_out"] = nc.dram_tensor("y_out", [D, TPC], F32, kind="ExternalOutput").ap()
    with ExitStack() as ctx:
        resid = ctx.enter_context(p.nc.sbuf_tensor("resid", [128, NDC * TPC], F32))
        bk = Banks(p, ctx, nf32=6, nbf=0)
        emit_fin(p, ctx, bk, io, resid, fused=False, o1_loader=None)
        p.render(ctx)
    return p.nc


def emit_fin(p, ctx, bk, io, resid, fused, o1_loader):
    nc = p.nc
    w_out = io["w_out"]
    y_out = io["y_out"]
    if True:
        sb = lambda n, s, d: ctx.enter_context(nc.sbuf_tensor(n, s, d, allow_name_mangling=True))
        o1_fm = sb("o1_fm", [128, NDC * TPC], BF16)
        t_wb = [sb(f"t_wb{i}", [128, NDC * 512], BF16) for i in range(2)]
        g_wb = [Guard(), Guard()]
        resid_guard = Guard()

        if not fused:
            t_m2 = p.dma("gpsimd", resid[:, :].rearrange("p (dc t) -> p dc t", dc=NDC),
                         io["m2_in"].rearrange("(dc p) t -> p dc t", p=128), slot="m2")
            resid_guard.wrote(t_m2)
        t_m2 = resid_guard.writer
        if o1_loader is None:
            t_o1 = p.dma("gpsimd", o1_fm[:, :].rearrange("p (dc t) -> p dc t", dc=NDC),
                         io["o1"].rearrange("(dc p) t -> p dc t", p=128), slot="o1")
        else:
            t_o1 = o1_loader(p, o1_fm)

        for blk in range(4):
            wb = t_wb[blk % 2]
            gw = g_wb[blk % 2]
            tok = p.dma("gpsimd", wb[:, :].rearrange("p (dc o) -> p dc o", dc=NDC),
                        w_out[:, blk * 512:(blk + 1) * 512]
                        .rearrange("(dc p) o -> p dc o", p=128),
                        after=gw.war(), slot=f"wb{blk % 2}")
            gw.wrote(tok)
            for ob in range(4):
                for cc_ in range(2):
                    n = bk.take("f")
                    mms = [(wb[:, dc * 512 + ob * 128:dc * 512 + (ob + 1) * 128],
                            o1_fm[:, dc * TPC + cc_ * 512:dc * TPC + cc_ * 512 + 512])
                           for dc in range(NDC)]
                    m = bk.matmuls(n, bk.t[n][:, 0:512], mms, extra_after=[tok, t_o1])
                    gw.rd(m)
                    oc = blk * 4 + ob
                    sl = resid[:, oc * TPC + cc_ * 512: oc * TPC + cc_ * 512 + 512]
                    ra = bk.evac(n, "vector", tt(sl, sl, bk.t[n][:, 0:512], ALU.add),
                                 after=[t_m2])
                    resid_guard.rd(ra)

        for dc in range(NDC):
            p.out_dma("sync", y_out[dc * 128:(dc + 1) * 128, :],
                      resid[:, dc * TPC:(dc + 1) * TPC],
                      after=resid_guard.readers)



GROUPS = [[0, 1, 2, 3], [4, 5, 6, 7]]


def build_fused(dbg=False):
    p = Prog()
    nc = p.nc
    io = pre0_dram(nc, inter=True)
    io.update(mid_dram(nc, inter=True))
    io["idf"] = io["idf"] if "idf" in io else None
    io["maskin"] = nc.dram_tensor("maskin", [512, 512], BF16, kind="ExternalInput").ap()
    io["w_out"] = nc.dram_tensor("w_out", [D, D], F32, kind="ExternalInput").ap()
    io["y_out"] = nc.dram_tensor("y_out", [D, TPC], F32, kind="ExternalOutput").ap()
    io["gidx"] = nc.dram_tensor("gidx", [128, 176], mybir.dt.uint32, kind="ExternalInput").ap()
    aqk = nc.dram_tensor("aqk", [4 * 4096, TPC], BF16).ap()
    av = nc.dram_tensor("av", [4 * 4096, 512], BF16).ap()
    o_c = nc.dram_tensor("o_c", [4096, 512], F32)
    ao = nc.dram_tensor("ao", [4 * 4096, 512], F32).ap()
    aqk1 = nc.dram_tensor("aqk1", [4 * 4096, TPC], BF16).ap()
    av1 = nc.dram_tensor("av1", [4 * 4096, 512], BF16).ap()
    o1_c = nc.dram_tensor("o1_c", [512, S], BF16)
    ao1 = nc.dram_tensor("ao1", [4 * 512, S], BF16).ap()
    o_c_ap = o_c.ap()
    o1_c_ap = o1_c.ap()

    with ExitStack() as ctx:
        resid = ctx.enter_context(nc.sbuf_tensor("resid", [128, NDC * TPC], F32))
        sb_gidx = ctx.enter_context(nc.sbuf_tensor("sb_gidx", [128, 176], mybir.dt.uint32))
        bk = Banks(p, ctx, nf32=6, nbf=2)
        t_gidx = p.dma("gpsimd", sb_gidx[:, :], io["gidx"], slot="gidx")

        def coll(name, cin, cout, after):
            return p.emit("gpsimd",
                          (lambda e, i=cin, o=cout:
                           e.collective_compute("AllGather", ALU.bypass,
                                                replica_groups=GROUPS,
                                                ins=[i.opt()], outs=[o.opt()])),
                          after=after, inc=1, sem="cc")

        ccd = {}

        def pre0_cb(p_):
            evtoks = [p_.slot_last.get(f"ds_ev{k}") for k in range(4)]
            ccd["qk"] = coll("qk", io["qk_out"], aqk, evtoks)
            ccd["v"] = coll("v", io["v_out"], av, [ccd["qk"]])

        with ExitStack() as stx:
            emit_pre0(p, stx, bk, io, resid, fused=True, qkv_done_cb=pre0_cb)
        cc_qk = ccd["qk"]
        cc_v = ccd["v"]
        p.barrier()

        aio = {"maskin": io["maskin"], "idf": io["idf"], "o_out": o_c_ap}

        def loader0(p_, sb_q, sb_k, sb_v, sb_m, c_idf):
            p_.dma("sync", sb_m[:, :].rearrange("p (ms q) -> p ms q", ms=4),
                   io["maskin"].rearrange("(ms p) q -> p ms q", p=128), slot="ldm")
            p_.dma("sync", c_idf[:, :], io["idf"], slot="ldm")
            t = None
            for pr in range(4):
                for j in range(4):
                    t = p_.igather(sb_q[:, pr * S + j * 1024: pr * S + (j + 1) * 1024],
                                   aqk, sb_gidx[:, pr * 4 + j: pr * 4 + j + 1],
                                   after=[cc_qk, t_gidx], slot="ld")
                    t = p_.igather(sb_k[:, pr * S + j * 1024: pr * S + (j + 1) * 1024],
                                   aqk, sb_gidx[:, 16 + pr * 4 + j: 16 + pr * 4 + j + 1],
                                   after=[cc_qk], slot="ld")
            for tc in range(32):
                t = p_.igather(sb_v[:, tc * 512:(tc + 1) * 512], av,
                               sb_gidx[:, 32 + tc: 32 + tc + 1],
                               after=[cc_v], slot="ld")
            return t

        with ExitStack() as stx:
            emit_attn(p, stx, bk, aio, True, loader=loader0, o_dt=F32)
        cc_o = coll("o", o_c_ap, ao, [p.slot_last.get("ds_od")])
        p.barrier()

        def o_loader(p_, xb, tt_, after):
            t = None
            for r in range(4):
                t = p_.igather(xb[:, r * 512:(r + 1) * 512], ao,
                               sb_gidx[:, 64 + tt_ * 4 + r: 64 + tt_ * 4 + r + 1],
                               after=[cc_o, t_gidx] + list(after), slot=f"ox{tt_ % 2}")
            return t

        def mid_cb(p_):
            evtoks = [p_.slot_last.get(f"ds_ev{k}") for k in range(4)]
            ccd["qk1"] = coll("qk1", io["qk1_out"], aqk1, evtoks)
            ccd["v1"] = coll("v1", io["v1_out"], av1, [ccd["qk1"]])

        with ExitStack() as stx:
            emit_mid(p, stx, bk, io, resid, fused=True, o_loader=o_loader,
                     ev_dt=BF16, qkv_done_cb=mid_cb)
        cc_qk1 = ccd["qk1"]
        cc_v1 = ccd["v1"]
        p.barrier()

        aio1 = {"maskin": io["maskin"], "idf": io["idf"], "o_out": o1_c_ap}

        def loader1(p_, sb_q, sb_k, sb_v, sb_m, c_idf):
            p_.dma("sync", sb_m[:, :].rearrange("p (ms q) -> p ms q", ms=4),
                   io["maskin"].rearrange("(ms p) q -> p ms q", p=128), slot="ldm")
            p_.dma("sync", c_idf[:, :], io["idf"], slot="ldm")
            t = None
            for pr in range(4):
                for j in range(4):
                    t = p_.igather(sb_q[:, pr * S + j * 1024: pr * S + (j + 1) * 1024],
                                   aqk1, sb_gidx[:, 96 + pr * 4 + j: 96 + pr * 4 + j + 1],
                                   after=[cc_qk1], slot="ld")
                    t = p_.igather(sb_k[:, pr * S + j * 1024: pr * S + (j + 1) * 1024],
                                   aqk1, sb_gidx[:, 112 + pr * 4 + j: 112 + pr * 4 + j + 1],
                                   after=[cc_qk1], slot="ld")
            for tc in range(32):
                t = p_.igather(sb_v[:, tc * 512:(tc + 1) * 512], av1,
                               sb_gidx[:, 128 + tc: 128 + tc + 1],
                               after=[cc_v1], slot="ld")
            return t

        with ExitStack() as stx:
            emit_attn(p, stx, bk, aio1, False, loader=loader1, o_dt=BF16)
        cc_o1 = coll("o1", o1_c_ap, ao1, [p.slot_last.get("ds_od")])
        p.barrier()

        ao1v = ao1.rearrange("r (j t) -> (r j) t", j=4)

        def o1_loader(p_, o1_fm):
            t = None
            for dc in range(NDC):
                t = p_.igather(o1_fm[:, dc * TPC:(dc + 1) * TPC], ao1v,
                               sb_gidx[:, 160 + dc: 160 + dc + 1],
                               after=[cc_o1], slot="o1")
            return t

        fio = {"w_out": io["w_out"], "y_out": io["y_out"]}
        with ExitStack() as stx:
            emit_fin(p, stx, bk, fio, resid, fused=True, o1_loader=o1_loader)

        if dbg:
            p.barrier()
            for nm, ap in (("qk_out", io["qk_out"]), ("v_out", io["v_out"]),
                           ("aqk", aqk), ("o_c", o_c_ap), ("ao", ao),
                           ("qk1_out", io["qk1_out"]), ("v1_out", io["v1_out"]),
                           ("o1_c", o1_c_ap), ("ao1", ao1)):
                dt_ = ap.tensor.dtype if hasattr(ap, "tensor") else BF16
                dbg_t = nc.dram_tensor("dbg_" + nm, list(ap.shape), dt_,
                                       kind="ExternalOutput").ap()
                p.out_dma("gpsimd", dbg_t, ap, slot="dbg")
        p.render(ctx)
    return p.nc


def host_gidx(c):
    j0 = c % 4
    idx = np.zeros((176, 128), np.uint64)
    pp = np.arange(128, dtype=np.uint64)
    for pr in range(4):
        for j in range(4):
            idx[pr * 4 + j] = j * 4096 + j0 * 1024 + pr * 128 + pp
            idx[16 + pr * 4 + j] = j * 4096 + j0 * 1024 + 512 + pr * 128 + pp
            idx[96 + pr * 4 + j] = j * 4096 + j0 * 1024 + pr * 128 + pp
            idx[112 + pr * 4 + j] = j * 4096 + j0 * 1024 + 512 + pr * 128 + pp
    for tc in range(32):
        idx[32 + tc] = (tc // 8) * 4096 + j0 * 1024 + (tc % 8) * 128 + pp
        idx[128 + tc] = (tc // 8) * 4096 + j0 * 1024 + (tc % 8) * 128 + pp
    for tt_ in range(8):
        for r in range(4):
            idx[64 + tt_ * 4 + r] = r * 4096 + j0 * 1024 + tt_ * 128 + pp
    for dc in range(NDC):
        idx[160 + dc] = ((dc // 4) * 512 + (dc % 4) * 128 + pp) * 4 + j0
    return np.ascontiguousarray(idx.T.astype(np.uint32))


def host_inputs_fused(inputs):
    pre = host_inputs_pre0(inputs)
    mid = host_mid_inputs(inputs, [np.zeros((4096, 512), np.float32)] * 8,
                          [np.zeros((D, TPC), np.float32)] * 8)
    w_out = np.ascontiguousarray(np.asarray(inputs["L1_attn_w_out"], np.float32))
    mask = np.tril(np.ones((512, 512), np.float32)).T.astype(BF)
    maps = []
    for c in range(NCORE):
        m = dict(pre[c])
        for kk in ("wot", "wqkv", "wqg_t", "wqu_t", "wqd_t", "sg", "su", "sd"):
            m[kk] = mid[c][kk]
        m["w_out"] = w_out
        m["maskin"] = mask
        m["gidx"] = host_gidx(c)
        maps.append(m)
    return maps



def _esel():
    e = np.zeros((8, 512), np.float32)
    for ec in range(4):
        for pp in range(128):
            e[2 * ec + pp // 64, ec * 128 + pp] = 1.0
    return e.astype(BF)


def host_inputs_pre0(inputs):
    x = np.ascontiguousarray(np.asarray(inputs["L0_attn_x"], np.float32).reshape(B * S, D))
    shared = {
        "wqt": np.ascontiguousarray(np.asarray(inputs["L0_attn_w_q"], np.float32).T),
        "wkt": np.ascontiguousarray(np.asarray(inputs["L0_attn_w_k"], np.float32).T),
        "wvt": np.ascontiguousarray(np.asarray(inputs["L0_attn_w_v"], np.float32).T),
        "a_dt": np.ascontiguousarray(np.asarray(inputs["L0_ffn_A"], np.float32)
                                     .transpose(2, 0, 1).reshape(D, 512)),
        "b_allt": np.ascontiguousarray(np.asarray(inputs["L0_ffn_B"], np.float32)
                                       .transpose(0, 2, 1).reshape(512, D)),
        "wrt": np.ascontiguousarray(np.asarray(inputs["L0_ffn_Wr"], np.float32).T),
        "br": np.ascontiguousarray(np.asarray(inputs["L0_ffn_br"], np.float32).reshape(1, 8)),
        "esel": _esel(),
        "idf": np.eye(128, dtype=np.float32),
        "idb": np.eye(128).astype(BF),
    }
    return [dict(shared, x=np.ascontiguousarray(x[c * TPC:(c + 1) * TPC])) for c in range(NCORE)]




def host_attn_inputs(qk_list, v_list, l0):
    mask = np.tril(np.ones((512, 512), np.float32)).T.astype(BF)
    idf = np.eye(128, dtype=np.float32)
    maps = []
    for a in range(NCORE):
        b, hg = a // 4, a % 4
        qin = np.concatenate([qk_list[4 * b + j][hg * 1024: hg * 1024 + 512, :]
                              for j in range(4)], axis=1)
        kin = np.concatenate([qk_list[4 * b + j][hg * 1024 + 512: hg * 1024 + 1024, :]
                              for j in range(4)], axis=1)
        vin = np.concatenate([v_list[4 * b + j][hg * 1024:(hg + 1) * 1024, :]
                              for j in range(4)], axis=0)
        maps.append(dict(qin=np.ascontiguousarray(qin), kin=np.ascontiguousarray(kin),
                         vin=np.ascontiguousarray(vin), maskin=mask, idf=idf))
    return maps


def host_mid_inputs(inputs, att0_list, m1_list):
    shared = {
        "wot": np.ascontiguousarray(np.asarray(inputs["L0_attn_w_o"], np.float32).T),
        "wqkv": np.ascontiguousarray(np.asarray(inputs["L1_attn_w_qkv"], np.float32)),
        "wqg_t": np.ascontiguousarray(np.asarray(inputs["L1_ffn_wq_gate"], np.int32).T),
        "wqu_t": np.ascontiguousarray(np.asarray(inputs["L1_ffn_wq_up"], np.int32).T),
        "wqd_t": np.ascontiguousarray(np.asarray(inputs["L1_ffn_wq_down"], np.int32).T),
        "sg": np.ascontiguousarray(np.asarray(inputs["L1_ffn_s_gate"], np.float32)
                                   .reshape(NHC, 128).T),
        "su": np.ascontiguousarray(np.asarray(inputs["L1_ffn_s_up"], np.float32)
                                   .reshape(NHC, 128).T),
        "sd": np.ascontiguousarray(np.asarray(inputs["L1_ffn_s_down"], np.float32)
                                   .reshape(NDC, 128).T),
        "idf": np.eye(128, dtype=np.float32),
        "idb": np.eye(128).astype(BF),
    }
    maps = []
    for c in range(NCORE):
        b, j = c // 4, c % 4
        o_as = np.concatenate([att0_list[4 * b + hg][j * 1024:(j + 1) * 1024, :]
                               for hg in range(4)], axis=1)
        maps.append(dict(shared, o_as=np.ascontiguousarray(o_as),
                         m1_in=np.ascontiguousarray(m1_list[c])))
    return maps


def host_fin_inputs(inputs, att1_list, m2_list):
    w_out = np.ascontiguousarray(np.asarray(inputs["L1_attn_w_out"], np.float32))
    maps = []
    for c in range(NCORE):
        b, j = c // 4, c % 4
        o1 = np.concatenate([att1_list[4 * b + hg][:, j * 1024:(j + 1) * 1024]
                             for hg in range(4)], axis=0)
        maps.append(dict(w_out=w_out, o1=np.ascontiguousarray(o1),
                         m2_in=np.ascontiguousarray(m2_list[c])))
    return maps


def _run(nc, in_maps):
    res = run_bass_kernel_spmd(nc, in_maps, core_ids=list(range(NCORE)))
    return res.results


def kernel(**inputs):
    nc = build_fused()
    in_maps = host_inputs_fused(inputs)
    res = run_bass_kernel_spmd(nc, in_maps, core_ids=list(range(NCORE)))
    out = np.concatenate([np.asarray(r["y_out"]).T for r in res.results], axis=0)
    return np.ascontiguousarray(out.reshape(B, S, D).astype(np.float32))


def kernel_unfused(**inputs):
    r1 = _run(build_pre0(), host_inputs_pre0(inputs))
    qk_l = [np.asarray(r["qk_out"]) for r in r1]
    v_l = [np.asarray(r["v_out"]) for r in r1]
    m1_l = [np.asarray(r["m1_out"]) for r in r1]
    r2 = _run(build_attn(True), host_attn_inputs(qk_l, v_l, l0=True))
    att0 = [np.asarray(r["o_out"]) for r in r2]
    r3 = _run(build_mid(), host_mid_inputs(inputs, att0, m1_l))
    qk1 = [np.asarray(r["qk1_out"]) for r in r3]
    v1 = [np.asarray(r["v1_out"]) for r in r3]
    m2_l = [np.asarray(r["m2_out"]) for r in r3]
    r4 = _run(build_attn(False), host_attn_inputs(qk1, v1, l0=False))
    att1 = [np.asarray(r["o_out"]) for r in r4]
    r5 = _run(build_fin(), host_fin_inputs(inputs, att1, m2_l))
    out = np.concatenate([np.asarray(r["y_out"]).T for r in r5], axis=0)
    return np.ascontiguousarray(out.reshape(B, S, D).astype(np.float32))


# version 14
# speedup vs baseline: 1.0128x; 1.0128x over previous
import numpy as np
import ml_dtypes
from contextlib import ExitStack
from collections import defaultdict

import concourse.bass as bass
import concourse.mybir as mybir
from concourse.alu_op_type import AluOpType as ALU
from concourse.bass_utils import run_bass_kernel_spmd

F32 = mybir.dt.float32
BF16 = mybir.dt.bfloat16
I32 = mybir.dt.int32
AF = mybir.ActivationFunctionType
AX = mybir.AxisListType
BF = ml_dtypes.bfloat16

NCORE = 8
B, S, D = 2, 4096, 2048
H, HD = 16, 128
TPC = 1024
NDC = D // 128
EPS = 1e-5
ISQD = float(1.0 / np.sqrt(128.0))

ENGINES = ("sync", "tensor", "vector", "scalar", "gpsimd")


class Prog:

    def __init__(self):
        self.nc = bass.Bass("TRN2", target_bir_lowering=False)
        self.items = {e: [] for e in ENGINES}
        self.counts = defaultdict(int)
        self.lastw = {e: defaultdict(int) for e in ENGINES}
        self.tok_engine = {}
        self.out_toks = []

    def _wait(self, eng, sem, val):
        if val > self.lastw[eng][sem]:
            self.items[eng].append(("w", sem, val))
            self.lastw[eng][sem] = val

    def emit(self, eng, fn, after=(), inc=1, sem=None):
        if inc == 1 and eng != "tensor":
            v = self.counts["e_" + eng]
            if v > 0:
                self._wait(eng, "e_" + eng, v)
        if inc == 0:
            for t in after:
                if t is None or self.tok_engine.get(t) == eng:
                    continue
                self._wait(eng, t[0], t[1])
            self.items[eng].append(("q", fn))
            return None
        for t in after:
            if t is None:
                continue
            self._wait(eng, t[0], t[1])
        if sem is None:
            sem = ("d_" if inc == 16 else "e_") + eng
        self.counts[sem] += inc
        tok = (sem, self.counts[sem])
        self.items[eng].append(("i", fn, sem, inc))
        self.tok_engine[tok] = eng
        return tok

    def dma(self, eng, out, in_, after=(), slot=None, **kw):
        sem = "ds_" + (slot if slot else "anon_" + eng)
        t = self.emit(eng, (lambda e, o=out, i=in_, kw=kw: e.dma_start(out=o, in_=i, **kw)),
                      after=after, inc=16, sem=sem)
        self.slot_last = getattr(self, "slot_last", {})
        self.slot_last[sem] = t
        return t

    def igather(self, dst, src_ap, idx_ap, after=(), slot="ig"):
        sem = "ds_" + slot
        t = self.emit("gpsimd",
                      (lambda e, o=dst, i=src_ap, x=idx_ap:
                       e.indirect_dma_start(out=o, out_offset=None, in_=i,
                                            in_offset=bass.IndirectOffsetOnAxis(x, 0))),
                      after=after, inc=16, sem=sem)
        self.slot_last = getattr(self, "slot_last", {})
        self.slot_last[sem] = t
        return t

    def out_dma(self, eng, out, in_, after=(), slot=None, **kw):
        t = self.dma(eng, out, in_, after=after, slot=(slot or "out_" + eng), **kw)
        self.out_toks.append(t)
        return t

    def barrier(self, exclude=("cc",)):
        snap = dict(self.counts)
        for eng in ENGINES:
            for sem, val in snap.items():
                if sem in exclude:
                    continue
                self._wait(eng, sem, val)

    def check(self):
        pos = {e: 0 for e in ENGINES}
        cnt = defaultdict(int)
        progressed = True
        while progressed:
            progressed = False
            for e in ENGINES:
                while pos[e] < len(self.items[e]):
                    it = self.items[e][pos[e]]
                    if it[0] == "w" and cnt[it[1]] < it[2]:
                        break
                    if it[0] == "i":
                        cnt[it[2]] += it[3]
                    pos[e] += 1
                    progressed = True
        stuck = {e: (pos[e], len(self.items[e])) for e in ENGINES if pos[e] < len(self.items[e])}
        if stuck:
            raise RuntimeError(f"build-time deadlock at {stuck}")

    def render(self, ctx):
        fin = {}
        for t in self.out_toks:
            fin[t[0]] = max(fin.get(t[0], 0), t[1])
        for s, v in sorted(fin.items()):
            self._wait("gpsimd", s, v)
        self.check()
        sems = {s: ctx.enter_context(self.nc.semaphore(s)) for s in sorted(self.counts)}
        block = ctx.enter_context(self.nc.Block())

        def body(eng_name):
            def _(eng):
                for it in self.items[eng_name]:
                    if it[0] == "w":
                        eng.wait_ge(sems[it[1]], it[2])
                    elif it[0] == "q":
                        it[1](eng)
                    else:
                        it[1](eng).then_inc(sems[it[2]], it[3])
            return _

        for e in ENGINES:
            getattr(block, e)(body(e))


class Guard:

    def __init__(self):
        self.readers = []
        self.writer = None

    def rd(self, tok):
        self.readers.append(tok)

    def war(self):
        return list(self.readers) + ([self.writer] if self.writer else [])

    def wrote(self, tok):
        self.writer = tok
        self.readers = []


class Banks:

    def __init__(self, p, ctx, nf32=6, nbf=2):
        self.p = p
        self.t = {}
        self.wr = {}
        self.rd = {}
        self.pools = {"f": [], "b": []}
        self.i = {"f": 0, "b": 0}
        for k in range(nf32):
            n = f"pbf{k}"
            self.t[n] = ctx.enter_context(p.nc.psum_tensor(n, [128, 512], F32))
            self.wr[n] = None
            self.rd[n] = None
            self.pools["f"].append(n)
        for k in range(nbf):
            n = f"pbb{k}"
            self.t[n] = ctx.enter_context(p.nc.psum_tensor(n, [128, 1024], BF16))
            self.wr[n] = None
            self.rd[n] = None
            self.pools["b"].append(n)

    def take(self, kind="f"):
        pool = self.pools[kind]
        n = pool[self.i[kind]]
        self.i[kind] = (self.i[kind] + 1) % len(pool)
        return n

    def matmuls(self, n, out_ap, mms, extra_after=()):
        p = self.p
        after = [self.rd[n]] + list(extra_after)
        tok = None
        for j, (lhsT, rhs) in enumerate(mms):
            last = (j == len(mms) - 1)
            tok = p.emit("tensor",
                         (lambda e, o=out_ap, l=lhsT, r=rhs, st=(j == 0), sp=last:
                          e.matmul(o, l, r, start=st, stop=sp)),
                         after=after if j == 0 else (), inc=1 if last else 0)
        self.wr[n] = tok
        return tok

    def transpose(self, n, out_ap, src, ident, extra_after=()):
        p = self.p
        tok = p.emit("tensor",
                     (lambda e, o=out_ap, s=src, idn=ident: e.transpose(o, s, idn)),
                     after=[self.rd[n]] + list(extra_after))
        self.wr[n] = tok
        return tok

    def evac(self, n, eng, fn, after=()):
        tok = self.p.emit(eng, fn, after=[self.wr[n]] + list(after))
        self.rd[n] = tok
        return tok



def ts(out, in0, s1, op0, s2=None, op1=None):
    kw = {} if op1 is None else {"op1": op1}
    return (lambda e, o=out, i=in0, a=s1, b=s2, x=op0, kw=kw:
            e.tensor_scalar(out=o, in0=i, scalar1=a, scalar2=b, op0=x, **kw))


def tt(out, in0, in1, op):
    return (lambda e, o=out, a=in0, b=in1, x=op: e.tensor_tensor(out=o, in0=a, in1=b, op=x))


def stt(out, in0, scalar, in1, op0, op1):
    return (lambda e, o=out, i=in0, s=scalar, j=in1, x=op0, y=op1:
            e.scalar_tensor_tensor(out=o, in0=i, scalar=s, in1=j, op0=x, op1=y))


def act(out, in_, func, bias=None, scale=None, accum=None):
    kw = {}
    if bias is not None:
        kw["bias"] = bias
    if scale is not None:
        kw["scale"] = scale
    if accum is not None:
        kw["accum_out"] = accum
    return (lambda e, o=out, i=in_, f=func, kw=kw: e.activation(o, i, f, **kw))


def scp(out, in_):
    return (lambda e, o=out, i=in_: e.copy(o, i))


def cp(out, in_):
    return (lambda e, o=out, i=in_: e.tensor_copy(o, i))


def red(out, in_, op, absval=None, negate=None):
    kw = {}
    if absval:
        kw["apply_absolute_value"] = True
    if negate:
        kw["negate"] = True
    return (lambda e, o=out, i=in_, op=op, kw=kw:
            e.tensor_reduce(out=o, in_=i, axis=AX.X, op=op, **kw))


def recip(out, in_):
    return (lambda e, o=out, i=in_: e.reciprocal(o, i))


def mset(ap, v):
    return (lambda e, a=ap, v=v: e.memset(a, v))


def bnstats(out, in_):
    return (lambda e, o=out, i=in_: e.bn_stats(o, i))


def bnaggr(out, in_):
    return (lambda e, o=out, i=in_: e.bn_aggr(o, i))



def pre0_dram(nc, inter=False):
    io = {}
    io["x"] = nc.dram_tensor("x", [TPC, D], F32, kind="ExternalInput").ap()
    for w in ("wqt", "wkt", "wvt"):
        io[w] = nc.dram_tensor(w, [D, D], F32, kind="ExternalInput").ap()
    io["a_dt"] = nc.dram_tensor("a_dt", [D, 512], F32, kind="ExternalInput").ap()
    io["b_allt"] = nc.dram_tensor("b_allt", [512, D], F32, kind="ExternalInput").ap()
    io["wrt"] = nc.dram_tensor("wrt", [D, 8], F32, kind="ExternalInput").ap()
    io["br"] = nc.dram_tensor("br", [1, 8], F32, kind="ExternalInput").ap()
    io["esel"] = nc.dram_tensor("esel", [8, 512], BF16, kind="ExternalInput").ap()
    io["idf"] = nc.dram_tensor("idf", [128, 128], F32, kind="ExternalInput").ap()
    io["idb"] = nc.dram_tensor("idb", [128, 128], BF16, kind="ExternalInput").ap()
    dt = BF16 if inter else F32
    kind = {} if inter else {"kind": "ExternalOutput"}
    io["qk_out"] = nc.dram_tensor("qk_out", [4096, TPC], dt, **kind).ap()
    io["v_out"] = nc.dram_tensor("v_out", [4096, 512], dt, **kind).ap()
    if not inter:
        io["m1_out"] = nc.dram_tensor("m1_out", [D, TPC], F32, kind="ExternalOutput").ap()
    return io


def build_pre0(phases="ABC"):
    p = Prog()
    io = pre0_dram(p.nc)
    with ExitStack() as ctx:
        resid = ctx.enter_context(p.nc.sbuf_tensor("resid", [128, NDC * TPC], F32))
        bk = Banks(p, ctx)
        emit_pre0(p, ctx, bk, io, resid, fused=False)
        p.render(ctx)
    return p.nc


def emit_pre0(p, ctx, bk, io, resid, fused, qkv_done_cb=None):
    nc = p.nc
    x_in = io["x"]
    wts = {w: io[w] for w in ("wqt", "wkt", "wvt")}
    a_dt, b_allt, wrt, br, esel = io["a_dt"], io["b_allt"], io["wrt"], io["br"], io["esel"]
    idf, idb = io["idf"], io["idb"]
    qk_out, v_out = io["qk_out"], io["v_out"]
    m1_out = io.get("m1_out")
    EVDT = BF16 if fused else F32
    if True:
        sb = lambda n, s, d: ctx.enter_context(nc.sbuf_tensor(n, s, d, allow_name_mangling=True))
        xq_fm = sb("xq_fm", [128, NDC * TPC], BF16)
        invsx_c = sb("invsx_c", [128, 8], F32)
        invsx_b = sb("invsx_b", [128, TPC], F32)
        invsx_r = sb("invsx_r", [1, TPC], F32)
        c_idf = sb("c_idf", [128, 128], F32)
        c_idb = sb("c_idb", [128, 128], BF16)
        ones_cf = sb("ones_cf", [128, 1], F32)
        ones_rf = sb("ones_rf", [1, 128], F32)
        ones_rb = sb("ones_rb", [1, 128], BF16)

        cAB = ExitStack()
        xn_fm = cAB.enter_context(nc.sbuf_tensor("xn_fm", [128, NDC * TPC], BF16))

        p.dma("sync", c_idf[:, :], idf, slot="init")
        t_init = p.dma("sync", c_idb[:, :], idb, slot="init")
        t_idf = t_idb = t_init
        t_ocf = p.emit("gpsimd", mset(ones_cf[:, :], 1.0))
        t_orf = p.emit("gpsimd", mset(ones_rf[:, :], 1.0))
        t_orb = p.emit("gpsimd", mset(ones_rb[:, :], 1.0))

        with ExitStack() as c2:
            sb2 = lambda n, s, d: c2.enter_context(nc.sbuf_tensor(n, s, d, allow_name_mangling=True))
            t_x = [sb2(f"t_x{i}", [128, D], F32) for i in range(2)]
            t_xn = sb2("t_xn", [128, D], F32)
            t_v = sb2("t_v", [128, D], F32)
            t_i = sb2("t_i", [128, D], I32)
            t_kf = sb2("t_kf", [128, D], BF16)
            t_ca = sb2("t_ca", [128, D], BF16)
            t_cb = sb2("t_cb", [128, D], BF16)
            t_xq = sb2("t_xq", [128, D], BF16)
            t_xnb = sb2("t_xnb", [128, D], BF16)
            t_bn = sb2("t_bn", [128, 24], F32)
            t_mv = sb2("t_mv", [128, 2], F32)
            t_s = [sb2(f"t_s{i}", [128, 1], F32) for i in range(6)]
            g_x = [Guard(), Guard()]
            g_xq, g_xnb, g_s1, g_s2 = Guard(), Guard(), Guard(), Guard()
            ivr_toks = []

            for tt_ in range(8):
                xb = t_x[tt_ % 2]
                gx = g_x[tt_ % 2]
                tok_x = p.dma("sync", xb[:, :], x_in[tt_ * 128:(tt_ + 1) * 128, :],
                              after=gx.war(), slot=f"x{tt_ % 2}")
                gx.wrote(tok_x)
                for dc in range(NDC):
                    n = bk.take("f")
                    w = bk.transpose(n, bk.t[n][:, 0:128].bitcast(F32),
                                     xb[:, dc * 128:(dc + 1) * 128], c_idf[:, :],
                                     extra_after=[tok_x, t_idf])
                    gx.rd(w)
                    e = "scalar" if dc % 2 else "vector"
                    dst = resid[:, dc * TPC + tt_ * 128: dc * TPC + (tt_ + 1) * 128]
                    fn = scp(dst, bk.t[n][:, 0:128]) if e == "scalar" else cp(dst, bk.t[n][:, 0:128])
                    bk.evac(n, e, fn)
                for qq in range(4):
                    a1 = p.emit("vector", bnstats(t_bn[:, qq * 6:(qq + 1) * 6],
                                                  xb[:, qq * 512:(qq + 1) * 512]),
                                after=[tok_x])
                    gx.rd(a1)
                p.emit("vector", bnaggr(t_mv[:, :], t_bn[:, :]))
                a3 = p.emit("vector", ts(t_s[0][:, :], t_mv[:, 1:2], EPS, ALU.add))
                a4 = p.emit("vector", recip(t_s[1][:, :], t_s[0][:, :]), after=g_s1.war())
                g_s1.wrote(a4)
                a5 = p.emit("scalar", act(t_s[2][:, :], t_s[1][:, :], AF.Sqrt),
                            after=[a4] + g_s2.war())
                g_s1.rd(a5)
                g_s2.wrote(a5)
                a6 = p.emit("vector", ts(t_xn[:, :], xb[:, :], t_mv[:, 0:1], ALU.subtract,
                                         t_s[2][:, :], ALU.mult), after=[a5])
                g_s2.rd(a6)
                gx.rd(a6)
                a7 = p.emit("scalar", scp(t_xnb[:, :], t_xn[:, :]),
                            after=[a6] + g_xnb.war())
                g_xnb.wrote(a7)
                p.emit("vector", red(t_s[3][:, :], t_xn[:, :], ALU.max, absval=True),
                       after=[a6])
                p.emit("vector", ts(t_s[3][:, :], t_s[3][:, :], 1e-5, ALU.max))
                p.emit("vector", recip(t_s[4][:, :], t_s[3][:, :]))
                p.emit("vector", ts(t_s[5][:, :], t_s[4][:, :], 127.0, ALU.mult))
                ivc = p.emit("vector", ts(invsx_c[:, tt_:tt_ + 1], t_s[3][:, :],
                                          1.0 / 127.0, ALU.mult))
                p.emit("vector", ts(t_v[:, :], t_xn[:, :], t_s[5][:, :], ALU.mult),
                       after=[a7])
                p.emit("vector", cp(t_i[:, :], t_v[:, :]))
                p.emit("vector", cp(t_kf[:, :], t_i[:, :]))
                p.emit("vector", tt(t_v[:, :], t_v[:, :], t_kf[:, :], ALU.subtract))
                p.emit("vector", stt(t_ca[:, :], t_v[:, :], 0.5, t_kf[:, :],
                                     ALU.is_gt, ALU.add))
                p.emit("vector", ts(t_cb[:, :], t_v[:, :], -0.5, ALU.is_lt))
                p.emit("vector", tt(t_xq[:, :], t_ca[:, :], t_cb[:, :], ALU.subtract),
                       after=g_xq.war())
                p.emit("vector", ts(t_i[:, :], t_i[:, :], 1, ALU.bitwise_and))
                p.emit("vector", stt(t_ca[:, :], t_v[:, :], 0.5, t_i[:, :],
                                     ALU.is_equal, ALU.mult))
                p.emit("vector", stt(t_cb[:, :], t_v[:, :], -0.5, t_i[:, :],
                                     ALU.is_equal, ALU.mult))
                p.emit("vector", tt(t_xq[:, :], t_xq[:, :], t_ca[:, :], ALU.add))
                aq = p.emit("vector", tt(t_xq[:, :], t_xq[:, :], t_cb[:, :], ALU.subtract))
                g_xq.wrote(aq)
                for dc in range(NDC):
                    n = bk.take("b")
                    w = bk.transpose(n, bk.t[n][:, 0:128],
                                     t_xq[:, dc * 128:(dc + 1) * 128], c_idb[:, :],
                                     extra_after=[aq, t_idb])
                    g_xq.rd(w)
                    e = "scalar" if dc % 2 else "vector"
                    dst = xq_fm[:, dc * TPC + tt_ * 128: dc * TPC + (tt_ + 1) * 128]
                    fn = scp(dst, bk.t[n][:, 0:128]) if e == "scalar" else cp(dst, bk.t[n][:, 0:128])
                    bk.evac(n, e, fn)
                    n = bk.take("b")
                    w = bk.transpose(n, bk.t[n][:, 0:128],
                                     t_xnb[:, dc * 128:(dc + 1) * 128], c_idb[:, :],
                                     extra_after=[a7, t_idb])
                    g_xnb.rd(w)
                    dst = xn_fm[:, dc * TPC + tt_ * 128: dc * TPC + (tt_ + 1) * 128]
                    fn = scp(dst, bk.t[n][:, 0:128]) if e == "scalar" else cp(dst, bk.t[n][:, 0:128])
                    bk.evac(n, e, fn)
                n = bk.take("f")
                w = bk.transpose(n, bk.t[n][0:1, 0:128].bitcast(F32),
                                 invsx_c[:, tt_:tt_ + 1], c_idf[:, :],
                                 extra_after=[ivc, t_idf])
                ivr_toks.append(bk.evac(n, "vector",
                                        cp(invsx_r[0:1, tt_ * 128:(tt_ + 1) * 128],
                                           bk.t[n][0:1, 0:128])))
            for cc in range(2):
                n = bk.take("f")
                mm = [(ones_rf[0:1, :], invsx_r[0:1, cc * 512:(cc + 1) * 512])]
                bk.matmuls(n, bk.t[n][:, 0:512], mm, extra_after=ivr_toks + [t_orf])
                bk.evac(n, "vector", cp(invsx_b[:, cc * 512:(cc + 1) * 512],
                                        bk.t[n][:, 0:512]))

        p.barrier()
        NBLK, BW = 8, 256
        with ExitStack() as c2:
            sb2 = lambda n, s, d: c2.enter_context(nc.sbuf_tensor(n, s, d, allow_name_mangling=True))
            t_w = [sb2(f"t_w{i}", [128, NDC * BW], F32) for i in range(2)]
            t_b1 = sb2("t_b1", [128, NDC * BW], BF16)
            t_tern2 = [sb2(f"t_tern{i}", [128, NDC * BW], BF16) for i in range(2)]
            t_acc = sb2("t_acc", [128, NBLK], F32)
            t_g = [sb2(f"t_g{i}", [1, 1], F32) for i in range(3)]
            t_thr = sb2("t_thr", [128, 1], F32)
            t_nthr = sb2("t_nthr", [128, 1], F32)
            t_sw = sb2("t_sw", [128, 1], F32)
            t_ev = [sb2(f"t_ev{i}", [128, 512], EVDT) for i in range(4)]
            g_w = [Guard(), Guard()]
            g_tern2 = [Guard(), Guard()]
            g_acc, g_b1 = Guard(), Guard()
            g_ev = [Guard() for _ in range(4)]

            for wi, wname in enumerate(("wqt", "wkt", "wvt")):
                wT = wts[wname]
                acc_toks = []
                for blk in range(NBLK):
                    wb = t_w[blk % 2]
                    gw = g_w[blk % 2]
                    tok = p.dma("sync", wb[:, :].rearrange("p (ec o) -> p ec o", ec=NDC),
                                wT[:, blk * BW:(blk + 1) * BW]
                                .rearrange("(ec p) o -> p ec o", p=128),
                                after=gw.war(), slot=f"w{blk % 2}")
                    gw.wrote(tok)
                    a = p.emit("scalar", act(t_b1[:, :], wb[:, :], AF.Abs,
                                             accum=t_acc[:, blk:blk + 1]),
                               after=[tok] + (g_acc.war() + g_b1.war() if blk == 0 else []))
                    gw.rd(a)
                    if blk == 0:
                        g_b1.wrote(a)
                    acc_toks.append(a)
                g_acc.wrote(acc_toks[-1])
                g_b1.wrote(acc_toks[-1])
                s1 = p.emit("vector", red(t_thr[:, :], t_acc[:, :], ALU.add),
                            after=acc_toks)
                g_acc.rd(s1)
                n = bk.take("f")
                bk.matmuls(n, bk.t[n][0:1, 0:1], [(t_thr[:, :], ones_cf[:, :])],
                           extra_after=[s1, t_ocf])
                bk.evac(n, "vector", ts(t_g[0][0:1, :], bk.t[n][0:1, 0:1],
                                        1.0 / (D * D), ALU.mult))
                p.emit("vector", ts(t_g[1][0:1, :], t_g[0][0:1, :], 1e-5, ALU.max))
                g3 = p.emit("vector", ts(t_g[2][0:1, :], t_g[1][0:1, :], 0.5, ALU.mult))
                n = bk.take("f")
                bk.matmuls(n, bk.t[n][:, 0:1], [(ones_rf[0:1, :], t_g[1][0:1, :])],
                           extra_after=[g3, t_orf])
                scale_const = ISQD if wi == 0 else 1.0
                bk.evac(n, "vector", ts(t_sw[:, :], bk.t[n][:, 0:1], scale_const, ALU.mult))
                n = bk.take("f")
                bk.matmuls(n, bk.t[n][:, 0:1], [(ones_rf[0:1, :], t_g[2][0:1, :])],
                           extra_after=[g3, t_orf])
                bk.evac(n, "vector", cp(t_thr[:, :], bk.t[n][:, 0:1]))
                p.emit("vector", ts(t_nthr[:, :], t_thr[:, :], -1.0, ALU.mult))

                for blk in range(NBLK):
                    wb = t_w[blk % 2]
                    gw = g_w[blk % 2]
                    tok = p.dma("sync", wb[:, :].rearrange("p (ec o) -> p ec o", ec=NDC),
                                wT[:, blk * BW:(blk + 1) * BW]
                                .rearrange("(ec p) o -> p ec o", p=128),
                                after=gw.war(), slot=f"w{blk % 2}")
                    gw.wrote(tok)
                    t_tern = t_tern2[blk % 2]
                    g_tern = g_tern2[blk % 2]
                    b1 = p.emit("vector", ts(t_b1[:, :], wb[:, :], t_nthr[:, :], ALU.is_lt),
                                after=[tok] + g_b1.war())
                    g_b1.wrote(b1)
                    gw.rd(b1)
                    tern = p.emit("vector", stt(t_tern[:, :], wb[:, :], t_thr[:, :],
                                                t_b1[:, :], ALU.is_gt, ALU.subtract),
                                  after=g_tern.war())
                    gw.rd(tern)
                    g_b1.rd(tern)
                    g_tern.wrote(tern)
                    if wi < 2:
                        for ob in range(2):
                            for cc_ in range(2):
                                n = bk.take("f")
                                mms = [(t_tern[:, dc * BW + ob * 128:dc * BW + (ob + 1) * 128],
                                        xq_fm[:, dc * TPC + cc_ * 512:dc * TPC + cc_ * 512 + 512])
                                       for dc in range(NDC)]
                                m = bk.matmuls(n, bk.t[n][:, 0:512], mms,
                                               extra_after=[tern])
                                g_tern.rd(m)
                                k = (ob * 2 + cc_) % 4
                                ev = t_ev[k]
                                evt = bk.evac(n, "vector",
                                              stt(ev[:, :], bk.t[n][:, 0:512], t_sw[:, :],
                                                  invsx_b[:, cc_ * 512:cc_ * 512 + 512],
                                                  ALU.mult, ALU.mult),
                                              after=g_ev[k].war())
                                g_ev[k].wrote(evt)
                                h = blk * 2 + ob
                                rbase = (h // 4) * 1024 + (0 if wi == 0 else 512) + (h % 4) * 128
                                dt_ = p.out_dma("sync",
                                                qk_out[rbase:rbase + 128,
                                                       cc_ * 512:cc_ * 512 + 512],
                                                ev[:, :], after=[evt], slot=f"ev{k}")
                                g_ev[k].rd(dt_)
                    else:
                        for tt_ in range(8):
                            n = bk.take("f")
                            mms = [(xq_fm[:, dc * TPC + tt_ * 128:dc * TPC + (tt_ + 1) * 128],
                                    t_tern[:, dc * BW:(dc + 1) * BW])
                                   for dc in range(NDC)]
                            m = bk.matmuls(n, bk.t[n][:, 0:BW], mms, extra_after=[tern])
                            g_tern.rd(m)
                            k = tt_ % 4
                            ev = t_ev[k]
                            evt = bk.evac(n, "vector",
                                          ts(ev[:, 0:BW], bk.t[n][:, 0:BW],
                                             invsx_c[:, tt_:tt_ + 1], ALU.mult,
                                             t_sw[:, :], ALU.mult),
                                          after=g_ev[k].war())
                            g_ev[k].wrote(evt)
                            rbase = (blk // 2) * 1024 + tt_ * 128
                            cbase = (blk % 2) * 256
                            dt_ = p.out_dma("sync",
                                            v_out[rbase:rbase + 128, cbase:cbase + BW],
                                            ev[:, 0:BW], after=[evt], slot=f"ev{k}")
                            g_ev[k].rd(dt_)

        if qkv_done_cb is not None:
            qkv_done_cb(p)
        p.barrier()
        resid_toks = {}
        with ExitStack() as c2:
            sb2 = lambda n, s, d: c2.enter_context(nc.sbuf_tensor(n, s, d, allow_name_mangling=True))
            t_A = sb2("t_A", [128, NDC * 512], BF16)
            t_B = sb2("t_B", [128, 4 * D], BF16)
            t_W = sb2("t_W", [128, NDC * 8], BF16)
            t_br = sb2("t_br", [1, 8], BF16)
            t_E = sb2("t_E", [8, 512], BF16)
            t_mwT = sb2("t_mwT", [8, TPC], BF16)
            t_hm = sb2("t_hm", [128, 4 * TPC], BF16)
            t_bc = [sb2(f"t_bc{i}", [128, 512], F32) for i in range(2)]
            t_e8 = sb2("t_e8", [128, 8], F32)
            t_mw = sb2("t_mw", [128, 8], F32)
            t_u = [sb2(f"t_u{i}", [128, 1], F32) for i in range(4)]
            g_e8, g_mw, g_u0 = Guard(), Guard(), Guard()
            g_bc = [Guard(), Guard()]

            t_fst = sb2("t_fst", [128, NDC * 256], F32)
            prev = None
            for half in range(2):
                dd = p.dma("sync",
                           t_fst[:, :].rearrange("p (dc r) -> p dc r", dc=NDC),
                           a_dt[:, half * 256:(half + 1) * 256]
                           .rearrange("(dc p) r -> p dc r", p=128),
                           after=[prev], slot="binit")
                prev = p.emit("vector",
                              cp(t_A[:, :].rearrange("p (dc r) -> p dc r", dc=NDC)
                                 [:, :, half * 256:(half + 1) * 256],
                                 t_fst[:, :].rearrange("p (dc r) -> p dc r", dc=NDC)),
                              after=[dd])
            for half in range(2):
                dd = p.dma("sync",
                           t_fst[:, :].rearrange("p (ec o) -> p ec o", ec=2),
                           b_allt[half * 256:(half + 1) * 256, :]
                           .rearrange("(ec p) o -> p ec o", p=128),
                           after=[prev], slot="binit")
                prev = p.emit("vector",
                              cp(t_B[:, half * 2 * D:(half + 1) * 2 * D], t_fst[:, :]),
                              after=[dd])
            d3 = p.dma("sync", t_fst[:, 0:NDC * 8].rearrange("p (dc e) -> p dc e", dc=NDC),
                       wrt.rearrange("(dc p) e -> p dc e", p=128),
                       after=[prev], slot="binit")
            c3 = p.emit("vector", cp(t_W[:, :], t_fst[:, 0:NDC * 8]), after=[d3])
            d4 = p.dma("sync", t_fst[0:1, 0:8], br, after=[c3], slot="binit")
            c4 = p.emit("vector", cp(t_br[:, :], t_fst[0:1, 0:8]), after=[d4])
            t_ld = p.dma("sync", t_E[:, :], esel, slot="binit2")
            t_binit = p.emit("vector", cp(t_E[0:1, 0:1], t_E[0:1, 0:1]), after=[c4, t_ld])
            tA = tB = tW = tbr = tE = t_binit

            mwT_toks = []
            for tt_ in range(8):
                n = bk.take("f")
                mms = [(xn_fm[:, dc * TPC + tt_ * 128:dc * TPC + (tt_ + 1) * 128],
                        t_W[:, dc * 8:(dc + 1) * 8]) for dc in range(NDC)]
                mms.append((ones_rb[0:1, :], t_br[0:1, :]))
                mtok = bk.matmuls(n, bk.t[n][:, 0:8], mms,
                                  extra_after=[tW, tbr, t_orb])
                nm = p.emit("vector", red(t_u[0][:, :], bk.t[n][:, 0:8], ALU.max,
                                          negate=True), after=[mtok] + g_u0.war())
                g_u0.wrote(nm)
                ex = p.emit("scalar", act(t_e8[:, :], bk.t[n][:, 0:8], AF.Exp,
                                          bias=t_u[0][:, :]),
                            after=[mtok, nm] + g_e8.war())
                g_u0.rd(ex)
                g_e8.wrote(ex)
                bk.rd[n] = ex
                p.emit("vector", red(t_u[1][:, :], t_e8[:, :], ALU.add), after=[ex])
                mx = p.emit("vector", red(t_u[2][:, :], t_e8[:, :], ALU.max), after=[ex])
                p.emit("vector", recip(t_u[3][:, :], t_u[1][:, :]))
                p.emit("vector", stt(t_u[3][:, :], t_u[3][:, :], 1.0 / 64.0, t_u[2][:, :],
                                     ALU.mult, ALU.mult))
                mw = p.emit("vector", ts(t_mw[:, :], t_e8[:, :], t_u[2][:, :], ALU.is_ge,
                                         t_u[3][:, :], ALU.mult), after=g_mw.war())
                g_e8.rd(mw)
                g_mw.wrote(mw)
                n3 = bk.take("f")
                w = bk.transpose(n3, bk.t[n3][0:8, 0:128].bitcast(F32), t_mw[:, :],
                                 c_idf[:, :], extra_after=[mw, t_idf])
                g_mw.rd(w)
                mwT_toks.append(bk.evac(n3, "vector",
                                        cp(t_mwT[0:8, tt_ * 128:(tt_ + 1) * 128],
                                           bk.t[n3][0:8, 0:128])))
            hm_toks = []
            for ec in range(4):
                for cc_ in range(2):
                    j = (ec * 2 + cc_) % 2
                    n = bk.take("f")
                    mms = [(t_A[:, dc * 512 + ec * 128:dc * 512 + (ec + 1) * 128],
                            xn_fm[:, dc * TPC + cc_ * 512:dc * TPC + cc_ * 512 + 512])
                           for dc in range(NDC)]
                    bk.matmuls(n, bk.t[n][:, 0:512], mms, extra_after=[tA])
                    n3 = bk.take("f")
                    bk.matmuls(n3, bk.t[n3][:, 0:512],
                               [(t_E[0:8, ec * 128:(ec + 1) * 128],
                                 t_mwT[0:8, cc_ * 512:cc_ * 512 + 512])],
                               extra_after=[tE] + mwT_toks)
                    bc = bk.evac(n3, "scalar", scp(t_bc[j][:, :], bk.t[n3][:, 0:512]),
                                 after=g_bc[j].war())
                    g_bc[j].wrote(bc)
                    hm = bk.evac(n, "vector",
                                 tt(t_hm[:, ec * TPC + cc_ * 512:ec * TPC + cc_ * 512 + 512],
                                    bk.t[n][:, 0:512], t_bc[j][:, :], ALU.mult),
                                 after=[bc])
                    g_bc[j].rd(hm)
                    hm_toks.append(hm)
            for oc in range(NDC):
                for cc_ in range(2):
                    n = bk.take("f")
                    mms = [(t_B[:, ec * D + oc * 128:ec * D + (oc + 1) * 128],
                            t_hm[:, ec * TPC + cc_ * 512:ec * TPC + cc_ * 512 + 512])
                           for ec in range(4)]
                    bk.matmuls(n, bk.t[n][:, 0:512], mms, extra_after=[tB] + hm_toks)
                    sl = resid[:, oc * TPC + cc_ * 512: oc * TPC + cc_ * 512 + 512]
                    resid_toks[(oc, cc_)] = bk.evac(
                        n, "vector", tt(sl, bk.t[n][:, 0:512], sl, ALU.add))

        cAB.close()
        if not fused:
            for dc in range(NDC):
                p.out_dma("sync", m1_out[dc * 128:(dc + 1) * 128, :],
                          resid[:, dc * TPC:(dc + 1) * TPC],
                          after=[resid_toks[(dc, 0)], resid_toks[(dc, 1)]])





def build_attn(l0: bool):
    p = Prog()
    nc = p.nc
    io = {}
    io["qin"] = nc.dram_tensor("qin", [512, S], F32, kind="ExternalInput").ap()
    io["kin"] = nc.dram_tensor("kin", [512, S], F32, kind="ExternalInput").ap()
    io["vin"] = nc.dram_tensor("vin", [S, 512], F32, kind="ExternalInput").ap()
    io["maskin"] = nc.dram_tensor("maskin", [512, 512], BF16, kind="ExternalInput").ap()
    io["idf"] = nc.dram_tensor("idf", [128, 128], F32, kind="ExternalInput").ap()
    if l0:
        io["o_out"] = nc.dram_tensor("o_out", [S, 512], F32, kind="ExternalOutput").ap()
    else:
        io["o_out"] = nc.dram_tensor("o_out", [512, S], F32, kind="ExternalOutput").ap()
    with ExitStack() as ctx:
        bk = Banks(p, ctx, nf32=6, nbf=2)
        emit_attn(p, ctx, bk, io, l0, loader=None, o_dt=F32)
        p.render(ctx)
    return p.nc


def emit_attn(p, ctx, bk, io, l0, loader, o_dt):
    nc = p.nc
    maskin, idf = io["maskin"], io["idf"]
    o_out = io["o_out"]
    if True:
        sb = lambda n, s, d: ctx.enter_context(nc.sbuf_tensor(n, s, d, allow_name_mangling=True))
        all_f = list(bk.pools["f"])
        bk.pools["f"] = all_f[:4]
        bk.i["f"] = 0
        sb_q = sb("sb_q", [128, 4 * S], BF16)
        sb_k = sb("sb_k", [128, 4 * S], BF16)
        sb_v = sb("sb_v", [128, 32 * 512], BF16)
        sb_m = sb("sb_m", [128, 4 * 512], BF16)
        sb_sq = sb("sb_sq", [128, 512], BF16)
        sb_P = sb("sb_P", [128, 4 * 512], BF16)
        sb_li = sb("sb_li", [1, 512], F32)
        sb_negc = sb("sb_negc", [128, 4], F32)
        sb_u = [sb(f"sb_u{i}", [1, 1], F32) for i in range(3)]
        sb_of = sb("sb_of", [128, 512], F32)
        sb_otm = sb("sb_otm", [128, 4 * 128], F32)
        sb_liT = sb("sb_liT", [128, 4], F32)
        sb_ob = sb("sb_ob", [128, 512], F32)
        sb_obf = sb("sb_obf", [128, 512], o_dt)
        c_idf = sb("c_idf", [128, 128], F32)
        ones_cb = sb("ones_cb", [128, 1], BF16)
        ones_rf = sb("ones_rf", [1, 128], F32)

        if loader is None:
            p.dma("gpsimd", sb_q[:, :].rearrange("p (ph t) -> p ph t", ph=4),
                  io["qin"].rearrange("(ph p) t -> p ph t", p=128), slot="init")
            p.dma("gpsimd", sb_k[:, :].rearrange("p (ph t) -> p ph t", ph=4),
                  io["kin"].rearrange("(ph p) t -> p ph t", p=128), slot="init")
            p.dma("gpsimd", sb_v[:, :].rearrange("p (tc d) -> p tc d", tc=32),
                  io["vin"].rearrange("(tc p) d -> p tc d", p=128), slot="init")
            p.dma("gpsimd", sb_m[:, :].rearrange("p (ms q) -> p ms q", ms=4),
                  maskin.rearrange("(ms p) q -> p ms q", p=128), slot="init")
            t_init = p.dma("gpsimd", c_idf[:, :], idf, slot="init")
            tq = tk = tv = tm_ = t_init
            qk_toks = [t_init] * 4
            v_toks = [t_init] * 4
        else:
            t_qkv = loader(p, sb_q, sb_k, sb_v, sb_m, c_idf)
            if isinstance(t_qkv, tuple):
                qk_toks, v_toks = t_qkv
                t_qkv = v_toks[-1]
            else:
                qk_toks = [t_qkv] * 4
                v_toks = [t_qkv] * 4
            t_ldm = p.slot_last.get("ds_ldm")
            tq = tk = tv = t_qkv
            tm_ = t_init = t_ldm
        p.emit("gpsimd", mset(ones_cb[:, :], 1.0))
        t_ones = p.emit("gpsimd", mset(ones_rf[:, :], 1.0))

        g_sq, g_n, g_u = Guard(), Guard(), Guard()
        g_P = [Guard() for _ in range(4)]
        g_li, g_of, g_otm, g_liT, g_ob, g_obf = (Guard(), Guard(), Guard(),
                                                 Guard(), Guard(), Guard())

        for pr in range(4):
            last_acc = None
            for which, buf in ((0, sb_q), (1, sb_k)):
                for cc in range(8):
                    sq = p.emit("scalar", act(sb_sq[:, :],
                                              buf[:, pr * S + cc * 512: pr * S + (cc + 1) * 512],
                                              AF.Square),
                                after=[t_init, qk_toks[pr]] + g_sq.war())
                    g_sq.wrote(sq)
                    n = bk.take("f")
                    mm = [(ones_cb[:, :], sb_sq[:, :])]
                    mt = bk.matmuls(n, bk.t[n][0:1, 0:512], mm,
                                    extra_after=[sq, t_ones])
                    g_sq.rd(mt)
                    if cc == 0:
                        ev = bk.evac(n, "vector",
                                     red(sb_u[which][0:1, :], bk.t[n][0:1, 0:512], ALU.max),
                                     after=g_u.war() if which == 0 else [])
                    else:
                        ev = bk.evac(n, "vector",
                                     red(sb_u[2][0:1, :], bk.t[n][0:1, 0:512], ALU.max))
                        ev = p.emit("vector", tt(sb_u[which][0:1, :], sb_u[which][0:1, :],
                                                 sb_u[2][0:1, :], ALU.max))
                    last_acc = ev
            r3 = p.emit("vector", tt(sb_u[2][0:1, :], sb_u[0][0:1, :], sb_u[1][0:1, :],
                                     ALU.mult), after=[last_acc])
            r4 = p.emit("scalar", act(sb_u[2][0:1, :], sb_u[2][0:1, :], AF.Sqrt),
                        after=[r3])
            g_u.wrote(r4)
            n = bk.take("f")
            bk.matmuls(n, bk.t[n][:, 0:1], [(ones_rf[0:1, :], sb_u[2][0:1, :])],
                       extra_after=[r4, t_ones])
            ng = bk.evac(n, "vector", ts(sb_negc[:, pr:pr + 1], bk.t[n][:, 0:1],
                                         -1.0, ALU.mult))
            g_u.rd(ng)

            bpool = bk.pools["b"]
            for qs in range(8):
                if qs % 2 == 0:
                    nO = all_f[4]
                    nL = all_f[5]
                else:
                    nO = bpool[0]
                    nL = bpool[1]
                units = [(tc, ts_) for tc in range(qs + 1) for ts_ in range(4)]
                nsub = len(units)
                AH = 4
                pend = {}
                for i in range(nsub + AH):
                    j = i - AH
                    if j >= 0:
                        tcc, ts_, ptok = pend.pop(j)
                        first = (j == 0)
                        last = (j == nsub - 1)
                        lt = p.emit("tensor",
                                    (lambda e, o=(bk.t[nL][0:1, 0:1024].bitcast(F32) if nL.startswith('pbb') else bk.t[nL][0:1, 0:512]),
                                     l=ones_cb[:, :], r=sb_P[:, ts_ * 512:(ts_ + 1) * 512],
                                     st=first, sp=last:
                                     e.matmul(o, l, r, start=st, stop=sp)),
                                    after=[ptok, t_ones, bk.rd[nL] if first else None],
                                    inc=0 if not last else 1)
                        if lt is not None:
                            bk.wr[nL] = lt
                        pv = p.emit("tensor",
                                    (lambda e, o=(bk.t[nO][:, 0:1024].bitcast(F32) if nO.startswith('pbb') else bk.t[nO][:, 0:512]),
                                     l=sb_v[:, tcc * 512 + pr * 128: tcc * 512 + (pr + 1) * 128],
                                     r=sb_P[:, ts_ * 512:(ts_ + 1) * 512],
                                     st=first, sp=last:
                                     e.matmul(o, l, r, start=st, stop=sp)),
                                    after=[ptok, v_toks[tcc // 32], bk.rd[nO] if first else None])
                        g_P[ts_].rd(pv)
                        bk.wr[nO] = pv
                    if i < nsub:
                        tc, ts_ = units[i]
                        n = bk.take("f")
                        mm = [(sb_k[:, pr * S + tc * 512 + ts_ * 128:
                                    pr * S + tc * 512 + (ts_ + 1) * 128],
                               sb_q[:, pr * S + qs * 512: pr * S + (qs + 1) * 512])]
                        mt = bk.matmuls(n, bk.t[n][:, 0:512], mm,
                                        extra_after=[qk_toks[pr]])
                        ex = bk.evac(n, "scalar",
                                     act(sb_P[:, ts_ * 512:(ts_ + 1) * 512],
                                         bk.t[n][:, 0:512], AF.Exp,
                                         bias=sb_negc[:, pr:pr + 1]),
                                     after=[ng] + g_P[ts_].war())
                        g_P[ts_].wrote(ex)
                        ptok = ex
                        if tc == qs:
                            mk = p.emit("vector",
                                        tt(sb_P[:, ts_ * 512:(ts_ + 1) * 512],
                                           sb_P[:, ts_ * 512:(ts_ + 1) * 512],
                                           sb_m[:, ts_ * 512:(ts_ + 1) * 512],
                                           ALU.mult), after=[ex, tm_])
                            g_P[ts_].wrote(mk)
                            ptok = mk
                        pend[i] = (tc * 4 + ts_, ts_, ptok)

                li = p.emit("vector", recip(sb_li[0:1, :], (bk.t[nL][0:1, 0:1024].bitcast(F32) if nL.startswith('pbb') else bk.t[nL][0:1, 0:512])),
                            after=[bk.wr[nL]] + g_li.war())
                bk.rd[nL] = li
                g_li.wrote(li)
                if l0:
                    ev = p.emit("vector", cp(sb_of[:, :], (bk.t[nO][:, 0:1024].bitcast(F32) if nO.startswith('pbb') else bk.t[nO][:, 0:512])),
                                after=[bk.wr[nO]] + g_of.war())
                    bk.rd[nO] = ev
                    g_of.wrote(ev)
                    for qsub in range(4):
                        n = bk.take("f")
                        w = bk.transpose(n, bk.t[n][:, 0:1].bitcast(F32),
                                         sb_li[0:1, qsub * 128:(qsub + 1) * 128],
                                         c_idf[0:1, 0:1], extra_after=[li, t_init])
                        g_li.rd(w)
                        lt_ = bk.evac(n, "vector", cp(sb_liT[:, qsub:qsub + 1],
                                                      bk.t[n][:, 0:1]),
                                      after=g_liT.war() if qsub == 0 else ())
                        if qsub == 0:
                            g_liT.wrote(lt_)
                    g_liT.wrote(lt_)
                    ot_toks = []
                    for qsub in range(4):
                        n = bk.take("f")
                        w = bk.transpose(n, bk.t[n][:, 0:128],
                                         sb_of[:, qsub * 128:(qsub + 1) * 128],
                                         c_idf[:, :], extra_after=[ev, t_init])
                        g_of.rd(w)
                        ot = bk.evac(n, "vector",
                                     ts(sb_otm[:, qsub * 128:(qsub + 1) * 128],
                                        bk.t[n][:, 0:128], sb_liT[:, qsub:qsub + 1],
                                        ALU.mult),
                                     after=[lt_] + (g_otm.war() if qsub == 0 else []))
                        g_liT.rd(ot)
                        if qsub == 0:
                            g_otm.wrote(ot)
                        ot_toks.append(ot)
                    g_otm.wrote(ot_toks[-1])
                    od = p.out_dma("sync",
                                   o_out[qs * 512:(qs + 1) * 512, pr * 128:(pr + 1) * 128]
                                   .rearrange("(qsub r) d -> r qsub d", r=128),
                                   sb_otm[:, :].rearrange("r (qsub d) -> r qsub d", qsub=4),
                                   after=ot_toks, slot="od")
                    g_otm.rd(od)
                else:
                    n = bk.take("f")
                    bk.matmuls(n, bk.t[n][:, 0:512], [(ones_rf[0:1, :], sb_li[0:1, :])],
                               extra_after=[li, t_ones])
                    g_li.rd(bk.wr[n])
                    bo = bk.evac(n, "vector", cp(sb_ob[:, :], bk.t[n][:, 0:512]),
                                 after=g_ob.war())
                    g_ob.wrote(bo)
                    ev = p.emit("vector", tt(sb_obf[:, :], (bk.t[nO][:, 0:1024].bitcast(F32) if nO.startswith('pbb') else bk.t[nO][:, 0:512]),
                                             sb_ob[:, :], ALU.mult),
                                after=[bk.wr[nO], bo] + g_obf.war())
                    bk.rd[nO] = ev
                    g_ob.rd(ev)
                    g_obf.wrote(ev)
                    od = p.out_dma("sync",
                                   o_out[pr * 128:(pr + 1) * 128, qs * 512:(qs + 1) * 512],
                                   sb_obf[:, :], after=[ev], slot="od")
                    g_obf.rd(od)
        bk.pools["f"] = all_f
        bk.i["f"] = 0



HID = 5632
NHC = HID // 128
ZP = 128.0


def mid_dram(nc, inter=False):
    io = {}
    if not inter:
        io["o_as"] = nc.dram_tensor("o_as", [TPC, D], F32, kind="ExternalInput").ap()
        io["m1_in"] = nc.dram_tensor("m1_in", [D, TPC], F32, kind="ExternalInput").ap()
    io["wot"] = nc.dram_tensor("wot", [D, D], F32, kind="ExternalInput").ap()
    io["wqkv"] = nc.dram_tensor("wqkv", [D, 3 * D], F32, kind="ExternalInput").ap()
    io["wqg_t"] = nc.dram_tensor("wqg_t", [D, HID], I32, kind="ExternalInput").ap()
    io["wqu_t"] = nc.dram_tensor("wqu_t", [D, HID], I32, kind="ExternalInput").ap()
    io["wqd_t"] = nc.dram_tensor("wqd_t", [HID, D], I32, kind="ExternalInput").ap()
    io["sg"] = nc.dram_tensor("sg", [128, NHC], F32, kind="ExternalInput").ap()
    io["su"] = nc.dram_tensor("su", [128, NHC], F32, kind="ExternalInput").ap()
    io["sd"] = nc.dram_tensor("sd", [128, NDC], F32, kind="ExternalInput").ap()
    if inter:
        io["qk1_out"] = nc.dram_tensor("qk1_out", [4096, TPC], BF16).ap()
        io["v1_out"] = nc.dram_tensor("v1_out", [4096, 512], BF16).ap()
    else:
        io["idf"] = nc.dram_tensor("idf", [128, 128], F32, kind="ExternalInput").ap()
        io["idb"] = nc.dram_tensor("idb", [128, 128], BF16, kind="ExternalInput").ap()
        io["qk1_out"] = nc.dram_tensor("qk1_out", [4096, TPC], F32, kind="ExternalOutput").ap()
        io["v1_out"] = nc.dram_tensor("v1_out", [4096, 512], F32, kind="ExternalOutput").ap()
        io["m2_out"] = nc.dram_tensor("m2_out", [D, TPC], F32, kind="ExternalOutput").ap()
    return io


def build_mid(phases="ABCDE"):
    p = Prog()
    io = mid_dram(p.nc)
    with ExitStack() as ctx:
        resid = ctx.enter_context(p.nc.sbuf_tensor("resid", [128, NDC * TPC], F32))
        bk = Banks(p, ctx, nf32=6, nbf=2)
        emit_mid(p, ctx, bk, io, resid, fused=False, o_loader=None, ev_dt=F32)
        p.render(ctx)
    return p.nc


def emit_mid(p, ctx, bk, io, resid, fused, o_loader, ev_dt, qkv_done_cb=None):
    nc = p.nc
    wot, wqkv = io["wot"], io["wqkv"]
    wqg_t, wqu_t, wqd_t = io["wqg_t"], io["wqu_t"], io["wqd_t"]
    sg_in, su_in, sd_in = io["sg"], io["su"], io["sd"]
    idf, idb = io["idf"], io["idb"]
    qk1_out, v1_out = io["qk1_out"], io["v1_out"]
    m2_out = io.get("m2_out")
    if True:
        sb = lambda n, s, d: ctx.enter_context(nc.sbuf_tensor(n, s, d, allow_name_mangling=True))
        xn1_fm = sb("xn1_fm", [128, NDC * TPC], BF16)
        invso_c = sb("invso_c", [128, 8], F32)
        invso_b = sb("invso_b", [128, TPC], F32)
        invso_r = sb("invso_r", [1, TPC], F32)
        c_idf = sb("c_idf", [128, 128], F32)
        c_idb = sb("c_idb", [128, 128], BF16)
        ones_cf = sb("ones_cf", [128, 1], F32)
        ones_cb = sb("ones_cb", [128, 1], BF16)
        ones_rf = sb("ones_rf", [1, 128], F32)
        m128_r = sb("m128_r", [1, 128], BF16)
        sg = sb("sgv", [128, NHC], F32)
        su = sb("suv", [128, NHC], F32)
        sd = sb("sdv", [128, NDC], F32)

        p.dma("sync", c_idf[:, :], idf, slot="init")
        p.dma("sync", c_idb[:, :], idb, slot="init")
        p.dma("sync", sg[:, :], sg_in, slot="init")
        p.dma("sync", su[:, :], su_in, slot="init")
        t_init = p.dma("sync", sd[:, :], sd_in, slot="init")
        t_idf = t_idb = t_scl = t_init
        p.emit("gpsimd", mset(ones_cf[:, :], 1.0))
        p.emit("gpsimd", mset(ones_cb[:, :], 1.0))
        p.emit("gpsimd", mset(ones_rf[:, :], 1.0))
        t_ones = p.emit("gpsimd", mset(m128_r[:, :], -128.0))

        resid_guard = Guard()
        if not fused:
            t_m1 = p.dma("gpsimd", resid[:, :].rearrange("p (dc t) -> p dc t", dc=NDC),
                         io["m1_in"].rearrange("(dc p) t -> p dc t", p=128), slot="m1")
            resid_guard.wrote(t_m1)
        t_m1 = resid_guard.writer

        t_thr = ctx.enter_context(nc.sbuf_tensor("t_thr", [128, 1], F32, allow_name_mangling=True))
        t_nthr = ctx.enter_context(nc.sbuf_tensor("t_nthr", [128, 1], F32, allow_name_mangling=True))
        t_sw = ctx.enter_context(nc.sbuf_tensor("t_sw", [128, 1], F32, allow_name_mangling=True))
        NBLK, BW = 8, 256
        with ExitStack() as cW:
            sbw = lambda n, s, d: cW.enter_context(nc.sbuf_tensor(n, s, d, allow_name_mangling=True))
            t_w = [sbw(f"t_w{i}", [128, NDC * BW], F32) for i in range(2)]
            t_ab = sbw("t_ab", [128, NDC * BW], BF16)
            t_acc = sbw("t_acc", [128, NBLK], F32)
            t_g = [sbw(f"t_g{i}", [1, 1], F32) for i in range(3)]
            g_w = [Guard(), Guard()]
            acc_toks = []
            for blk in range(NBLK):
                wb = t_w[blk % 2]
                gw = g_w[blk % 2]
                tok = p.dma("sync", wb[:, :].rearrange("p (ec o) -> p ec o", ec=NDC),
                            wot[:, blk * BW:(blk + 1) * BW]
                            .rearrange("(ec p) o -> p ec o", p=128),
                            after=gw.war(), slot=f"wm{blk % 2}")
                gw.wrote(tok)
                a = p.emit("scalar", act(t_ab[:, :], wb[:, :], AF.Abs,
                                         accum=t_acc[:, blk:blk + 1]), after=[tok])
                gw.rd(a)
                acc_toks.append(a)
            s1 = p.emit("vector", red(t_thr[:, :], t_acc[:, :], ALU.add), after=acc_toks)
            n = bk.take("f")
            bk.matmuls(n, bk.t[n][0:1, 0:1], [(t_thr[:, :], ones_cf[:, :])],
                       extra_after=[s1, t_ones])
            bk.evac(n, "vector", ts(t_g[0][0:1, :], bk.t[n][0:1, 0:1],
                                    1.0 / (D * D), ALU.mult))
            p.emit("vector", ts(t_g[1][0:1, :], t_g[0][0:1, :], 1e-5, ALU.max))
            g3 = p.emit("vector", ts(t_g[2][0:1, :], t_g[1][0:1, :], 0.5, ALU.mult))
            n = bk.take("f")
            bk.matmuls(n, bk.t[n][:, 0:1], [(ones_rf[0:1, :], t_g[1][0:1, :])],
                       extra_after=[g3, t_ones])
            bk.evac(n, "vector", cp(t_sw[:, :], bk.t[n][:, 0:1]))
            n = bk.take("f")
            bk.matmuls(n, bk.t[n][:, 0:1], [(ones_rf[0:1, :], t_g[2][0:1, :])],
                       extra_after=[g3, t_ones])
            bk.evac(n, "vector", cp(t_thr[:, :], bk.t[n][:, 0:1]))
            t_nth = p.emit("vector", ts(t_nthr[:, :], t_thr[:, :], -1.0, ALU.mult))

        oq_fm = None
        cA = ExitStack()
        oq_fm = cA.enter_context(nc.sbuf_tensor("oq_fm", [128, NDC * TPC], BF16))
        with ExitStack() as c2:
            sb2 = lambda n, s, d: c2.enter_context(nc.sbuf_tensor(n, s, d, allow_name_mangling=True))
            t_x = [sb2(f"t_x{i}", [128, D], F32) for i in range(2)]
            t_v = sb2("t_v", [128, D], F32)
            t_i = sb2("t_i", [128, D], I32)
            t_kf = sb2("t_kf", [128, D], BF16)
            t_ca = sb2("t_ca", [128, D], BF16)
            t_cb = sb2("t_cb", [128, D], BF16)
            t_xq = sb2("t_xq", [128, D], BF16)
            t_s = [sb2(f"t_s{i}", [128, 1], F32) for i in range(6)]
            g_x = [Guard(), Guard()]
            g_xq = Guard()
            ivr_toks = []
            for tt_ in range(8):
                xb = t_x[tt_ % 2]
                gx = g_x[tt_ % 2]
                if o_loader is None:
                    tok_x = p.dma("sync", xb[:, :], io["o_as"][tt_ * 128:(tt_ + 1) * 128, :],
                                  after=gx.war(), slot=f"x{tt_ % 2}")
                else:
                    tok_x = o_loader(p, xb, tt_, gx.war())
                gx.wrote(tok_x)
                a6 = tok_x
                r0 = p.emit("vector", red(t_s[3][:, :], xb[:, :], ALU.max, absval=True),
                            after=[tok_x])
                gx.rd(r0)
                p.emit("vector", ts(t_s[3][:, :], t_s[3][:, :], 1e-5, ALU.max))
                p.emit("vector", recip(t_s[4][:, :], t_s[3][:, :]))
                p.emit("vector", ts(t_s[5][:, :], t_s[4][:, :], 127.0, ALU.mult))
                ivc = p.emit("vector", ts(invso_c[:, tt_:tt_ + 1], t_s[3][:, :],
                                          1.0 / 127.0, ALU.mult))
                vv = p.emit("vector", ts(t_v[:, :], xb[:, :], t_s[5][:, :], ALU.mult))
                gx.rd(vv)
                p.emit("vector", cp(t_i[:, :], t_v[:, :]))
                p.emit("vector", cp(t_kf[:, :], t_i[:, :]))
                p.emit("vector", tt(t_v[:, :], t_v[:, :], t_kf[:, :], ALU.subtract))
                p.emit("vector", stt(t_ca[:, :], t_v[:, :], 0.5, t_kf[:, :],
                                     ALU.is_gt, ALU.add))
                p.emit("vector", ts(t_cb[:, :], t_v[:, :], -0.5, ALU.is_lt))
                p.emit("vector", tt(t_xq[:, :], t_ca[:, :], t_cb[:, :], ALU.subtract),
                       after=g_xq.war())
                p.emit("vector", ts(t_i[:, :], t_i[:, :], 1, ALU.bitwise_and))
                p.emit("vector", stt(t_ca[:, :], t_v[:, :], 0.5, t_i[:, :],
                                     ALU.is_equal, ALU.mult))
                p.emit("vector", stt(t_cb[:, :], t_v[:, :], -0.5, t_i[:, :],
                                     ALU.is_equal, ALU.mult))
                p.emit("vector", tt(t_xq[:, :], t_xq[:, :], t_ca[:, :], ALU.add))
                aq = p.emit("vector", tt(t_xq[:, :], t_xq[:, :], t_cb[:, :], ALU.subtract))
                g_xq.wrote(aq)
                for dc in range(NDC):
                    n = bk.take("b")
                    w = bk.transpose(n, bk.t[n][:, 0:128],
                                     t_xq[:, dc * 128:(dc + 1) * 128], c_idb[:, :],
                                     extra_after=[aq, t_idb])
                    g_xq.rd(w)
                    e = "scalar" if dc % 2 else "vector"
                    dst = oq_fm[:, dc * TPC + tt_ * 128: dc * TPC + (tt_ + 1) * 128]
                    fn = scp(dst, bk.t[n][:, 0:128]) if e == "scalar" else cp(dst, bk.t[n][:, 0:128])
                    bk.evac(n, e, fn)
                n = bk.take("f")
                w = bk.transpose(n, bk.t[n][0:1, 0:128],
                                 invso_c[:, tt_:tt_ + 1], c_idf[:, :],
                                 extra_after=[ivc, t_idf])
                ivr_toks.append(bk.evac(n, "vector",
                                        cp(invso_r[0:1, tt_ * 128:(tt_ + 1) * 128],
                                           bk.t[n][0:1, 0:128])))
            for cc in range(2):
                n = bk.take("f")
                mm = [(ones_rf[0:1, :], invso_r[0:1, cc * 512:(cc + 1) * 512])]
                bk.matmuls(n, bk.t[n][:, 0:512], mm, extra_after=ivr_toks + [t_ones])
                bk.evac(n, "vector", cp(invso_b[:, cc * 512:(cc + 1) * 512],
                                        bk.t[n][:, 0:512]))

        p.barrier()
        with ExitStack() as c2:
            sb2 = lambda n, s, d: c2.enter_context(nc.sbuf_tensor(n, s, d, allow_name_mangling=True))
            t_w = [sb2(f"t_w{i}", [128, NDC * BW], F32) for i in range(2)]
            t_b1 = sb2("t_b1", [128, NDC * BW], BF16)
            t_tern2 = [sb2(f"t_tern{i}", [128, NDC * BW], BF16) for i in range(2)]
            t_y = [sb2(f"t_y{i}", [128, 512], F32) for i in range(2)]
            g_w = [Guard(), Guard()]
            g_tern2 = [Guard(), Guard()]
            g_b1 = Guard()
            g_y = [Guard(), Guard()]
            isv = p.emit("vector", ts(invso_b[:, :], invso_b[:, :], t_sw[:, :], ALU.mult),
                         after=[t_nth])

            for blk in range(NBLK):
                wb = t_w[blk % 2]
                gw = g_w[blk % 2]
                tok = p.dma("sync", wb[:, :].rearrange("p (ec o) -> p ec o", ec=NDC),
                            wot[:, blk * BW:(blk + 1) * BW]
                            .rearrange("(ec p) o -> p ec o", p=128),
                            after=gw.war(), slot=f"w{blk % 2}")
                gw.wrote(tok)
                t_tern = t_tern2[blk % 2]
                g_tern = g_tern2[blk % 2]
                b1 = p.emit("vector", ts(t_b1[:, :], wb[:, :], t_nthr[:, :], ALU.is_lt),
                            after=[tok] + g_b1.war())
                g_b1.wrote(b1)
                gw.rd(b1)
                tern = p.emit("vector", stt(t_tern[:, :], wb[:, :], t_thr[:, :],
                                            t_b1[:, :], ALU.is_gt, ALU.subtract),
                              after=g_tern.war())
                gw.rd(tern)
                g_b1.rd(tern)
                g_tern.wrote(tern)
                for ob in range(2):
                    for cc_ in range(2):
                        n = bk.take("f")
                        mms = [(t_tern[:, dc * BW + ob * 128:dc * BW + (ob + 1) * 128],
                                oq_fm[:, dc * TPC + cc_ * 512:dc * TPC + cc_ * 512 + 512])
                               for dc in range(NDC)]
                        m = bk.matmuls(n, bk.t[n][:, 0:512], mms, extra_after=[tern])
                        g_tern.rd(m)
                        k = (ob * 2 + cc_) % 2
                        yt = bk.evac(n, "vector",
                                     tt(t_y[k][:, :], bk.t[n][:, 0:512],
                                        invso_b[:, cc_ * 512:cc_ * 512 + 512], ALU.mult),
                                     after=[isv] + g_y[k].war())
                        g_y[k].wrote(yt)
                        oc = blk * 2 + ob
                        sl = resid[:, oc * TPC + cc_ * 512: oc * TPC + cc_ * 512 + 512]
                        ra = p.emit("vector", tt(sl, sl, t_y[k][:, :], ALU.add),
                                    after=[yt, t_m1])
                        g_y[k].rd(ra)
                        resid_guard.rd(ra)
        cA.close()

        p.barrier()
        with ExitStack() as c2:
            sb2 = lambda n, s, d: c2.enter_context(nc.sbuf_tensor(n, s, d, allow_name_mangling=True))
            rb = sb2("rb", [128, NDC * TPC], BF16)
            rbsq = sb2("rbsq", [128, NDC * TPC], BF16)
            row = [sb2(f"row{i}", [1, TPC], F32) for i in range(3)]
            mb = sb2("mb", [128, TPC], F32)
            rsb = sb2("rsb", [128, TPC], F32)
            t_t1 = sb2("t_t1", [128, TPC], F32)
            g_t1 = Guard()

            cb0 = p.emit("scalar", scp(rb[:, :], resid[:, :]))
            sq0 = p.emit("scalar", act(rbsq[:, :], rb[:, :], AF.Square), after=[cb0])
            r0_toks, r1_toks = [], []
            for cc in range(2):
                n = bk.take("f")
                mms = [(ones_cb[:, :], rb[:, dc * TPC + cc * 512:dc * TPC + cc * 512 + 512])
                       for dc in range(NDC)]
                bk.matmuls(n, bk.t[n][0:1, 0:512], mms, extra_after=[cb0, t_ones])
                r0_toks.append(bk.evac(n, "vector", ts(row[0][0:1, cc * 512:(cc + 1) * 512],
                                                       bk.t[n][0:1, 0:512], 1.0 / D, ALU.mult)))
                n = bk.take("f")
                mms = [(ones_cb[:, :], rbsq[:, dc * TPC + cc * 512:dc * TPC + cc * 512 + 512])
                       for dc in range(NDC)]
                bk.matmuls(n, bk.t[n][0:1, 0:512], mms, extra_after=[sq0, t_ones])
                r1_toks.append(bk.evac(n, "vector", ts(row[1][0:1, cc * 512:(cc + 1) * 512],
                                                       bk.t[n][0:1, 0:512], 1.0 / D, ALU.mult)))
            sqm = p.emit("scalar", act(row[2][0:1, :], row[0][0:1, :], AF.Square),
                         after=r0_toks)
            vr = p.emit("vector", tt(row[1][0:1, :], row[1][0:1, :], row[2][0:1, :],
                                     ALU.subtract), after=[sqm] + r1_toks)
            p.emit("vector", ts(row[1][0:1, :], row[1][0:1, :], EPS, ALU.add))
            rc = p.emit("vector", recip(row[1][0:1, :], row[1][0:1, :]))
            rs = p.emit("scalar", act(row[1][0:1, :], row[1][0:1, :], AF.Sqrt), after=[rc])
            for cc in range(2):
                n = bk.take("f")
                bk.matmuls(n, bk.t[n][:, 0:512],
                           [(ones_rf[0:1, :], row[0][0:1, cc * 512:(cc + 1) * 512])],
                           extra_after=[rs, t_ones] + r0_toks)
                bk.evac(n, "vector", cp(mb[:, cc * 512:(cc + 1) * 512], bk.t[n][:, 0:512]))
                n = bk.take("f")
                bk.matmuls(n, bk.t[n][:, 0:512],
                           [(ones_rf[0:1, :], row[1][0:1, cc * 512:(cc + 1) * 512])],
                           extra_after=[rs, t_ones])
                mbt = bk.evac(n, "vector", cp(rsb[:, cc * 512:(cc + 1) * 512],
                                              bk.t[n][:, 0:512]))
            for dc in range(NDC):
                d1 = p.emit("vector", tt(t_t1[:, :], resid[:, dc * TPC:(dc + 1) * TPC],
                                         mb[:, :], ALU.subtract),
                            after=[mbt] + g_t1.war())
                g_t1.wrote(d1)
                d2 = p.emit("vector", tt(xn1_fm[:, dc * TPC:(dc + 1) * TPC],
                                         t_t1[:, :], rsb[:, :], ALU.mult))
                g_t1.rd(d2)

        p.barrier()
        with ExitStack() as c2:
            sb2 = lambda n, s, d: c2.enter_context(nc.sbuf_tensor(n, s, d, allow_name_mangling=True))
            t_wb = [sb2(f"t_wb{i}", [128, NDC * 512], BF16) for i in range(2)]
            t_wf = sb2("t_wf", [128, NDC * 512], F32)
            t_ev = [sb2(f"t_ev{i}", [128, 512], ev_dt) for i in range(4)]
            g_wb = [Guard(), Guard()]
            g_wf = Guard()
            g_ev = [Guard() for _ in range(4)]
            for blk in range(12):
                wb = t_wb[blk % 2]
                gw = g_wb[blk % 2]
                tokf = p.dma("sync", t_wf[:, :].rearrange("p (dc o) -> p dc o", dc=NDC),
                             wqkv[:, blk * 512:(blk + 1) * 512]
                             .rearrange("(dc p) o -> p dc o", p=128),
                             after=g_wf.war(), slot="wf")
                g_wf.wrote(tokf)
                tok = p.emit("vector", cp(wb[:, :], t_wf[:, :]),
                             after=[tokf] + gw.war())
                g_wf.rd(tok)
                gw.wrote(tok)
                if blk < 8:
                    for ob in range(4):
                        for cc_ in range(2):
                            n = bk.take("f")
                            mms = [(wb[:, dc * 512 + ob * 128:dc * 512 + (ob + 1) * 128],
                                    xn1_fm[:, dc * TPC + cc_ * 512:dc * TPC + cc_ * 512 + 512])
                                   for dc in range(NDC)]
                            m = bk.matmuls(n, bk.t[n][:, 0:512], mms, extra_after=[tok])
                            gw.rd(m)
                            k = (ob * 2 + cc_) % 4
                            ev = t_ev[k]
                            sc_ = ISQD if blk < 4 else 1.0
                            evt = bk.evac(n, "scalar",
                                          act(ev[:, :], bk.t[n][:, 0:512], AF.Copy,
                                              bias=0.0, scale=sc_),
                                          after=g_ev[k].war())
                            g_ev[k].wrote(evt)
                            if blk < 4:
                                h = blk * 4 + ob
                                rbase = (h // 4) * 1024 + (h % 4) * 128
                            else:
                                h = (blk - 4) * 4 + ob
                                rbase = (h // 4) * 1024 + 512 + (h % 4) * 128
                            dt_ = p.out_dma("sync",
                                            qk1_out[rbase:rbase + 128,
                                                    cc_ * 512:cc_ * 512 + 512],
                                            ev[:, :], after=[evt], slot=f"ev{k}")
                            g_ev[k].rd(dt_)
                else:
                    for tt_ in range(8):
                        n = bk.take("f")
                        mms = [(xn1_fm[:, dc * TPC + tt_ * 128:dc * TPC + (tt_ + 1) * 128],
                                wb[:, dc * 512:(dc + 1) * 512])
                               for dc in range(NDC)]
                        m = bk.matmuls(n, bk.t[n][:, 0:512], mms, extra_after=[tok])
                        gw.rd(m)
                        k = tt_ % 4
                        ev = t_ev[k]
                        evt = bk.evac(n, "scalar", scp(ev[:, :], bk.t[n][:, 0:512]),
                                      after=g_ev[k].war())
                        g_ev[k].wrote(evt)
                        rbase = (blk - 8) * 1024 + tt_ * 128
                        dt_ = p.out_dma("sync", v1_out[rbase:rbase + 128, :],
                                        ev[:, :], after=[evt], slot=f"ev{k}")
                        g_ev[k].rd(dt_)

        if qkv_done_cb is not None:
            qkv_done_cb(p)
        p.barrier()
        with ExitStack() as c2:
            sb2 = lambda n, s, d: c2.enter_context(nc.sbuf_tensor(n, s, d, allow_name_mangling=True))
            gu_sb = sb2("gu_sb", [128, NHC * 512], BF16)
            t_gs = sb2("t_gs", [128, 512], BF16)
            trow = sb2("trow", [1, 512], BF16)
            t2row = sb2("t2row", [1, 512], BF16)
            g_gs, g_trow, g_t2, g_gu = Guard(), Guard(), Guard(), Guard()
            down_drain = []

            for th in range(2):
                n = bk.take("f")
                mms = [(ones_cb[:, :], xn1_fm[:, dc * TPC + th * 512:dc * TPC + th * 512 + 512])
                       for dc in range(NDC)]
                bk.matmuls(n, bk.t[n][0:1, 0:512], mms, extra_after=[t_ones] + g_trow.war())
                ttr = bk.evac(n, "vector", cp(trow[0:1, :], bk.t[n][0:1, 0:512]))
                g_trow.wrote(ttr)
                gu_toks = []
                with ExitStack() as c3:
                    sb3 = lambda n_, s_, d_: c3.enter_context(
                        nc.sbuf_tensor(n_, s_, d_, allow_name_mangling=True))
                    stg = sb3("stg", [128, 4 * 2048], BF16)
                    stu = sb3("stu", [128, 4 * 2048], BF16)
                    t_wi = [sb3(f"t_wi{i}", [128, 2048], I32) for i in range(2)]
                    g_wi = [Guard(), Guard()]
                    g_q = [Guard() for _ in range(8)]
                    witer = 0
                    for hb in range(11):
                        cvt = {}
                        for wsel, (wt, stage) in enumerate(((wqg_t, stg), (wqu_t, stu))):
                            for q4 in range(4):
                                wi_ = t_wi[witer % 2]
                                gwi = g_wi[witer % 2]
                                witer += 1
                                tok = p.dma("sync",
                                            wi_[:, :].rearrange("p (dc o) -> p dc o", dc=NDC),
                                            wt[:, hb * 512 + q4 * 128: hb * 512 + (q4 + 1) * 128]
                                            .rearrange("(dc p) o -> p dc o", p=128),
                                            after=gwi.war() + (down_drain if hb == 0 else []),
                                            slot=f"wi{witer % 2}")
                                gwi.wrote(tok)
                                eng = "scalar" if wsel == 0 else "vector"
                                cv = p.emit(eng,
                                            (scp if eng == "scalar" else cp)(
                                                stage[:, q4 * 2048:(q4 + 1) * 2048], wi_[:, :]),
                                            after=[tok] + g_q[wsel * 4 + q4].war())
                                gwi.rd(cv)
                                g_q[wsel * 4 + q4].wrote(cv)
                                cvt[(wsel, q4)] = cv
                        for hc4 in range(4):
                            hc = hb * 4 + hc4
                            n = bk.take("f")
                            mms = [(stg[:, hc4 * 2048 + dc * 128: hc4 * 2048 + (dc + 1) * 128],
                                    xn1_fm[:, dc * TPC + th * 512:dc * TPC + th * 512 + 512])
                                   for dc in range(NDC)]
                            mms.append((m128_r[0:1, :], trow[0:1, :]))
                            mg = bk.matmuls(n, bk.t[n][:, 0:512], mms,
                                            extra_after=[cvt[(0, hc4)], ttr, t_ones])
                            g_q[hc4].rd(mg)
                            g_trow.rd(mg)
                            gs = bk.evac(n, "scalar",
                                         act(t_gs[:, :], bk.t[n][:, 0:512], AF.Silu,
                                             scale=sg[:, hc:hc + 1]),
                                         after=[t_scl] + g_gs.war())
                            g_gs.wrote(gs)
                            n2 = bk.take("f")
                            mms = [(stu[:, hc4 * 2048 + dc * 128: hc4 * 2048 + (dc + 1) * 128],
                                    xn1_fm[:, dc * TPC + th * 512:dc * TPC + th * 512 + 512])
                                   for dc in range(NDC)]
                            mms.append((m128_r[0:1, :], trow[0:1, :]))
                            mu = bk.matmuls(n2, bk.t[n2][:, 0:512], mms,
                                            extra_after=[cvt[(1, hc4)], ttr, t_ones])
                            g_q[4 + hc4].rd(mu)
                            g_trow.rd(mu)
                            gut = bk.evac(n2, "vector",
                                          stt(gu_sb[:, hc * 512:(hc + 1) * 512],
                                              bk.t[n2][:, 0:512], su[:, hc:hc + 1],
                                              t_gs[:, :], ALU.mult, ALU.mult),
                                          after=[gs, t_scl] + g_gu.war())
                            g_gs.rd(gut)
                            gu_toks.append(gut)
                g_gu.wrote(gu_toks[-1])
                stage_drain = []
                for gq in g_q:
                    stage_drain += gq.readers + ([gq.writer] if gq.writer else [])
                n = bk.take("f")
                mms = [(ones_cb[:, :], gu_sb[:, hc * 512:(hc + 1) * 512])
                       for hc in range(NHC)]
                bk.matmuls(n, bk.t[n][0:1, 0:512], mms,
                           extra_after=gu_toks + [t_ones] + g_t2.war())
                tt2 = bk.evac(n, "vector", cp(t2row[0:1, :], bk.t[n][0:1, 0:512]))
                g_t2.wrote(tt2)
                with ExitStack() as c3:
                    sb3 = lambda n_, s_, d_: c3.enter_context(
                        nc.sbuf_tensor(n_, s_, d_, allow_name_mangling=True))
                    std2 = [sb3(f"std{i}", [128, NHC * 128], BF16) for i in range(2)]
                    wdi = sb3("wdi", [128, NHC * 128], I32)
                    g_sd = [Guard(), Guard()]
                    g_di = Guard()
                    for oc in range(NDC):
                        std_ = std2[oc % 2]
                        td = p.dma("sync", wdi[:, :].rearrange("p (hc o) -> p hc o", hc=NHC),
                                   wqd_t[:, oc * 128:(oc + 1) * 128]
                                   .rearrange("(hc p) o -> p hc o", p=128),
                                   after=g_di.war() + (stage_drain if oc == 0 else []),
                                   slot="wd0")
                        g_di.wrote(td)
                        cv = p.emit("scalar" if oc % 2 else "vector",
                                    (scp if oc % 2 else cp)(std_[:, :], wdi[:, :]),
                                    after=[td] + g_sd[oc % 2].war())
                        g_di.rd(cv)
                        g_sd[oc % 2].wrote(cv)
                        n = bk.take("f")
                        mms = [(std_[:, hc * 128:(hc + 1) * 128],
                                gu_sb[:, hc * 512:(hc + 1) * 512]) for hc in range(NHC)]
                        mms.append((m128_r[0:1, :], t2row[0:1, :]))
                        md = bk.matmuls(n, bk.t[n][:, 0:512], mms,
                                        extra_after=[cv, tt2, t_ones] + gu_toks)
                        g_sd[oc % 2].rd(md)
                        g_t2.rd(md)
                        g_gu.rd(md)
                        sl = resid[:, oc * TPC + th * 512: oc * TPC + th * 512 + 512]
                        ra = bk.evac(n, "vector",
                                     stt(sl, bk.t[n][:, 0:512], sd[:, oc:oc + 1],
                                         sl, ALU.mult, ALU.add), after=[t_scl])
                        resid_guard.rd(ra)
                        down_drain.append(md)

        if not fused:
            for dc in range(NDC):
                p.out_dma("sync", m2_out[dc * 128:(dc + 1) * 128, :],
                          resid[:, dc * TPC:(dc + 1) * TPC],
                          after=resid_guard.readers[-40:])



def build_fin():
    p = Prog()
    nc = p.nc
    io = {}
    io["o1"] = nc.dram_tensor("o1", [D, TPC], F32, kind="ExternalInput").ap()
    io["m2_in"] = nc.dram_tensor("m2_in", [D, TPC], F32, kind="ExternalInput").ap()
    io["w_out"] = nc.dram_tensor("w_out", [D, D], F32, kind="ExternalInput").ap()
    io["y_out"] = nc.dram_tensor("y_out", [D, TPC], F32, kind="ExternalOutput").ap()
    with ExitStack() as ctx:
        resid = ctx.enter_context(p.nc.sbuf_tensor("resid", [128, NDC * TPC], F32))
        bk = Banks(p, ctx, nf32=6, nbf=0)
        emit_fin(p, ctx, bk, io, resid, fused=False, o1_loader=None)
        p.render(ctx)
    return p.nc


def emit_fin(p, ctx, bk, io, resid, fused, o1_loader):
    nc = p.nc
    w_out = io["w_out"]
    y_out = io["y_out"]
    if True:
        sb = lambda n, s, d: ctx.enter_context(nc.sbuf_tensor(n, s, d, allow_name_mangling=True))
        o1_fm = sb("o1_fm", [128, NDC * TPC], BF16)
        t_wb = [sb(f"t_wb{i}", [128, NDC * 512], BF16) for i in range(2)]
        g_wb = [Guard(), Guard()]
        resid_guard = Guard()

        if not fused:
            t_m2 = p.dma("gpsimd", resid[:, :].rearrange("p (dc t) -> p dc t", dc=NDC),
                         io["m2_in"].rearrange("(dc p) t -> p dc t", p=128), slot="m2")
            resid_guard.wrote(t_m2)
        t_m2 = resid_guard.writer
        if o1_loader is None:
            t_o1 = p.dma("gpsimd", o1_fm[:, :].rearrange("p (dc t) -> p dc t", dc=NDC),
                         io["o1"].rearrange("(dc p) t -> p dc t", p=128), slot="o1")
        else:
            t_o1 = o1_loader(p, o1_fm)

        for blk in range(4):
            wb = t_wb[blk % 2]
            gw = g_wb[blk % 2]
            tok = p.dma("gpsimd", wb[:, :].rearrange("p (dc o) -> p dc o", dc=NDC),
                        w_out[:, blk * 512:(blk + 1) * 512]
                        .rearrange("(dc p) o -> p dc o", p=128),
                        after=gw.war(), slot=f"wb{blk % 2}")
            gw.wrote(tok)
            for ob in range(4):
                for cc_ in range(2):
                    n = bk.take("f")
                    mms = [(wb[:, dc * 512 + ob * 128:dc * 512 + (ob + 1) * 128],
                            o1_fm[:, dc * TPC + cc_ * 512:dc * TPC + cc_ * 512 + 512])
                           for dc in range(NDC)]
                    m = bk.matmuls(n, bk.t[n][:, 0:512], mms, extra_after=[tok, t_o1])
                    gw.rd(m)
                    oc = blk * 4 + ob
                    sl = resid[:, oc * TPC + cc_ * 512: oc * TPC + cc_ * 512 + 512]
                    ra = bk.evac(n, "vector", tt(sl, sl, bk.t[n][:, 0:512], ALU.add),
                                 after=[t_m2])
                    resid_guard.rd(ra)

        for dc in range(NDC):
            p.out_dma("sync", y_out[dc * 128:(dc + 1) * 128, :],
                      resid[:, dc * TPC:(dc + 1) * TPC],
                      after=resid_guard.readers)



GROUPS = [[0, 1, 2, 3], [4, 5, 6, 7]]


def build_fused(dbg=False):
    p = Prog()
    nc = p.nc
    io = pre0_dram(nc, inter=True)
    io.update(mid_dram(nc, inter=True))
    io["idf"] = io["idf"] if "idf" in io else None
    io["maskin"] = nc.dram_tensor("maskin", [512, 512], BF16, kind="ExternalInput").ap()
    io["w_out"] = nc.dram_tensor("w_out", [D, D], F32, kind="ExternalInput").ap()
    io["y_out"] = nc.dram_tensor("y_out", [D, TPC], F32, kind="ExternalOutput").ap()
    io["gidx"] = nc.dram_tensor("gidx", [128, 176], mybir.dt.uint32, kind="ExternalInput").ap()
    aqk = nc.dram_tensor("aqk", [4 * 4096, TPC], BF16).ap()
    av = nc.dram_tensor("av", [4 * 4096, 512], BF16).ap()
    o_c = nc.dram_tensor("o_c", [4096, 512], F32)
    ao = nc.dram_tensor("ao", [4 * 4096, 512], F32).ap()
    aqk1 = nc.dram_tensor("aqk1", [4 * 4096, TPC], BF16).ap()
    av1 = nc.dram_tensor("av1", [4 * 4096, 512], BF16).ap()
    o1_c = nc.dram_tensor("o1_c", [512, S], BF16)
    ao1 = nc.dram_tensor("ao1", [4 * 512, S], BF16).ap()
    o_c_ap = o_c.ap()
    o1_c_ap = o1_c.ap()

    with ExitStack() as ctx:
        resid = ctx.enter_context(nc.sbuf_tensor("resid", [128, NDC * TPC], F32))
        sb_gidx = ctx.enter_context(nc.sbuf_tensor("sb_gidx", [128, 176], mybir.dt.uint32))
        bk = Banks(p, ctx, nf32=6, nbf=2)
        t_gidx = p.dma("gpsimd", sb_gidx[:, :], io["gidx"], slot="gidx")

        def coll(name, cin, cout, after):
            return p.emit("gpsimd",
                          (lambda e, i=cin, o=cout:
                           e.collective_compute("AllGather", ALU.bypass,
                                                replica_groups=GROUPS,
                                                ins=[i.opt()], outs=[o.opt()])),
                          after=after, inc=1, sem="cc")

        ccd = {}

        def pre0_cb(p_):
            evtoks = [p_.slot_last.get(f"ds_ev{k}") for k in range(4)]
            ccd["qk"] = coll("qk", io["qk_out"], aqk, evtoks)
            ccd["v"] = coll("v", io["v_out"], av, [ccd["qk"]])

        with ExitStack() as stx:
            emit_pre0(p, stx, bk, io, resid, fused=True, qkv_done_cb=pre0_cb)
        cc_qk = ccd["qk"]
        cc_v = ccd["v"]
        p.barrier()

        aio = {"maskin": io["maskin"], "idf": io["idf"], "o_out": o_c_ap}

        def loader0(p_, sb_q, sb_k, sb_v, sb_m, c_idf):
            p_.dma("sync", sb_m[:, :].rearrange("p (ms q) -> p ms q", ms=4),
                   io["maskin"].rearrange("(ms p) q -> p ms q", p=128), slot="ldm")
            p_.dma("sync", c_idf[:, :], io["idf"], slot="ldm")
            qk_toks, v_toks = [], []
            for pr in range(4):
                for j in range(4):
                    p_.igather(sb_q[:, pr * S + j * 1024: pr * S + (j + 1) * 1024],
                               aqk, sb_gidx[:, pr * 4 + j: pr * 4 + j + 1],
                               after=[cc_qk, t_gidx], slot=f"ld{pr}")
                    t = p_.igather(sb_k[:, pr * S + j * 1024: pr * S + (j + 1) * 1024],
                                   aqk, sb_gidx[:, 16 + pr * 4 + j: 16 + pr * 4 + j + 1],
                                   after=[cc_qk], slot=f"ld{pr}")
                qk_toks.append(t)
            for j in range(4):
                for tcs in range(8):
                    tc = j * 8 + tcs
                    t = p_.igather(sb_v[:, tc * 512:(tc + 1) * 512], av,
                                   sb_gidx[:, 32 + tc: 32 + tc + 1],
                                   after=[cc_v], slot=f"lv{j}")
                v_toks.append(t)
            return (qk_toks, v_toks)

        with ExitStack() as stx:
            emit_attn(p, stx, bk, aio, True, loader=loader0, o_dt=F32)
        cc_o = coll("o", o_c_ap, ao, [p.slot_last.get("ds_od")])
        p.barrier()

        def o_loader(p_, xb, tt_, after):
            t = None
            for r in range(4):
                t = p_.igather(xb[:, r * 512:(r + 1) * 512], ao,
                               sb_gidx[:, 64 + tt_ * 4 + r: 64 + tt_ * 4 + r + 1],
                               after=[cc_o, t_gidx] + list(after), slot=f"ox{tt_ % 2}")
            return t

        def mid_cb(p_):
            evtoks = [p_.slot_last.get(f"ds_ev{k}") for k in range(4)]
            ccd["qk1"] = coll("qk1", io["qk1_out"], aqk1, evtoks)
            ccd["v1"] = coll("v1", io["v1_out"], av1, [ccd["qk1"]])

        with ExitStack() as stx:
            emit_mid(p, stx, bk, io, resid, fused=True, o_loader=o_loader,
                     ev_dt=BF16, qkv_done_cb=mid_cb)
        cc_qk1 = ccd["qk1"]
        cc_v1 = ccd["v1"]
        p.barrier()

        aio1 = {"maskin": io["maskin"], "idf": io["idf"], "o_out": o1_c_ap}

        def loader1(p_, sb_q, sb_k, sb_v, sb_m, c_idf):
            p_.dma("sync", sb_m[:, :].rearrange("p (ms q) -> p ms q", ms=4),
                   io["maskin"].rearrange("(ms p) q -> p ms q", p=128), slot="ldm")
            p_.dma("sync", c_idf[:, :], io["idf"], slot="ldm")
            qk_toks, v_toks = [], []
            for pr in range(4):
                for j in range(4):
                    p_.igather(sb_q[:, pr * S + j * 1024: pr * S + (j + 1) * 1024],
                               aqk1, sb_gidx[:, 96 + pr * 4 + j: 96 + pr * 4 + j + 1],
                               after=[cc_qk1], slot=f"ld{pr}")
                    t = p_.igather(sb_k[:, pr * S + j * 1024: pr * S + (j + 1) * 1024],
                                   aqk1, sb_gidx[:, 112 + pr * 4 + j: 112 + pr * 4 + j + 1],
                                   after=[cc_qk1], slot=f"ld{pr}")
                qk_toks.append(t)
            for j in range(4):
                for tcs in range(8):
                    tc = j * 8 + tcs
                    t = p_.igather(sb_v[:, tc * 512:(tc + 1) * 512], av1,
                                   sb_gidx[:, 128 + tc: 128 + tc + 1],
                                   after=[cc_v1], slot=f"lv{j}")
                v_toks.append(t)
            return (qk_toks, v_toks)

        with ExitStack() as stx:
            emit_attn(p, stx, bk, aio1, False, loader=loader1, o_dt=BF16)
        cc_o1 = coll("o1", o1_c_ap, ao1, [p.slot_last.get("ds_od")])
        p.barrier()

        ao1v = ao1.rearrange("r (j t) -> (r j) t", j=4)

        def o1_loader(p_, o1_fm):
            t = None
            for dc in range(NDC):
                t = p_.igather(o1_fm[:, dc * TPC:(dc + 1) * TPC], ao1v,
                               sb_gidx[:, 160 + dc: 160 + dc + 1],
                               after=[cc_o1], slot="o1")
            return t

        fio = {"w_out": io["w_out"], "y_out": io["y_out"]}
        with ExitStack() as stx:
            emit_fin(p, stx, bk, fio, resid, fused=True, o1_loader=o1_loader)

        if dbg:
            p.barrier()
            for nm, ap in (("qk_out", io["qk_out"]), ("v_out", io["v_out"]),
                           ("aqk", aqk), ("o_c", o_c_ap), ("ao", ao),
                           ("qk1_out", io["qk1_out"]), ("v1_out", io["v1_out"]),
                           ("o1_c", o1_c_ap), ("ao1", ao1)):
                dt_ = ap.tensor.dtype if hasattr(ap, "tensor") else BF16
                dbg_t = nc.dram_tensor("dbg_" + nm, list(ap.shape), dt_,
                                       kind="ExternalOutput").ap()
                p.out_dma("gpsimd", dbg_t, ap, slot="dbg")
        p.render(ctx)
    return p.nc


def host_gidx(c):
    j0 = c % 4
    idx = np.zeros((176, 128), np.uint64)
    pp = np.arange(128, dtype=np.uint64)
    for pr in range(4):
        for j in range(4):
            idx[pr * 4 + j] = j * 4096 + j0 * 1024 + pr * 128 + pp
            idx[16 + pr * 4 + j] = j * 4096 + j0 * 1024 + 512 + pr * 128 + pp
            idx[96 + pr * 4 + j] = j * 4096 + j0 * 1024 + pr * 128 + pp
            idx[112 + pr * 4 + j] = j * 4096 + j0 * 1024 + 512 + pr * 128 + pp
    for tc in range(32):
        idx[32 + tc] = (tc // 8) * 4096 + j0 * 1024 + (tc % 8) * 128 + pp
        idx[128 + tc] = (tc // 8) * 4096 + j0 * 1024 + (tc % 8) * 128 + pp
    for tt_ in range(8):
        for r in range(4):
            idx[64 + tt_ * 4 + r] = r * 4096 + j0 * 1024 + tt_ * 128 + pp
    for dc in range(NDC):
        idx[160 + dc] = ((dc // 4) * 512 + (dc % 4) * 128 + pp) * 4 + j0
    return np.ascontiguousarray(idx.T.astype(np.uint32))


def host_inputs_fused(inputs):
    pre = host_inputs_pre0(inputs)
    mid = host_mid_inputs(inputs, [np.zeros((4096, 512), np.float32)] * 8,
                          [np.zeros((D, TPC), np.float32)] * 8)
    w_out = np.ascontiguousarray(np.asarray(inputs["L1_attn_w_out"], np.float32))
    mask = np.tril(np.ones((512, 512), np.float32)).T.astype(BF)
    maps = []
    for c in range(NCORE):
        m = dict(pre[c])
        for kk in ("wot", "wqkv", "wqg_t", "wqu_t", "wqd_t", "sg", "su", "sd"):
            m[kk] = mid[c][kk]
        m["w_out"] = w_out
        m["maskin"] = mask
        m["gidx"] = host_gidx(c)
        maps.append(m)
    return maps



def _esel():
    e = np.zeros((8, 512), np.float32)
    for ec in range(4):
        for pp in range(128):
            e[2 * ec + pp // 64, ec * 128 + pp] = 1.0
    return e.astype(BF)


def host_inputs_pre0(inputs):
    x = np.ascontiguousarray(np.asarray(inputs["L0_attn_x"], np.float32).reshape(B * S, D))
    shared = {
        "wqt": np.ascontiguousarray(np.asarray(inputs["L0_attn_w_q"], np.float32).T),
        "wkt": np.ascontiguousarray(np.asarray(inputs["L0_attn_w_k"], np.float32).T),
        "wvt": np.ascontiguousarray(np.asarray(inputs["L0_attn_w_v"], np.float32).T),
        "a_dt": np.ascontiguousarray(np.asarray(inputs["L0_ffn_A"], np.float32)
                                     .transpose(2, 0, 1).reshape(D, 512)),
        "b_allt": np.ascontiguousarray(np.asarray(inputs["L0_ffn_B"], np.float32)
                                       .transpose(0, 2, 1).reshape(512, D)),
        "wrt": np.ascontiguousarray(np.asarray(inputs["L0_ffn_Wr"], np.float32).T),
        "br": np.ascontiguousarray(np.asarray(inputs["L0_ffn_br"], np.float32).reshape(1, 8)),
        "esel": _esel(),
        "idf": np.eye(128, dtype=np.float32),
        "idb": np.eye(128).astype(BF),
    }
    return [dict(shared, x=np.ascontiguousarray(x[c * TPC:(c + 1) * TPC])) for c in range(NCORE)]




def host_attn_inputs(qk_list, v_list, l0):
    mask = np.tril(np.ones((512, 512), np.float32)).T.astype(BF)
    idf = np.eye(128, dtype=np.float32)
    maps = []
    for a in range(NCORE):
        b, hg = a // 4, a % 4
        qin = np.concatenate([qk_list[4 * b + j][hg * 1024: hg * 1024 + 512, :]
                              for j in range(4)], axis=1)
        kin = np.concatenate([qk_list[4 * b + j][hg * 1024 + 512: hg * 1024 + 1024, :]
                              for j in range(4)], axis=1)
        vin = np.concatenate([v_list[4 * b + j][hg * 1024:(hg + 1) * 1024, :]
                              for j in range(4)], axis=0)
        maps.append(dict(qin=np.ascontiguousarray(qin), kin=np.ascontiguousarray(kin),
                         vin=np.ascontiguousarray(vin), maskin=mask, idf=idf))
    return maps


def host_mid_inputs(inputs, att0_list, m1_list):
    shared = {
        "wot": np.ascontiguousarray(np.asarray(inputs["L0_attn_w_o"], np.float32).T),
        "wqkv": np.ascontiguousarray(np.asarray(inputs["L1_attn_w_qkv"], np.float32)),
        "wqg_t": np.ascontiguousarray(np.asarray(inputs["L1_ffn_wq_gate"], np.int32).T),
        "wqu_t": np.ascontiguousarray(np.asarray(inputs["L1_ffn_wq_up"], np.int32).T),
        "wqd_t": np.ascontiguousarray(np.asarray(inputs["L1_ffn_wq_down"], np.int32).T),
        "sg": np.ascontiguousarray(np.asarray(inputs["L1_ffn_s_gate"], np.float32)
                                   .reshape(NHC, 128).T),
        "su": np.ascontiguousarray(np.asarray(inputs["L1_ffn_s_up"], np.float32)
                                   .reshape(NHC, 128).T),
        "sd": np.ascontiguousarray(np.asarray(inputs["L1_ffn_s_down"], np.float32)
                                   .reshape(NDC, 128).T),
        "idf": np.eye(128, dtype=np.float32),
        "idb": np.eye(128).astype(BF),
    }
    maps = []
    for c in range(NCORE):
        b, j = c // 4, c % 4
        o_as = np.concatenate([att0_list[4 * b + hg][j * 1024:(j + 1) * 1024, :]
                               for hg in range(4)], axis=1)
        maps.append(dict(shared, o_as=np.ascontiguousarray(o_as),
                         m1_in=np.ascontiguousarray(m1_list[c])))
    return maps


def host_fin_inputs(inputs, att1_list, m2_list):
    w_out = np.ascontiguousarray(np.asarray(inputs["L1_attn_w_out"], np.float32))
    maps = []
    for c in range(NCORE):
        b, j = c // 4, c % 4
        o1 = np.concatenate([att1_list[4 * b + hg][:, j * 1024:(j + 1) * 1024]
                             for hg in range(4)], axis=0)
        maps.append(dict(w_out=w_out, o1=np.ascontiguousarray(o1),
                         m2_in=np.ascontiguousarray(m2_list[c])))
    return maps


def _run(nc, in_maps):
    res = run_bass_kernel_spmd(nc, in_maps, core_ids=list(range(NCORE)))
    return res.results


def kernel(**inputs):
    nc = build_fused()
    in_maps = host_inputs_fused(inputs)
    res = run_bass_kernel_spmd(nc, in_maps, core_ids=list(range(NCORE)))
    out = np.concatenate([np.asarray(r["y_out"]).T for r in res.results], axis=0)
    return np.ascontiguousarray(out.reshape(B, S, D).astype(np.float32))


def kernel_unfused(**inputs):
    r1 = _run(build_pre0(), host_inputs_pre0(inputs))
    qk_l = [np.asarray(r["qk_out"]) for r in r1]
    v_l = [np.asarray(r["v_out"]) for r in r1]
    m1_l = [np.asarray(r["m1_out"]) for r in r1]
    r2 = _run(build_attn(True), host_attn_inputs(qk_l, v_l, l0=True))
    att0 = [np.asarray(r["o_out"]) for r in r2]
    r3 = _run(build_mid(), host_mid_inputs(inputs, att0, m1_l))
    qk1 = [np.asarray(r["qk1_out"]) for r in r3]
    v1 = [np.asarray(r["v1_out"]) for r in r3]
    m2_l = [np.asarray(r["m2_out"]) for r in r3]
    r4 = _run(build_attn(False), host_attn_inputs(qk1, v1, l0=False))
    att1 = [np.asarray(r["o_out"]) for r in r4]
    r5 = _run(build_fin(), host_fin_inputs(inputs, att1, m2_l))
    out = np.concatenate([np.asarray(r["y_out"]).T for r in r5], axis=0)
    return np.ascontiguousarray(out.reshape(B, S, D).astype(np.float32))


# version 15 (final)
# speedup vs baseline: 1.0205x; 1.0076x over previous
import numpy as np
import ml_dtypes
from contextlib import ExitStack
from collections import defaultdict

import concourse.bass as bass
import concourse.mybir as mybir
from concourse.alu_op_type import AluOpType as ALU
from concourse.bass_utils import run_bass_kernel_spmd

F32 = mybir.dt.float32
BF16 = mybir.dt.bfloat16
I32 = mybir.dt.int32
AF = mybir.ActivationFunctionType
AX = mybir.AxisListType
BF = ml_dtypes.bfloat16

NCORE = 8
B, S, D = 2, 4096, 2048
H, HD = 16, 128
TPC = 1024
NDC = D // 128
EPS = 1e-5
ISQD = float(1.0 / np.sqrt(128.0))

ENGINES = ("sync", "tensor", "vector", "scalar", "gpsimd")


class Prog:

    def __init__(self):
        self.nc = bass.Bass("TRN2", target_bir_lowering=False)
        self.items = {e: [] for e in ENGINES}
        self.counts = defaultdict(int)
        self.lastw = {e: defaultdict(int) for e in ENGINES}
        self.tok_engine = {}
        self.out_toks = []

    def _wait(self, eng, sem, val):
        if val > self.lastw[eng][sem]:
            self.items[eng].append(("w", sem, val))
            self.lastw[eng][sem] = val

    def emit(self, eng, fn, after=(), inc=1, sem=None):
        if inc == 1 and eng != "tensor":
            v = self.counts["e_" + eng]
            if v > 0:
                self._wait(eng, "e_" + eng, v)
        if inc == 0:
            for t in after:
                if t is None or self.tok_engine.get(t) == eng:
                    continue
                self._wait(eng, t[0], t[1])
            self.items[eng].append(("q", fn))
            return None
        for t in after:
            if t is None:
                continue
            self._wait(eng, t[0], t[1])
        if sem is None:
            sem = ("d_" if inc == 16 else "e_") + eng
        self.counts[sem] += inc
        tok = (sem, self.counts[sem])
        self.items[eng].append(("i", fn, sem, inc))
        self.tok_engine[tok] = eng
        return tok

    def dma(self, eng, out, in_, after=(), slot=None, **kw):
        sem = "ds_" + (slot if slot else "anon_" + eng)
        t = self.emit(eng, (lambda e, o=out, i=in_, kw=kw: e.dma_start(out=o, in_=i, **kw)),
                      after=after, inc=16, sem=sem)
        self.slot_last = getattr(self, "slot_last", {})
        self.slot_last[sem] = t
        return t

    def igather(self, dst, src_ap, idx_ap, after=(), slot="ig"):
        sem = "ds_" + slot
        t = self.emit("gpsimd",
                      (lambda e, o=dst, i=src_ap, x=idx_ap:
                       e.indirect_dma_start(out=o, out_offset=None, in_=i,
                                            in_offset=bass.IndirectOffsetOnAxis(x, 0))),
                      after=after, inc=16, sem=sem)
        self.slot_last = getattr(self, "slot_last", {})
        self.slot_last[sem] = t
        return t

    def out_dma(self, eng, out, in_, after=(), slot=None, **kw):
        t = self.dma(eng, out, in_, after=after, slot=(slot or "out_" + eng), **kw)
        self.out_toks.append(t)
        return t

    def barrier(self, exclude=("cc",)):
        snap = dict(self.counts)
        for eng in ENGINES:
            for sem, val in snap.items():
                if sem in exclude:
                    continue
                self._wait(eng, sem, val)

    def check(self):
        pos = {e: 0 for e in ENGINES}
        cnt = defaultdict(int)
        progressed = True
        while progressed:
            progressed = False
            for e in ENGINES:
                while pos[e] < len(self.items[e]):
                    it = self.items[e][pos[e]]
                    if it[0] == "w" and cnt[it[1]] < it[2]:
                        break
                    if it[0] == "i":
                        cnt[it[2]] += it[3]
                    pos[e] += 1
                    progressed = True
        stuck = {e: (pos[e], len(self.items[e])) for e in ENGINES if pos[e] < len(self.items[e])}
        if stuck:
            raise RuntimeError(f"build-time deadlock at {stuck}")

    def render(self, ctx):
        fin = {}
        for t in self.out_toks:
            fin[t[0]] = max(fin.get(t[0], 0), t[1])
        for s, v in sorted(fin.items()):
            self._wait("gpsimd", s, v)
        self.check()
        sems = {s: ctx.enter_context(self.nc.semaphore(s)) for s in sorted(self.counts)}
        block = ctx.enter_context(self.nc.Block())

        def body(eng_name):
            def _(eng):
                for it in self.items[eng_name]:
                    if it[0] == "w":
                        eng.wait_ge(sems[it[1]], it[2])
                    elif it[0] == "q":
                        it[1](eng)
                    else:
                        it[1](eng).then_inc(sems[it[2]], it[3])
            return _

        for e in ENGINES:
            getattr(block, e)(body(e))


class Guard:

    def __init__(self):
        self.readers = []
        self.writer = None

    def rd(self, tok):
        self.readers.append(tok)

    def war(self):
        return list(self.readers) + ([self.writer] if self.writer else [])

    def wrote(self, tok):
        self.writer = tok
        self.readers = []


class Banks:

    def __init__(self, p, ctx, nf32=6, nbf=2):
        self.p = p
        self.t = {}
        self.wr = {}
        self.rd = {}
        self.pools = {"f": [], "b": []}
        self.i = {"f": 0, "b": 0}
        for k in range(nf32):
            n = f"pbf{k}"
            self.t[n] = ctx.enter_context(p.nc.psum_tensor(n, [128, 512], F32))
            self.wr[n] = None
            self.rd[n] = None
            self.pools["f"].append(n)
        for k in range(nbf):
            n = f"pbb{k}"
            self.t[n] = ctx.enter_context(p.nc.psum_tensor(n, [128, 1024], BF16))
            self.wr[n] = None
            self.rd[n] = None
            self.pools["b"].append(n)

    def take(self, kind="f"):
        pool = self.pools[kind]
        n = pool[self.i[kind]]
        self.i[kind] = (self.i[kind] + 1) % len(pool)
        return n

    def matmuls(self, n, out_ap, mms, extra_after=()):
        p = self.p
        after = [self.rd[n]] + list(extra_after)
        tok = None
        for j, (lhsT, rhs) in enumerate(mms):
            last = (j == len(mms) - 1)
            tok = p.emit("tensor",
                         (lambda e, o=out_ap, l=lhsT, r=rhs, st=(j == 0), sp=last:
                          e.matmul(o, l, r, start=st, stop=sp)),
                         after=after if j == 0 else (), inc=1 if last else 0)
        self.wr[n] = tok
        return tok

    def transpose(self, n, out_ap, src, ident, extra_after=()):
        p = self.p
        tok = p.emit("tensor",
                     (lambda e, o=out_ap, s=src, idn=ident: e.transpose(o, s, idn)),
                     after=[self.rd[n]] + list(extra_after))
        self.wr[n] = tok
        return tok

    def evac(self, n, eng, fn, after=()):
        tok = self.p.emit(eng, fn, after=[self.wr[n]] + list(after))
        self.rd[n] = tok
        return tok



def ts(out, in0, s1, op0, s2=None, op1=None):
    kw = {} if op1 is None else {"op1": op1}
    return (lambda e, o=out, i=in0, a=s1, b=s2, x=op0, kw=kw:
            e.tensor_scalar(out=o, in0=i, scalar1=a, scalar2=b, op0=x, **kw))


def tt(out, in0, in1, op):
    return (lambda e, o=out, a=in0, b=in1, x=op: e.tensor_tensor(out=o, in0=a, in1=b, op=x))


def stt(out, in0, scalar, in1, op0, op1):
    return (lambda e, o=out, i=in0, s=scalar, j=in1, x=op0, y=op1:
            e.scalar_tensor_tensor(out=o, in0=i, scalar=s, in1=j, op0=x, op1=y))


def act(out, in_, func, bias=None, scale=None, accum=None):
    kw = {}
    if bias is not None:
        kw["bias"] = bias
    if scale is not None:
        kw["scale"] = scale
    if accum is not None:
        kw["accum_out"] = accum
    return (lambda e, o=out, i=in_, f=func, kw=kw: e.activation(o, i, f, **kw))


def scp(out, in_):
    return (lambda e, o=out, i=in_: e.copy(o, i))


def cp(out, in_):
    return (lambda e, o=out, i=in_: e.tensor_copy(o, i))


def red(out, in_, op, absval=None, negate=None):
    kw = {}
    if absval:
        kw["apply_absolute_value"] = True
    if negate:
        kw["negate"] = True
    return (lambda e, o=out, i=in_, op=op, kw=kw:
            e.tensor_reduce(out=o, in_=i, axis=AX.X, op=op, **kw))


def recip(out, in_):
    return (lambda e, o=out, i=in_: e.reciprocal(o, i))


def mset(ap, v):
    return (lambda e, a=ap, v=v: e.memset(a, v))


def bnstats(out, in_):
    return (lambda e, o=out, i=in_: e.bn_stats(o, i))


def bnaggr(out, in_):
    return (lambda e, o=out, i=in_: e.bn_aggr(o, i))



def pre0_dram(nc, inter=False):
    io = {}
    io["x"] = nc.dram_tensor("x", [TPC, D], F32, kind="ExternalInput").ap()
    for w in ("wqt", "wkt", "wvt"):
        io[w] = nc.dram_tensor(w, [D, D], F32, kind="ExternalInput").ap()
    io["a_dt"] = nc.dram_tensor("a_dt", [D, 512], F32, kind="ExternalInput").ap()
    io["b_allt"] = nc.dram_tensor("b_allt", [512, D], F32, kind="ExternalInput").ap()
    io["wrt"] = nc.dram_tensor("wrt", [D, 8], F32, kind="ExternalInput").ap()
    io["br"] = nc.dram_tensor("br", [1, 8], F32, kind="ExternalInput").ap()
    io["esel"] = nc.dram_tensor("esel", [8, 512], BF16, kind="ExternalInput").ap()
    io["idf"] = nc.dram_tensor("idf", [128, 128], F32, kind="ExternalInput").ap()
    io["idb"] = nc.dram_tensor("idb", [128, 128], BF16, kind="ExternalInput").ap()
    dt = BF16 if inter else F32
    kind = {} if inter else {"kind": "ExternalOutput"}
    io["qk_out"] = nc.dram_tensor("qk_out", [4096, TPC], dt, **kind).ap()
    io["v_out"] = nc.dram_tensor("v_out", [4096, 512], dt, **kind).ap()
    if not inter:
        io["m1_out"] = nc.dram_tensor("m1_out", [D, TPC], F32, kind="ExternalOutput").ap()
    return io


def build_pre0(phases="ABC"):
    p = Prog()
    io = pre0_dram(p.nc)
    with ExitStack() as ctx:
        resid = ctx.enter_context(p.nc.sbuf_tensor("resid", [128, NDC * TPC], F32))
        bk = Banks(p, ctx)
        emit_pre0(p, ctx, bk, io, resid, fused=False)
        p.render(ctx)
    return p.nc


def emit_pre0(p, ctx, bk, io, resid, fused, qkv_done_cb=None):
    nc = p.nc
    x_in = io["x"]
    wts = {w: io[w] for w in ("wqt", "wkt", "wvt")}
    a_dt, b_allt, wrt, br, esel = io["a_dt"], io["b_allt"], io["wrt"], io["br"], io["esel"]
    idf, idb = io["idf"], io["idb"]
    qk_out, v_out = io["qk_out"], io["v_out"]
    m1_out = io.get("m1_out")
    EVDT = BF16 if fused else F32
    if True:
        sb = lambda n, s, d: ctx.enter_context(nc.sbuf_tensor(n, s, d, allow_name_mangling=True))
        xq_fm = sb("xq_fm", [128, NDC * TPC], BF16)
        invsx_c = sb("invsx_c", [128, 8], F32)
        invsx_b = sb("invsx_b", [128, TPC], F32)
        invsx_r = sb("invsx_r", [1, TPC], F32)
        c_idf = sb("c_idf", [128, 128], F32)
        c_idb = sb("c_idb", [128, 128], BF16)
        ones_cf = sb("ones_cf", [128, 1], F32)
        ones_rf = sb("ones_rf", [1, 128], F32)
        ones_rb = sb("ones_rb", [1, 128], BF16)

        cAB = ExitStack()
        xn_fm = cAB.enter_context(nc.sbuf_tensor("xn_fm", [128, NDC * TPC], BF16))

        p.dma("sync", c_idf[:, :], idf, slot="init")
        t_init = p.dma("sync", c_idb[:, :], idb, slot="init")
        t_idf = t_idb = t_init
        t_ocf = p.emit("gpsimd", mset(ones_cf[:, :], 1.0))
        t_orf = p.emit("gpsimd", mset(ones_rf[:, :], 1.0))
        t_orb = p.emit("gpsimd", mset(ones_rb[:, :], 1.0))

        with ExitStack() as c2:
            sb2 = lambda n, s, d: c2.enter_context(nc.sbuf_tensor(n, s, d, allow_name_mangling=True))
            t_x = [sb2(f"t_x{i}", [128, D], F32) for i in range(2)]
            t_xn = sb2("t_xn", [128, D], F32)
            t_v = sb2("t_v", [128, D], F32)
            t_i = sb2("t_i", [128, D], I32)
            t_kf = sb2("t_kf", [128, D], BF16)
            t_ca = sb2("t_ca", [128, D], BF16)
            t_cb = sb2("t_cb", [128, D], BF16)
            t_xq = sb2("t_xq", [128, D], BF16)
            t_xnb = sb2("t_xnb", [128, D], BF16)
            t_bn = sb2("t_bn", [128, 24], F32)
            t_mv = sb2("t_mv", [128, 2], F32)
            t_s = [sb2(f"t_s{i}", [128, 1], F32) for i in range(6)]
            g_x = [Guard(), Guard()]
            g_xq, g_xnb, g_s1, g_s2 = Guard(), Guard(), Guard(), Guard()
            ivr_toks = []

            for tt_ in range(8):
                xb = t_x[tt_ % 2]
                gx = g_x[tt_ % 2]
                tok_x = p.dma("sync", xb[:, :], x_in[tt_ * 128:(tt_ + 1) * 128, :],
                              after=gx.war(), slot=f"x{tt_ % 2}")
                gx.wrote(tok_x)
                for dc in range(NDC):
                    n = bk.take("f")
                    w = bk.transpose(n, bk.t[n][:, 0:128].bitcast(F32),
                                     xb[:, dc * 128:(dc + 1) * 128], c_idf[:, :],
                                     extra_after=[tok_x, t_idf])
                    gx.rd(w)
                    e = "scalar" if dc % 2 else "vector"
                    dst = resid[:, dc * TPC + tt_ * 128: dc * TPC + (tt_ + 1) * 128]
                    fn = scp(dst, bk.t[n][:, 0:128]) if e == "scalar" else cp(dst, bk.t[n][:, 0:128])
                    bk.evac(n, e, fn)
                for qq in range(4):
                    a1 = p.emit("vector", bnstats(t_bn[:, qq * 6:(qq + 1) * 6],
                                                  xb[:, qq * 512:(qq + 1) * 512]),
                                after=[tok_x])
                    gx.rd(a1)
                p.emit("vector", bnaggr(t_mv[:, :], t_bn[:, :]))
                a3 = p.emit("vector", ts(t_s[0][:, :], t_mv[:, 1:2], EPS, ALU.add))
                a4 = p.emit("vector", recip(t_s[1][:, :], t_s[0][:, :]), after=g_s1.war())
                g_s1.wrote(a4)
                a5 = p.emit("scalar", act(t_s[2][:, :], t_s[1][:, :], AF.Sqrt),
                            after=[a4] + g_s2.war())
                g_s1.rd(a5)
                g_s2.wrote(a5)
                a6 = p.emit("vector", ts(t_xn[:, :], xb[:, :], t_mv[:, 0:1], ALU.subtract,
                                         t_s[2][:, :], ALU.mult), after=[a5])
                g_s2.rd(a6)
                gx.rd(a6)
                a7 = p.emit("scalar", scp(t_xnb[:, :], t_xn[:, :]),
                            after=[a6] + g_xnb.war())
                g_xnb.wrote(a7)
                p.emit("vector", red(t_s[3][:, :], t_xn[:, :], ALU.max, absval=True),
                       after=[a6])
                p.emit("vector", ts(t_s[3][:, :], t_s[3][:, :], 1e-5, ALU.max))
                p.emit("vector", recip(t_s[4][:, :], t_s[3][:, :]))
                p.emit("vector", ts(t_s[5][:, :], t_s[4][:, :], 127.0, ALU.mult))
                ivc = p.emit("vector", ts(invsx_c[:, tt_:tt_ + 1], t_s[3][:, :],
                                          1.0 / 127.0, ALU.mult))
                p.emit("vector", ts(t_v[:, :], t_xn[:, :], t_s[5][:, :], ALU.mult),
                       after=[a7])
                p.emit("vector", cp(t_i[:, :], t_v[:, :]))
                p.emit("vector", cp(t_kf[:, :], t_i[:, :]))
                p.emit("vector", tt(t_v[:, :], t_v[:, :], t_kf[:, :], ALU.subtract))
                p.emit("vector", stt(t_ca[:, :], t_v[:, :], 0.5, t_kf[:, :],
                                     ALU.is_gt, ALU.add))
                p.emit("vector", ts(t_cb[:, :], t_v[:, :], -0.5, ALU.is_lt))
                p.emit("vector", tt(t_xq[:, :], t_ca[:, :], t_cb[:, :], ALU.subtract),
                       after=g_xq.war())
                p.emit("vector", ts(t_i[:, :], t_i[:, :], 1, ALU.bitwise_and))
                p.emit("vector", stt(t_ca[:, :], t_v[:, :], 0.5, t_i[:, :],
                                     ALU.is_equal, ALU.mult))
                p.emit("vector", stt(t_cb[:, :], t_v[:, :], -0.5, t_i[:, :],
                                     ALU.is_equal, ALU.mult))
                p.emit("vector", tt(t_xq[:, :], t_xq[:, :], t_ca[:, :], ALU.add))
                aq = p.emit("vector", tt(t_xq[:, :], t_xq[:, :], t_cb[:, :], ALU.subtract))
                g_xq.wrote(aq)
                for dc in range(NDC):
                    n = bk.take("b")
                    w = bk.transpose(n, bk.t[n][:, 0:128],
                                     t_xq[:, dc * 128:(dc + 1) * 128], c_idb[:, :],
                                     extra_after=[aq, t_idb])
                    g_xq.rd(w)
                    e = "scalar" if dc % 2 else "vector"
                    dst = xq_fm[:, dc * TPC + tt_ * 128: dc * TPC + (tt_ + 1) * 128]
                    fn = scp(dst, bk.t[n][:, 0:128]) if e == "scalar" else cp(dst, bk.t[n][:, 0:128])
                    bk.evac(n, e, fn)
                    n = bk.take("b")
                    w = bk.transpose(n, bk.t[n][:, 0:128],
                                     t_xnb[:, dc * 128:(dc + 1) * 128], c_idb[:, :],
                                     extra_after=[a7, t_idb])
                    g_xnb.rd(w)
                    dst = xn_fm[:, dc * TPC + tt_ * 128: dc * TPC + (tt_ + 1) * 128]
                    fn = scp(dst, bk.t[n][:, 0:128]) if e == "scalar" else cp(dst, bk.t[n][:, 0:128])
                    bk.evac(n, e, fn)
                n = bk.take("f")
                w = bk.transpose(n, bk.t[n][0:1, 0:128].bitcast(F32),
                                 invsx_c[:, tt_:tt_ + 1], c_idf[:, :],
                                 extra_after=[ivc, t_idf])
                ivr_toks.append(bk.evac(n, "vector",
                                        cp(invsx_r[0:1, tt_ * 128:(tt_ + 1) * 128],
                                           bk.t[n][0:1, 0:128])))
            for cc in range(2):
                n = bk.take("f")
                mm = [(ones_rf[0:1, :], invsx_r[0:1, cc * 512:(cc + 1) * 512])]
                bk.matmuls(n, bk.t[n][:, 0:512], mm, extra_after=ivr_toks + [t_orf])
                bk.evac(n, "vector", cp(invsx_b[:, cc * 512:(cc + 1) * 512],
                                        bk.t[n][:, 0:512]))

        p.barrier()
        NBLK, BW = 8, 256
        with ExitStack() as c2:
            sb2 = lambda n, s, d: c2.enter_context(nc.sbuf_tensor(n, s, d, allow_name_mangling=True))
            t_w = [sb2(f"t_w{i}", [128, NDC * BW], F32) for i in range(2)]
            t_b1 = sb2("t_b1", [128, NDC * BW], BF16)
            t_tern2 = [sb2(f"t_tern{i}", [128, NDC * BW], BF16) for i in range(2)]
            t_acc = sb2("t_acc", [128, NBLK], F32)
            t_g = [sb2(f"t_g{i}", [1, 1], F32) for i in range(3)]
            t_thr = sb2("t_thr", [128, 1], F32)
            t_nthr = sb2("t_nthr", [128, 1], F32)
            t_sw = sb2("t_sw", [128, 1], F32)
            t_ev = [sb2(f"t_ev{i}", [128, 512], EVDT) for i in range(4)]
            g_w = [Guard(), Guard()]
            g_tern2 = [Guard(), Guard()]
            g_acc, g_b1 = Guard(), Guard()
            g_ev = [Guard() for _ in range(4)]

            for wi, wname in enumerate(("wqt", "wkt", "wvt")):
                wT = wts[wname]
                acc_toks = []
                for blk in range(NBLK):
                    wb = t_w[blk % 2]
                    gw = g_w[blk % 2]
                    tok = p.dma("sync", wb[:, :].rearrange("p (ec o) -> p ec o", ec=NDC),
                                wT[:, blk * BW:(blk + 1) * BW]
                                .rearrange("(ec p) o -> p ec o", p=128),
                                after=gw.war(), slot=f"w{blk % 2}")
                    gw.wrote(tok)
                    a = p.emit("scalar", act(t_b1[:, :], wb[:, :], AF.Abs,
                                             accum=t_acc[:, blk:blk + 1]),
                               after=[tok] + (g_acc.war() + g_b1.war() if blk == 0 else []))
                    gw.rd(a)
                    if blk == 0:
                        g_b1.wrote(a)
                    acc_toks.append(a)
                g_acc.wrote(acc_toks[-1])
                g_b1.wrote(acc_toks[-1])
                s1 = p.emit("vector", red(t_thr[:, :], t_acc[:, :], ALU.add),
                            after=acc_toks)
                g_acc.rd(s1)
                n = bk.take("f")
                bk.matmuls(n, bk.t[n][0:1, 0:1], [(t_thr[:, :], ones_cf[:, :])],
                           extra_after=[s1, t_ocf])
                bk.evac(n, "vector", ts(t_g[0][0:1, :], bk.t[n][0:1, 0:1],
                                        1.0 / (D * D), ALU.mult))
                p.emit("vector", ts(t_g[1][0:1, :], t_g[0][0:1, :], 1e-5, ALU.max))
                g3 = p.emit("vector", ts(t_g[2][0:1, :], t_g[1][0:1, :], 0.5, ALU.mult))
                n = bk.take("f")
                bk.matmuls(n, bk.t[n][:, 0:1], [(ones_rf[0:1, :], t_g[1][0:1, :])],
                           extra_after=[g3, t_orf])
                scale_const = ISQD if wi == 0 else 1.0
                bk.evac(n, "vector", ts(t_sw[:, :], bk.t[n][:, 0:1], scale_const, ALU.mult))
                n = bk.take("f")
                bk.matmuls(n, bk.t[n][:, 0:1], [(ones_rf[0:1, :], t_g[2][0:1, :])],
                           extra_after=[g3, t_orf])
                bk.evac(n, "vector", cp(t_thr[:, :], bk.t[n][:, 0:1]))
                p.emit("vector", ts(t_nthr[:, :], t_thr[:, :], -1.0, ALU.mult))

                for blk in range(NBLK):
                    wb = t_w[blk % 2]
                    gw = g_w[blk % 2]
                    tok = p.dma("sync", wb[:, :].rearrange("p (ec o) -> p ec o", ec=NDC),
                                wT[:, blk * BW:(blk + 1) * BW]
                                .rearrange("(ec p) o -> p ec o", p=128),
                                after=gw.war(), slot=f"w{blk % 2}")
                    gw.wrote(tok)
                    t_tern = t_tern2[blk % 2]
                    g_tern = g_tern2[blk % 2]
                    b1 = p.emit("vector", ts(t_b1[:, :], wb[:, :], t_nthr[:, :], ALU.is_lt),
                                after=[tok] + g_b1.war())
                    g_b1.wrote(b1)
                    gw.rd(b1)
                    tern = p.emit("vector", stt(t_tern[:, :], wb[:, :], t_thr[:, :],
                                                t_b1[:, :], ALU.is_gt, ALU.subtract),
                                  after=g_tern.war())
                    gw.rd(tern)
                    g_b1.rd(tern)
                    g_tern.wrote(tern)
                    if wi < 2:
                        for ob in range(2):
                            for cc_ in range(2):
                                n = bk.take("f")
                                mms = [(t_tern[:, dc * BW + ob * 128:dc * BW + (ob + 1) * 128],
                                        xq_fm[:, dc * TPC + cc_ * 512:dc * TPC + cc_ * 512 + 512])
                                       for dc in range(NDC)]
                                m = bk.matmuls(n, bk.t[n][:, 0:512], mms,
                                               extra_after=[tern])
                                g_tern.rd(m)
                                k = (ob * 2 + cc_) % 4
                                ev = t_ev[k]
                                evt = bk.evac(n, "vector",
                                              stt(ev[:, :], bk.t[n][:, 0:512], t_sw[:, :],
                                                  invsx_b[:, cc_ * 512:cc_ * 512 + 512],
                                                  ALU.mult, ALU.mult),
                                              after=g_ev[k].war())
                                g_ev[k].wrote(evt)
                                h = blk * 2 + ob
                                rbase = (h // 4) * 1024 + (0 if wi == 0 else 512) + (h % 4) * 128
                                dt_ = p.out_dma("sync",
                                                qk_out[rbase:rbase + 128,
                                                       cc_ * 512:cc_ * 512 + 512],
                                                ev[:, :], after=[evt], slot=f"ev{k}")
                                g_ev[k].rd(dt_)
                    else:
                        for tt_ in range(8):
                            n = bk.take("f")
                            mms = [(xq_fm[:, dc * TPC + tt_ * 128:dc * TPC + (tt_ + 1) * 128],
                                    t_tern[:, dc * BW:(dc + 1) * BW])
                                   for dc in range(NDC)]
                            m = bk.matmuls(n, bk.t[n][:, 0:BW], mms, extra_after=[tern])
                            g_tern.rd(m)
                            k = tt_ % 4
                            ev = t_ev[k]
                            evt = bk.evac(n, "vector",
                                          ts(ev[:, 0:BW], bk.t[n][:, 0:BW],
                                             invsx_c[:, tt_:tt_ + 1], ALU.mult,
                                             t_sw[:, :], ALU.mult),
                                          after=g_ev[k].war())
                            g_ev[k].wrote(evt)
                            rbase = (blk // 2) * 1024 + tt_ * 128
                            cbase = (blk % 2) * 256
                            dt_ = p.out_dma("sync",
                                            v_out[rbase:rbase + 128, cbase:cbase + BW],
                                            ev[:, 0:BW], after=[evt], slot=f"ev{k}")
                            g_ev[k].rd(dt_)

        if qkv_done_cb is not None:
            qkv_done_cb(p)
        p.barrier()
        resid_toks = {}
        with ExitStack() as c2:
            sb2 = lambda n, s, d: c2.enter_context(nc.sbuf_tensor(n, s, d, allow_name_mangling=True))
            t_A = sb2("t_A", [128, NDC * 512], BF16)
            t_B = sb2("t_B", [128, 4 * D], BF16)
            t_W = sb2("t_W", [128, NDC * 8], BF16)
            t_br = sb2("t_br", [1, 8], BF16)
            t_E = sb2("t_E", [8, 512], BF16)
            t_mwT = sb2("t_mwT", [8, TPC], BF16)
            t_hm = sb2("t_hm", [128, 4 * TPC], BF16)
            t_bc = [sb2(f"t_bc{i}", [128, 512], F32) for i in range(2)]
            t_e8 = sb2("t_e8", [128, 8], F32)
            t_mw = sb2("t_mw", [128, 8], F32)
            t_u = [sb2(f"t_u{i}", [128, 1], F32) for i in range(4)]
            g_e8, g_mw, g_u0 = Guard(), Guard(), Guard()
            g_bc = [Guard(), Guard()]

            t_fst = sb2("t_fst", [128, NDC * 256], F32)
            prev = None
            for half in range(2):
                dd = p.dma("sync",
                           t_fst[:, :].rearrange("p (dc r) -> p dc r", dc=NDC),
                           a_dt[:, half * 256:(half + 1) * 256]
                           .rearrange("(dc p) r -> p dc r", p=128),
                           after=[prev], slot="binit")
                prev = p.emit("vector",
                              cp(t_A[:, :].rearrange("p (dc r) -> p dc r", dc=NDC)
                                 [:, :, half * 256:(half + 1) * 256],
                                 t_fst[:, :].rearrange("p (dc r) -> p dc r", dc=NDC)),
                              after=[dd])
            for half in range(2):
                dd = p.dma("sync",
                           t_fst[:, :].rearrange("p (ec o) -> p ec o", ec=2),
                           b_allt[half * 256:(half + 1) * 256, :]
                           .rearrange("(ec p) o -> p ec o", p=128),
                           after=[prev], slot="binit")
                prev = p.emit("vector",
                              cp(t_B[:, half * 2 * D:(half + 1) * 2 * D], t_fst[:, :]),
                              after=[dd])
            d3 = p.dma("sync", t_fst[:, 0:NDC * 8].rearrange("p (dc e) -> p dc e", dc=NDC),
                       wrt.rearrange("(dc p) e -> p dc e", p=128),
                       after=[prev], slot="binit")
            c3 = p.emit("vector", cp(t_W[:, :], t_fst[:, 0:NDC * 8]), after=[d3])
            d4 = p.dma("sync", t_fst[0:1, 0:8], br, after=[c3], slot="binit")
            c4 = p.emit("vector", cp(t_br[:, :], t_fst[0:1, 0:8]), after=[d4])
            t_ld = p.dma("sync", t_E[:, :], esel, slot="binit2")
            t_binit = p.emit("vector", cp(t_E[0:1, 0:1], t_E[0:1, 0:1]), after=[c4, t_ld])
            tA = tB = tW = tbr = tE = t_binit

            mwT_toks = []
            for tt_ in range(8):
                n = bk.take("f")
                mms = [(xn_fm[:, dc * TPC + tt_ * 128:dc * TPC + (tt_ + 1) * 128],
                        t_W[:, dc * 8:(dc + 1) * 8]) for dc in range(NDC)]
                mms.append((ones_rb[0:1, :], t_br[0:1, :]))
                mtok = bk.matmuls(n, bk.t[n][:, 0:8], mms,
                                  extra_after=[tW, tbr, t_orb])
                nm = p.emit("vector", red(t_u[0][:, :], bk.t[n][:, 0:8], ALU.max,
                                          negate=True), after=[mtok] + g_u0.war())
                g_u0.wrote(nm)
                ex = p.emit("scalar", act(t_e8[:, :], bk.t[n][:, 0:8], AF.Exp,
                                          bias=t_u[0][:, :]),
                            after=[mtok, nm] + g_e8.war())
                g_u0.rd(ex)
                g_e8.wrote(ex)
                bk.rd[n] = ex
                p.emit("vector", red(t_u[1][:, :], t_e8[:, :], ALU.add), after=[ex])
                mx = p.emit("vector", red(t_u[2][:, :], t_e8[:, :], ALU.max), after=[ex])
                p.emit("vector", recip(t_u[3][:, :], t_u[1][:, :]))
                p.emit("vector", stt(t_u[3][:, :], t_u[3][:, :], 1.0 / 64.0, t_u[2][:, :],
                                     ALU.mult, ALU.mult))
                mw = p.emit("vector", ts(t_mw[:, :], t_e8[:, :], t_u[2][:, :], ALU.is_ge,
                                         t_u[3][:, :], ALU.mult), after=g_mw.war())
                g_e8.rd(mw)
                g_mw.wrote(mw)
                n3 = bk.take("f")
                w = bk.transpose(n3, bk.t[n3][0:8, 0:128].bitcast(F32), t_mw[:, :],
                                 c_idf[:, :], extra_after=[mw, t_idf])
                g_mw.rd(w)
                mwT_toks.append(bk.evac(n3, "vector",
                                        cp(t_mwT[0:8, tt_ * 128:(tt_ + 1) * 128],
                                           bk.t[n3][0:8, 0:128])))
            hm_toks = []
            for ec in range(4):
                for cc_ in range(2):
                    j = (ec * 2 + cc_) % 2
                    n = bk.take("f")
                    mms = [(t_A[:, dc * 512 + ec * 128:dc * 512 + (ec + 1) * 128],
                            xn_fm[:, dc * TPC + cc_ * 512:dc * TPC + cc_ * 512 + 512])
                           for dc in range(NDC)]
                    bk.matmuls(n, bk.t[n][:, 0:512], mms, extra_after=[tA])
                    n3 = bk.take("f")
                    bk.matmuls(n3, bk.t[n3][:, 0:512],
                               [(t_E[0:8, ec * 128:(ec + 1) * 128],
                                 t_mwT[0:8, cc_ * 512:cc_ * 512 + 512])],
                               extra_after=[tE] + mwT_toks)
                    bc = bk.evac(n3, "scalar", scp(t_bc[j][:, :], bk.t[n3][:, 0:512]),
                                 after=g_bc[j].war())
                    g_bc[j].wrote(bc)
                    hm = bk.evac(n, "vector",
                                 tt(t_hm[:, ec * TPC + cc_ * 512:ec * TPC + cc_ * 512 + 512],
                                    bk.t[n][:, 0:512], t_bc[j][:, :], ALU.mult),
                                 after=[bc])
                    g_bc[j].rd(hm)
                    hm_toks.append(hm)
            for oc in range(NDC):
                for cc_ in range(2):
                    n = bk.take("f")
                    mms = [(t_B[:, ec * D + oc * 128:ec * D + (oc + 1) * 128],
                            t_hm[:, ec * TPC + cc_ * 512:ec * TPC + cc_ * 512 + 512])
                           for ec in range(4)]
                    bk.matmuls(n, bk.t[n][:, 0:512], mms, extra_after=[tB] + hm_toks)
                    sl = resid[:, oc * TPC + cc_ * 512: oc * TPC + cc_ * 512 + 512]
                    resid_toks[(oc, cc_)] = bk.evac(
                        n, "vector", tt(sl, bk.t[n][:, 0:512], sl, ALU.add))

        cAB.close()
        if not fused:
            for dc in range(NDC):
                p.out_dma("sync", m1_out[dc * 128:(dc + 1) * 128, :],
                          resid[:, dc * TPC:(dc + 1) * TPC],
                          after=[resid_toks[(dc, 0)], resid_toks[(dc, 1)]])





def build_attn(l0: bool):
    p = Prog()
    nc = p.nc
    io = {}
    io["qin"] = nc.dram_tensor("qin", [512, S], F32, kind="ExternalInput").ap()
    io["kin"] = nc.dram_tensor("kin", [512, S], F32, kind="ExternalInput").ap()
    io["vin"] = nc.dram_tensor("vin", [S, 512], F32, kind="ExternalInput").ap()
    io["maskin"] = nc.dram_tensor("maskin", [512, 512], BF16, kind="ExternalInput").ap()
    io["idf"] = nc.dram_tensor("idf", [128, 128], F32, kind="ExternalInput").ap()
    if l0:
        io["o_out"] = nc.dram_tensor("o_out", [S, 512], F32, kind="ExternalOutput").ap()
    else:
        io["o_out"] = nc.dram_tensor("o_out", [512, S], F32, kind="ExternalOutput").ap()
    with ExitStack() as ctx:
        bk = Banks(p, ctx, nf32=6, nbf=2)
        emit_attn(p, ctx, bk, io, l0, loader=None, o_dt=F32)
        p.render(ctx)
    return p.nc


def emit_attn(p, ctx, bk, io, l0, loader, o_dt, pair_done_cb=None):
    nc = p.nc
    maskin, idf = io["maskin"], io["idf"]
    o_out = io["o_out"]
    if True:
        sb = lambda n, s, d: ctx.enter_context(nc.sbuf_tensor(n, s, d, allow_name_mangling=True))
        all_f = list(bk.pools["f"])
        bk.pools["f"] = all_f[:4]
        bk.i["f"] = 0
        sb_q = sb("sb_q", [128, 4 * S], BF16)
        sb_k = sb("sb_k", [128, 4 * S], BF16)
        sb_v = sb("sb_v", [128, 32 * 512], BF16)
        sb_m = sb("sb_m", [128, 4 * 512], BF16)
        sb_sq = sb("sb_sq", [128, 512], BF16)
        sb_P = sb("sb_P", [128, 4 * 512], BF16)
        sb_li = sb("sb_li", [1, 512], F32)
        sb_negc = sb("sb_negc", [128, 4], F32)
        sb_u = [sb(f"sb_u{i}", [1, 1], F32) for i in range(3)]
        sb_of = sb("sb_of", [128, 512], F32)
        sb_otm = sb("sb_otm", [128, 4 * 128], F32)
        sb_liT = sb("sb_liT", [128, 4], F32)
        sb_ob = sb("sb_ob", [128, 512], F32)
        sb_obf = sb("sb_obf", [128, 512], o_dt)
        c_idf = sb("c_idf", [128, 128], F32)
        ones_cb = sb("ones_cb", [128, 1], BF16)
        ones_rf = sb("ones_rf", [1, 128], F32)

        if loader is None:
            p.dma("gpsimd", sb_q[:, :].rearrange("p (ph t) -> p ph t", ph=4),
                  io["qin"].rearrange("(ph p) t -> p ph t", p=128), slot="init")
            p.dma("gpsimd", sb_k[:, :].rearrange("p (ph t) -> p ph t", ph=4),
                  io["kin"].rearrange("(ph p) t -> p ph t", p=128), slot="init")
            p.dma("gpsimd", sb_v[:, :].rearrange("p (tc d) -> p tc d", tc=32),
                  io["vin"].rearrange("(tc p) d -> p tc d", p=128), slot="init")
            p.dma("gpsimd", sb_m[:, :].rearrange("p (ms q) -> p ms q", ms=4),
                  maskin.rearrange("(ms p) q -> p ms q", p=128), slot="init")
            t_init = p.dma("gpsimd", c_idf[:, :], idf, slot="init")
            tq = tk = tv = tm_ = t_init
            qk_toks = [t_init] * 4
            v_toks = [t_init] * 4
        else:
            t_qkv = loader(p, sb_q, sb_k, sb_v, sb_m, c_idf)
            if isinstance(t_qkv, tuple):
                qk_toks, v_toks = t_qkv
                t_qkv = v_toks[-1]
            else:
                qk_toks = [t_qkv] * 4
                v_toks = [t_qkv] * 4
            t_ldm = p.slot_last.get("ds_ldm")
            tq = tk = tv = t_qkv
            tm_ = t_init = t_ldm
        p.emit("gpsimd", mset(ones_cb[:, :], 1.0))
        t_ones = p.emit("gpsimd", mset(ones_rf[:, :], 1.0))

        g_sq, g_n, g_u = Guard(), Guard(), Guard()
        g_P = [Guard() for _ in range(4)]
        g_li, g_of, g_otm, g_liT, g_ob, g_obf = (Guard(), Guard(), Guard(),
                                                 Guard(), Guard(), Guard())

        for pr in range(4):
            last_acc = None
            for which, buf in ((0, sb_q), (1, sb_k)):
                for cc in range(8):
                    sq = p.emit("scalar", act(sb_sq[:, :],
                                              buf[:, pr * S + cc * 512: pr * S + (cc + 1) * 512],
                                              AF.Square),
                                after=[t_init, qk_toks[pr]] + g_sq.war())
                    g_sq.wrote(sq)
                    n = bk.take("f")
                    mm = [(ones_cb[:, :], sb_sq[:, :])]
                    mt = bk.matmuls(n, bk.t[n][0:1, 0:512], mm,
                                    extra_after=[sq, t_ones])
                    g_sq.rd(mt)
                    if cc == 0:
                        ev = bk.evac(n, "vector",
                                     red(sb_u[which][0:1, :], bk.t[n][0:1, 0:512], ALU.max),
                                     after=g_u.war() if which == 0 else [])
                    else:
                        ev = bk.evac(n, "vector",
                                     red(sb_u[2][0:1, :], bk.t[n][0:1, 0:512], ALU.max))
                        ev = p.emit("vector", tt(sb_u[which][0:1, :], sb_u[which][0:1, :],
                                                 sb_u[2][0:1, :], ALU.max))
                    last_acc = ev
            r3 = p.emit("vector", tt(sb_u[2][0:1, :], sb_u[0][0:1, :], sb_u[1][0:1, :],
                                     ALU.mult), after=[last_acc])
            r4 = p.emit("scalar", act(sb_u[2][0:1, :], sb_u[2][0:1, :], AF.Sqrt),
                        after=[r3])
            g_u.wrote(r4)
            n = bk.take("f")
            bk.matmuls(n, bk.t[n][:, 0:1], [(ones_rf[0:1, :], sb_u[2][0:1, :])],
                       extra_after=[r4, t_ones])
            ng = bk.evac(n, "vector", ts(sb_negc[:, pr:pr + 1], bk.t[n][:, 0:1],
                                         -1.0, ALU.mult))
            g_u.rd(ng)

            bpool = bk.pools["b"]
            for qs in range(8):
                if qs % 2 == 0:
                    nO = all_f[4]
                    nL = all_f[5]
                else:
                    nO = bpool[0]
                    nL = bpool[1]
                units = [(tc, ts_) for tc in range(qs + 1) for ts_ in range(4)]
                nsub = len(units)
                AH = 4
                pend = {}
                for i in range(nsub + AH):
                    j = i - AH
                    if j >= 0:
                        tcc, ts_, ptok = pend.pop(j)
                        first = (j == 0)
                        last = (j == nsub - 1)
                        lt = p.emit("tensor",
                                    (lambda e, o=(bk.t[nL][0:1, 0:1024].bitcast(F32) if nL.startswith('pbb') else bk.t[nL][0:1, 0:512]),
                                     l=ones_cb[:, :], r=sb_P[:, ts_ * 512:(ts_ + 1) * 512],
                                     st=first, sp=last:
                                     e.matmul(o, l, r, start=st, stop=sp)),
                                    after=[ptok, t_ones, bk.rd[nL] if first else None],
                                    inc=0 if not last else 1)
                        if lt is not None:
                            bk.wr[nL] = lt
                        pv = p.emit("tensor",
                                    (lambda e, o=(bk.t[nO][:, 0:1024].bitcast(F32) if nO.startswith('pbb') else bk.t[nO][:, 0:512]),
                                     l=sb_v[:, tcc * 512 + pr * 128: tcc * 512 + (pr + 1) * 128],
                                     r=sb_P[:, ts_ * 512:(ts_ + 1) * 512],
                                     st=first, sp=last:
                                     e.matmul(o, l, r, start=st, stop=sp)),
                                    after=[ptok, v_toks[tcc // 32], bk.rd[nO] if first else None])
                        g_P[ts_].rd(pv)
                        bk.wr[nO] = pv
                    if i < nsub:
                        tc, ts_ = units[i]
                        n = bk.take("f")
                        mm = [(sb_k[:, pr * S + tc * 512 + ts_ * 128:
                                    pr * S + tc * 512 + (ts_ + 1) * 128],
                               sb_q[:, pr * S + qs * 512: pr * S + (qs + 1) * 512])]
                        mt = bk.matmuls(n, bk.t[n][:, 0:512], mm,
                                        extra_after=[qk_toks[pr]])
                        ex = bk.evac(n, "scalar",
                                     act(sb_P[:, ts_ * 512:(ts_ + 1) * 512],
                                         bk.t[n][:, 0:512], AF.Exp,
                                         bias=sb_negc[:, pr:pr + 1]),
                                     after=[ng] + g_P[ts_].war())
                        g_P[ts_].wrote(ex)
                        ptok = ex
                        if tc == qs:
                            mk = p.emit("vector",
                                        tt(sb_P[:, ts_ * 512:(ts_ + 1) * 512],
                                           sb_P[:, ts_ * 512:(ts_ + 1) * 512],
                                           sb_m[:, ts_ * 512:(ts_ + 1) * 512],
                                           ALU.mult), after=[ex, tm_])
                            g_P[ts_].wrote(mk)
                            ptok = mk
                        pend[i] = (tc * 4 + ts_, ts_, ptok)

                li = p.emit("vector", recip(sb_li[0:1, :], (bk.t[nL][0:1, 0:1024].bitcast(F32) if nL.startswith('pbb') else bk.t[nL][0:1, 0:512])),
                            after=[bk.wr[nL]] + g_li.war())
                bk.rd[nL] = li
                g_li.wrote(li)
                if l0:
                    ev = p.emit("vector", cp(sb_of[:, :], (bk.t[nO][:, 0:1024].bitcast(F32) if nO.startswith('pbb') else bk.t[nO][:, 0:512])),
                                after=[bk.wr[nO]] + g_of.war())
                    bk.rd[nO] = ev
                    g_of.wrote(ev)
                    for qsub in range(4):
                        n = bk.take("f")
                        w = bk.transpose(n, bk.t[n][:, 0:1].bitcast(F32),
                                         sb_li[0:1, qsub * 128:(qsub + 1) * 128],
                                         c_idf[0:1, 0:1], extra_after=[li, t_init])
                        g_li.rd(w)
                        lt_ = bk.evac(n, "vector", cp(sb_liT[:, qsub:qsub + 1],
                                                      bk.t[n][:, 0:1]),
                                      after=g_liT.war() if qsub == 0 else ())
                        if qsub == 0:
                            g_liT.wrote(lt_)
                    g_liT.wrote(lt_)
                    ot_toks = []
                    for qsub in range(4):
                        n = bk.take("f")
                        w = bk.transpose(n, bk.t[n][:, 0:128],
                                         sb_of[:, qsub * 128:(qsub + 1) * 128],
                                         c_idf[:, :], extra_after=[ev, t_init])
                        g_of.rd(w)
                        ot = bk.evac(n, "vector",
                                     ts(sb_otm[:, qsub * 128:(qsub + 1) * 128],
                                        bk.t[n][:, 0:128], sb_liT[:, qsub:qsub + 1],
                                        ALU.mult),
                                     after=[lt_] + (g_otm.war() if qsub == 0 else []))
                        g_liT.rd(ot)
                        if qsub == 0:
                            g_otm.wrote(ot)
                        ot_toks.append(ot)
                    g_otm.wrote(ot_toks[-1])
                    od = p.out_dma("sync",
                                   o_out[qs * 512:(qs + 1) * 512, pr * 128:(pr + 1) * 128]
                                   .rearrange("(qsub r) d -> r qsub d", r=128),
                                   sb_otm[:, :].rearrange("r (qsub d) -> r qsub d", qsub=4),
                                   after=ot_toks, slot="od")
                    g_otm.rd(od)
                else:
                    n = bk.take("f")
                    bk.matmuls(n, bk.t[n][:, 0:512], [(ones_rf[0:1, :], sb_li[0:1, :])],
                               extra_after=[li, t_ones])
                    g_li.rd(bk.wr[n])
                    bo = bk.evac(n, "vector", cp(sb_ob[:, :], bk.t[n][:, 0:512]),
                                 after=g_ob.war())
                    g_ob.wrote(bo)
                    ev = p.emit("vector", tt(sb_obf[:, :], (bk.t[nO][:, 0:1024].bitcast(F32) if nO.startswith('pbb') else bk.t[nO][:, 0:512]),
                                             sb_ob[:, :], ALU.mult),
                                after=[bk.wr[nO], bo] + g_obf.war())
                    bk.rd[nO] = ev
                    g_ob.rd(ev)
                    g_obf.wrote(ev)
                    od = p.out_dma("sync",
                                   o_out[pr * 128:(pr + 1) * 128, qs * 512:(qs + 1) * 512],
                                   sb_obf[:, :], after=[ev], slot="od")
                    g_obf.rd(od)
            if pair_done_cb is not None:
                pair_done_cb(pr)
        bk.pools["f"] = all_f
        bk.i["f"] = 0



HID = 5632
NHC = HID // 128
ZP = 128.0


def mid_dram(nc, inter=False):
    io = {}
    if not inter:
        io["o_as"] = nc.dram_tensor("o_as", [TPC, D], F32, kind="ExternalInput").ap()
        io["m1_in"] = nc.dram_tensor("m1_in", [D, TPC], F32, kind="ExternalInput").ap()
    io["wot"] = nc.dram_tensor("wot", [D, D], F32, kind="ExternalInput").ap()
    io["wqkv"] = nc.dram_tensor("wqkv", [D, 3 * D], F32, kind="ExternalInput").ap()
    io["wqg_t"] = nc.dram_tensor("wqg_t", [D, HID], I32, kind="ExternalInput").ap()
    io["wqu_t"] = nc.dram_tensor("wqu_t", [D, HID], I32, kind="ExternalInput").ap()
    io["wqd_t"] = nc.dram_tensor("wqd_t", [HID, D], I32, kind="ExternalInput").ap()
    io["sg"] = nc.dram_tensor("sg", [128, NHC], F32, kind="ExternalInput").ap()
    io["su"] = nc.dram_tensor("su", [128, NHC], F32, kind="ExternalInput").ap()
    io["sd"] = nc.dram_tensor("sd", [128, NDC], F32, kind="ExternalInput").ap()
    if inter:
        io["qk1_out"] = nc.dram_tensor("qk1_out", [4096, TPC], BF16).ap()
        io["v1_out"] = nc.dram_tensor("v1_out", [4096, 512], BF16).ap()
    else:
        io["idf"] = nc.dram_tensor("idf", [128, 128], F32, kind="ExternalInput").ap()
        io["idb"] = nc.dram_tensor("idb", [128, 128], BF16, kind="ExternalInput").ap()
        io["qk1_out"] = nc.dram_tensor("qk1_out", [4096, TPC], F32, kind="ExternalOutput").ap()
        io["v1_out"] = nc.dram_tensor("v1_out", [4096, 512], F32, kind="ExternalOutput").ap()
        io["m2_out"] = nc.dram_tensor("m2_out", [D, TPC], F32, kind="ExternalOutput").ap()
    return io


def build_mid(phases="ABCDE"):
    p = Prog()
    io = mid_dram(p.nc)
    with ExitStack() as ctx:
        resid = ctx.enter_context(p.nc.sbuf_tensor("resid", [128, NDC * TPC], F32))
        bk = Banks(p, ctx, nf32=6, nbf=2)
        emit_mid(p, ctx, bk, io, resid, fused=False, o_loader=None, ev_dt=F32)
        p.render(ctx)
    return p.nc


def emit_mid(p, ctx, bk, io, resid, fused, o_loader, ev_dt, qkv_done_cb=None):
    nc = p.nc
    wot, wqkv = io["wot"], io["wqkv"]
    wqg_t, wqu_t, wqd_t = io["wqg_t"], io["wqu_t"], io["wqd_t"]
    sg_in, su_in, sd_in = io["sg"], io["su"], io["sd"]
    idf, idb = io["idf"], io["idb"]
    qk1_out, v1_out = io["qk1_out"], io["v1_out"]
    m2_out = io.get("m2_out")
    if True:
        sb = lambda n, s, d: ctx.enter_context(nc.sbuf_tensor(n, s, d, allow_name_mangling=True))
        xn1_fm = sb("xn1_fm", [128, NDC * TPC], BF16)
        invso_c = sb("invso_c", [128, 8], F32)
        invso_b = sb("invso_b", [128, TPC], F32)
        invso_r = sb("invso_r", [1, TPC], F32)
        c_idf = sb("c_idf", [128, 128], F32)
        c_idb = sb("c_idb", [128, 128], BF16)
        ones_cf = sb("ones_cf", [128, 1], F32)
        ones_cb = sb("ones_cb", [128, 1], BF16)
        ones_rf = sb("ones_rf", [1, 128], F32)
        m128_r = sb("m128_r", [1, 128], BF16)
        sg = sb("sgv", [128, NHC], F32)
        su = sb("suv", [128, NHC], F32)
        sd = sb("sdv", [128, NDC], F32)

        p.dma("sync", c_idf[:, :], idf, slot="init")
        p.dma("sync", c_idb[:, :], idb, slot="init")
        p.dma("sync", sg[:, :], sg_in, slot="init")
        p.dma("sync", su[:, :], su_in, slot="init")
        t_init = p.dma("sync", sd[:, :], sd_in, slot="init")
        t_idf = t_idb = t_scl = t_init
        p.emit("gpsimd", mset(ones_cf[:, :], 1.0))
        p.emit("gpsimd", mset(ones_cb[:, :], 1.0))
        p.emit("gpsimd", mset(ones_rf[:, :], 1.0))
        t_ones = p.emit("gpsimd", mset(m128_r[:, :], -128.0))

        resid_guard = Guard()
        if not fused:
            t_m1 = p.dma("gpsimd", resid[:, :].rearrange("p (dc t) -> p dc t", dc=NDC),
                         io["m1_in"].rearrange("(dc p) t -> p dc t", p=128), slot="m1")
            resid_guard.wrote(t_m1)
        t_m1 = resid_guard.writer

        t_thr = ctx.enter_context(nc.sbuf_tensor("t_thr", [128, 1], F32, allow_name_mangling=True))
        t_nthr = ctx.enter_context(nc.sbuf_tensor("t_nthr", [128, 1], F32, allow_name_mangling=True))
        t_sw = ctx.enter_context(nc.sbuf_tensor("t_sw", [128, 1], F32, allow_name_mangling=True))
        NBLK, BW = 8, 256
        with ExitStack() as cW:
            sbw = lambda n, s, d: cW.enter_context(nc.sbuf_tensor(n, s, d, allow_name_mangling=True))
            t_w = [sbw(f"t_w{i}", [128, NDC * BW], F32) for i in range(2)]
            t_ab = sbw("t_ab", [128, NDC * BW], BF16)
            t_acc = sbw("t_acc", [128, NBLK], F32)
            t_g = [sbw(f"t_g{i}", [1, 1], F32) for i in range(3)]
            g_w = [Guard(), Guard()]
            acc_toks = []
            for blk in range(NBLK):
                wb = t_w[blk % 2]
                gw = g_w[blk % 2]
                tok = p.dma("sync", wb[:, :].rearrange("p (ec o) -> p ec o", ec=NDC),
                            wot[:, blk * BW:(blk + 1) * BW]
                            .rearrange("(ec p) o -> p ec o", p=128),
                            after=gw.war(), slot=f"wm{blk % 2}")
                gw.wrote(tok)
                a = p.emit("scalar", act(t_ab[:, :], wb[:, :], AF.Abs,
                                         accum=t_acc[:, blk:blk + 1]), after=[tok])
                gw.rd(a)
                acc_toks.append(a)
            s1 = p.emit("vector", red(t_thr[:, :], t_acc[:, :], ALU.add), after=acc_toks)
            n = bk.take("f")
            bk.matmuls(n, bk.t[n][0:1, 0:1], [(t_thr[:, :], ones_cf[:, :])],
                       extra_after=[s1, t_ones])
            bk.evac(n, "vector", ts(t_g[0][0:1, :], bk.t[n][0:1, 0:1],
                                    1.0 / (D * D), ALU.mult))
            p.emit("vector", ts(t_g[1][0:1, :], t_g[0][0:1, :], 1e-5, ALU.max))
            g3 = p.emit("vector", ts(t_g[2][0:1, :], t_g[1][0:1, :], 0.5, ALU.mult))
            n = bk.take("f")
            bk.matmuls(n, bk.t[n][:, 0:1], [(ones_rf[0:1, :], t_g[1][0:1, :])],
                       extra_after=[g3, t_ones])
            bk.evac(n, "vector", cp(t_sw[:, :], bk.t[n][:, 0:1]))
            n = bk.take("f")
            bk.matmuls(n, bk.t[n][:, 0:1], [(ones_rf[0:1, :], t_g[2][0:1, :])],
                       extra_after=[g3, t_ones])
            bk.evac(n, "vector", cp(t_thr[:, :], bk.t[n][:, 0:1]))
            t_nth = p.emit("vector", ts(t_nthr[:, :], t_thr[:, :], -1.0, ALU.mult))

        oq_fm = None
        cA = ExitStack()
        oq_fm = cA.enter_context(nc.sbuf_tensor("oq_fm", [128, NDC * TPC], BF16))
        with ExitStack() as c2:
            sb2 = lambda n, s, d: c2.enter_context(nc.sbuf_tensor(n, s, d, allow_name_mangling=True))
            t_x = [sb2(f"t_x{i}", [128, D], F32) for i in range(2)]
            t_v = sb2("t_v", [128, D], F32)
            t_i = sb2("t_i", [128, D], I32)
            t_kf = sb2("t_kf", [128, D], BF16)
            t_ca = sb2("t_ca", [128, D], BF16)
            t_cb = sb2("t_cb", [128, D], BF16)
            t_xq = sb2("t_xq", [128, D], BF16)
            t_s = [sb2(f"t_s{i}", [128, 1], F32) for i in range(6)]
            g_x = [Guard(), Guard()]
            g_xq = Guard()
            ivr_toks = []
            for tt_ in range(8):
                xb = t_x[tt_ % 2]
                gx = g_x[tt_ % 2]
                if o_loader is None:
                    tok_x = p.dma("sync", xb[:, :], io["o_as"][tt_ * 128:(tt_ + 1) * 128, :],
                                  after=gx.war(), slot=f"x{tt_ % 2}")
                else:
                    tok_x = o_loader(p, xb, tt_, gx.war())
                gx.wrote(tok_x)
                a6 = tok_x
                r0 = p.emit("vector", red(t_s[3][:, :], xb[:, :], ALU.max, absval=True),
                            after=[tok_x])
                gx.rd(r0)
                p.emit("vector", ts(t_s[3][:, :], t_s[3][:, :], 1e-5, ALU.max))
                p.emit("vector", recip(t_s[4][:, :], t_s[3][:, :]))
                p.emit("vector", ts(t_s[5][:, :], t_s[4][:, :], 127.0, ALU.mult))
                ivc = p.emit("vector", ts(invso_c[:, tt_:tt_ + 1], t_s[3][:, :],
                                          1.0 / 127.0, ALU.mult))
                vv = p.emit("vector", ts(t_v[:, :], xb[:, :], t_s[5][:, :], ALU.mult))
                gx.rd(vv)
                p.emit("vector", cp(t_i[:, :], t_v[:, :]))
                p.emit("vector", cp(t_kf[:, :], t_i[:, :]))
                p.emit("vector", tt(t_v[:, :], t_v[:, :], t_kf[:, :], ALU.subtract))
                p.emit("vector", stt(t_ca[:, :], t_v[:, :], 0.5, t_kf[:, :],
                                     ALU.is_gt, ALU.add))
                p.emit("vector", ts(t_cb[:, :], t_v[:, :], -0.5, ALU.is_lt))
                p.emit("vector", tt(t_xq[:, :], t_ca[:, :], t_cb[:, :], ALU.subtract),
                       after=g_xq.war())
                p.emit("vector", ts(t_i[:, :], t_i[:, :], 1, ALU.bitwise_and))
                p.emit("vector", stt(t_ca[:, :], t_v[:, :], 0.5, t_i[:, :],
                                     ALU.is_equal, ALU.mult))
                p.emit("vector", stt(t_cb[:, :], t_v[:, :], -0.5, t_i[:, :],
                                     ALU.is_equal, ALU.mult))
                p.emit("vector", tt(t_xq[:, :], t_xq[:, :], t_ca[:, :], ALU.add))
                aq = p.emit("vector", tt(t_xq[:, :], t_xq[:, :], t_cb[:, :], ALU.subtract))
                g_xq.wrote(aq)
                for dc in range(NDC):
                    n = bk.take("b")
                    w = bk.transpose(n, bk.t[n][:, 0:128],
                                     t_xq[:, dc * 128:(dc + 1) * 128], c_idb[:, :],
                                     extra_after=[aq, t_idb])
                    g_xq.rd(w)
                    e = "scalar" if dc % 2 else "vector"
                    dst = oq_fm[:, dc * TPC + tt_ * 128: dc * TPC + (tt_ + 1) * 128]
                    fn = scp(dst, bk.t[n][:, 0:128]) if e == "scalar" else cp(dst, bk.t[n][:, 0:128])
                    bk.evac(n, e, fn)
                n = bk.take("f")
                w = bk.transpose(n, bk.t[n][0:1, 0:128],
                                 invso_c[:, tt_:tt_ + 1], c_idf[:, :],
                                 extra_after=[ivc, t_idf])
                ivr_toks.append(bk.evac(n, "vector",
                                        cp(invso_r[0:1, tt_ * 128:(tt_ + 1) * 128],
                                           bk.t[n][0:1, 0:128])))
            for cc in range(2):
                n = bk.take("f")
                mm = [(ones_rf[0:1, :], invso_r[0:1, cc * 512:(cc + 1) * 512])]
                bk.matmuls(n, bk.t[n][:, 0:512], mm, extra_after=ivr_toks + [t_ones])
                bk.evac(n, "vector", cp(invso_b[:, cc * 512:(cc + 1) * 512],
                                        bk.t[n][:, 0:512]))

        p.barrier()
        with ExitStack() as c2:
            sb2 = lambda n, s, d: c2.enter_context(nc.sbuf_tensor(n, s, d, allow_name_mangling=True))
            t_w = [sb2(f"t_w{i}", [128, NDC * BW], F32) for i in range(2)]
            t_b1 = sb2("t_b1", [128, NDC * BW], BF16)
            t_tern2 = [sb2(f"t_tern{i}", [128, NDC * BW], BF16) for i in range(2)]
            t_y = [sb2(f"t_y{i}", [128, 512], F32) for i in range(2)]
            g_w = [Guard(), Guard()]
            g_tern2 = [Guard(), Guard()]
            g_b1 = Guard()
            g_y = [Guard(), Guard()]
            isv = p.emit("vector", ts(invso_b[:, :], invso_b[:, :], t_sw[:, :], ALU.mult),
                         after=[t_nth])

            for blk in range(NBLK):
                wb = t_w[blk % 2]
                gw = g_w[blk % 2]
                tok = p.dma("sync", wb[:, :].rearrange("p (ec o) -> p ec o", ec=NDC),
                            wot[:, blk * BW:(blk + 1) * BW]
                            .rearrange("(ec p) o -> p ec o", p=128),
                            after=gw.war(), slot=f"w{blk % 2}")
                gw.wrote(tok)
                t_tern = t_tern2[blk % 2]
                g_tern = g_tern2[blk % 2]
                b1 = p.emit("vector", ts(t_b1[:, :], wb[:, :], t_nthr[:, :], ALU.is_lt),
                            after=[tok] + g_b1.war())
                g_b1.wrote(b1)
                gw.rd(b1)
                tern = p.emit("vector", stt(t_tern[:, :], wb[:, :], t_thr[:, :],
                                            t_b1[:, :], ALU.is_gt, ALU.subtract),
                              after=g_tern.war())
                gw.rd(tern)
                g_b1.rd(tern)
                g_tern.wrote(tern)
                for ob in range(2):
                    for cc_ in range(2):
                        n = bk.take("f")
                        mms = [(t_tern[:, dc * BW + ob * 128:dc * BW + (ob + 1) * 128],
                                oq_fm[:, dc * TPC + cc_ * 512:dc * TPC + cc_ * 512 + 512])
                               for dc in range(NDC)]
                        m = bk.matmuls(n, bk.t[n][:, 0:512], mms, extra_after=[tern])
                        g_tern.rd(m)
                        k = (ob * 2 + cc_) % 2
                        yt = bk.evac(n, "vector",
                                     tt(t_y[k][:, :], bk.t[n][:, 0:512],
                                        invso_b[:, cc_ * 512:cc_ * 512 + 512], ALU.mult),
                                     after=[isv] + g_y[k].war())
                        g_y[k].wrote(yt)
                        oc = blk * 2 + ob
                        sl = resid[:, oc * TPC + cc_ * 512: oc * TPC + cc_ * 512 + 512]
                        ra = p.emit("vector", tt(sl, sl, t_y[k][:, :], ALU.add),
                                    after=[yt, t_m1])
                        g_y[k].rd(ra)
                        resid_guard.rd(ra)
        cA.close()

        p.barrier()
        with ExitStack() as c2:
            sb2 = lambda n, s, d: c2.enter_context(nc.sbuf_tensor(n, s, d, allow_name_mangling=True))
            rb = sb2("rb", [128, NDC * TPC], BF16)
            rbsq = sb2("rbsq", [128, NDC * TPC], BF16)
            row = [sb2(f"row{i}", [1, TPC], F32) for i in range(3)]
            mb = sb2("mb", [128, TPC], F32)
            rsb = sb2("rsb", [128, TPC], F32)
            t_t1 = sb2("t_t1", [128, TPC], F32)
            g_t1 = Guard()

            cb0 = p.emit("scalar", scp(rb[:, :], resid[:, :]))
            sq0 = p.emit("scalar", act(rbsq[:, :], rb[:, :], AF.Square), after=[cb0])
            r0_toks, r1_toks = [], []
            for cc in range(2):
                n = bk.take("f")
                mms = [(ones_cb[:, :], rb[:, dc * TPC + cc * 512:dc * TPC + cc * 512 + 512])
                       for dc in range(NDC)]
                bk.matmuls(n, bk.t[n][0:1, 0:512], mms, extra_after=[cb0, t_ones])
                r0_toks.append(bk.evac(n, "vector", ts(row[0][0:1, cc * 512:(cc + 1) * 512],
                                                       bk.t[n][0:1, 0:512], 1.0 / D, ALU.mult)))
                n = bk.take("f")
                mms = [(ones_cb[:, :], rbsq[:, dc * TPC + cc * 512:dc * TPC + cc * 512 + 512])
                       for dc in range(NDC)]
                bk.matmuls(n, bk.t[n][0:1, 0:512], mms, extra_after=[sq0, t_ones])
                r1_toks.append(bk.evac(n, "vector", ts(row[1][0:1, cc * 512:(cc + 1) * 512],
                                                       bk.t[n][0:1, 0:512], 1.0 / D, ALU.mult)))
            sqm = p.emit("scalar", act(row[2][0:1, :], row[0][0:1, :], AF.Square),
                         after=r0_toks)
            vr = p.emit("vector", tt(row[1][0:1, :], row[1][0:1, :], row[2][0:1, :],
                                     ALU.subtract), after=[sqm] + r1_toks)
            p.emit("vector", ts(row[1][0:1, :], row[1][0:1, :], EPS, ALU.add))
            rc = p.emit("vector", recip(row[1][0:1, :], row[1][0:1, :]))
            rs = p.emit("scalar", act(row[1][0:1, :], row[1][0:1, :], AF.Sqrt), after=[rc])
            for cc in range(2):
                n = bk.take("f")
                bk.matmuls(n, bk.t[n][:, 0:512],
                           [(ones_rf[0:1, :], row[0][0:1, cc * 512:(cc + 1) * 512])],
                           extra_after=[rs, t_ones] + r0_toks)
                bk.evac(n, "vector", cp(mb[:, cc * 512:(cc + 1) * 512], bk.t[n][:, 0:512]))
                n = bk.take("f")
                bk.matmuls(n, bk.t[n][:, 0:512],
                           [(ones_rf[0:1, :], row[1][0:1, cc * 512:(cc + 1) * 512])],
                           extra_after=[rs, t_ones])
                mbt = bk.evac(n, "vector", cp(rsb[:, cc * 512:(cc + 1) * 512],
                                              bk.t[n][:, 0:512]))
            for dc in range(NDC):
                d1 = p.emit("vector", tt(t_t1[:, :], resid[:, dc * TPC:(dc + 1) * TPC],
                                         mb[:, :], ALU.subtract),
                            after=[mbt] + g_t1.war())
                g_t1.wrote(d1)
                d2 = p.emit("vector", tt(xn1_fm[:, dc * TPC:(dc + 1) * TPC],
                                         t_t1[:, :], rsb[:, :], ALU.mult))
                g_t1.rd(d2)

        p.barrier()
        with ExitStack() as c2:
            sb2 = lambda n, s, d: c2.enter_context(nc.sbuf_tensor(n, s, d, allow_name_mangling=True))
            t_wb = [sb2(f"t_wb{i}", [128, NDC * 512], BF16) for i in range(2)]
            t_wf = sb2("t_wf", [128, NDC * 512], F32)
            t_ev = [sb2(f"t_ev{i}", [128, 512], ev_dt) for i in range(4)]
            g_wb = [Guard(), Guard()]
            g_wf = Guard()
            g_ev = [Guard() for _ in range(4)]
            for blk in range(12):
                wb = t_wb[blk % 2]
                gw = g_wb[blk % 2]
                tokf = p.dma("sync", t_wf[:, :].rearrange("p (dc o) -> p dc o", dc=NDC),
                             wqkv[:, blk * 512:(blk + 1) * 512]
                             .rearrange("(dc p) o -> p dc o", p=128),
                             after=g_wf.war(), slot="wf")
                g_wf.wrote(tokf)
                tok = p.emit("vector", cp(wb[:, :], t_wf[:, :]),
                             after=[tokf] + gw.war())
                g_wf.rd(tok)
                gw.wrote(tok)
                if blk < 8:
                    for ob in range(4):
                        for cc_ in range(2):
                            n = bk.take("f")
                            mms = [(wb[:, dc * 512 + ob * 128:dc * 512 + (ob + 1) * 128],
                                    xn1_fm[:, dc * TPC + cc_ * 512:dc * TPC + cc_ * 512 + 512])
                                   for dc in range(NDC)]
                            m = bk.matmuls(n, bk.t[n][:, 0:512], mms, extra_after=[tok])
                            gw.rd(m)
                            k = (ob * 2 + cc_) % 4
                            ev = t_ev[k]
                            sc_ = ISQD if blk < 4 else 1.0
                            evt = bk.evac(n, "scalar",
                                          act(ev[:, :], bk.t[n][:, 0:512], AF.Copy,
                                              bias=0.0, scale=sc_),
                                          after=g_ev[k].war())
                            g_ev[k].wrote(evt)
                            if blk < 4:
                                h = blk * 4 + ob
                                rbase = (h // 4) * 1024 + (h % 4) * 128
                            else:
                                h = (blk - 4) * 4 + ob
                                rbase = (h // 4) * 1024 + 512 + (h % 4) * 128
                            dt_ = p.out_dma("sync",
                                            qk1_out[rbase:rbase + 128,
                                                    cc_ * 512:cc_ * 512 + 512],
                                            ev[:, :], after=[evt], slot=f"ev{k}")
                            g_ev[k].rd(dt_)
                else:
                    for tt_ in range(8):
                        n = bk.take("f")
                        mms = [(xn1_fm[:, dc * TPC + tt_ * 128:dc * TPC + (tt_ + 1) * 128],
                                wb[:, dc * 512:(dc + 1) * 512])
                               for dc in range(NDC)]
                        m = bk.matmuls(n, bk.t[n][:, 0:512], mms, extra_after=[tok])
                        gw.rd(m)
                        k = tt_ % 4
                        ev = t_ev[k]
                        evt = bk.evac(n, "scalar", scp(ev[:, :], bk.t[n][:, 0:512]),
                                      after=g_ev[k].war())
                        g_ev[k].wrote(evt)
                        rbase = (blk - 8) * 1024 + tt_ * 128
                        dt_ = p.out_dma("sync", v1_out[rbase:rbase + 128, :],
                                        ev[:, :], after=[evt], slot=f"ev{k}")
                        g_ev[k].rd(dt_)

        if qkv_done_cb is not None:
            qkv_done_cb(p)
        p.barrier()
        with ExitStack() as c2:
            sb2 = lambda n, s, d: c2.enter_context(nc.sbuf_tensor(n, s, d, allow_name_mangling=True))
            gu_sb = sb2("gu_sb", [128, NHC * 512], BF16)
            t_gs = sb2("t_gs", [128, 512], BF16)
            trow = sb2("trow", [1, 512], BF16)
            t2row = sb2("t2row", [1, 512], BF16)
            g_gs, g_trow, g_t2, g_gu = Guard(), Guard(), Guard(), Guard()
            down_drain = []

            for th in range(2):
                n = bk.take("f")
                mms = [(ones_cb[:, :], xn1_fm[:, dc * TPC + th * 512:dc * TPC + th * 512 + 512])
                       for dc in range(NDC)]
                bk.matmuls(n, bk.t[n][0:1, 0:512], mms, extra_after=[t_ones] + g_trow.war())
                ttr = bk.evac(n, "vector", cp(trow[0:1, :], bk.t[n][0:1, 0:512]))
                g_trow.wrote(ttr)
                gu_toks = []
                with ExitStack() as c3:
                    sb3 = lambda n_, s_, d_: c3.enter_context(
                        nc.sbuf_tensor(n_, s_, d_, allow_name_mangling=True))
                    stg = sb3("stg", [128, 4 * 2048], BF16)
                    stu = sb3("stu", [128, 4 * 2048], BF16)
                    t_wi = [sb3(f"t_wi{i}", [128, 2048], I32) for i in range(2)]
                    g_wi = [Guard(), Guard()]
                    g_q = [Guard() for _ in range(8)]
                    witer = 0
                    for hb in range(11):
                        cvt = {}
                        for wsel, (wt, stage) in enumerate(((wqg_t, stg), (wqu_t, stu))):
                            for q4 in range(4):
                                wi_ = t_wi[witer % 2]
                                gwi = g_wi[witer % 2]
                                witer += 1
                                tok = p.dma("sync",
                                            wi_[:, :].rearrange("p (dc o) -> p dc o", dc=NDC),
                                            wt[:, hb * 512 + q4 * 128: hb * 512 + (q4 + 1) * 128]
                                            .rearrange("(dc p) o -> p dc o", p=128),
                                            after=gwi.war() + (down_drain if hb == 0 else []),
                                            slot=f"wi{witer % 2}")
                                gwi.wrote(tok)
                                eng = "scalar" if wsel == 0 else "vector"
                                cv = p.emit(eng,
                                            (scp if eng == "scalar" else cp)(
                                                stage[:, q4 * 2048:(q4 + 1) * 2048], wi_[:, :]),
                                            after=[tok] + g_q[wsel * 4 + q4].war())
                                gwi.rd(cv)
                                g_q[wsel * 4 + q4].wrote(cv)
                                cvt[(wsel, q4)] = cv
                        for hc4 in range(4):
                            hc = hb * 4 + hc4
                            n = bk.take("f")
                            mms = [(stg[:, hc4 * 2048 + dc * 128: hc4 * 2048 + (dc + 1) * 128],
                                    xn1_fm[:, dc * TPC + th * 512:dc * TPC + th * 512 + 512])
                                   for dc in range(NDC)]
                            mms.append((m128_r[0:1, :], trow[0:1, :]))
                            mg = bk.matmuls(n, bk.t[n][:, 0:512], mms,
                                            extra_after=[cvt[(0, hc4)], ttr, t_ones])
                            g_q[hc4].rd(mg)
                            g_trow.rd(mg)
                            gs = bk.evac(n, "scalar",
                                         act(t_gs[:, :], bk.t[n][:, 0:512], AF.Silu,
                                             scale=sg[:, hc:hc + 1]),
                                         after=[t_scl] + g_gs.war())
                            g_gs.wrote(gs)
                            n2 = bk.take("f")
                            mms = [(stu[:, hc4 * 2048 + dc * 128: hc4 * 2048 + (dc + 1) * 128],
                                    xn1_fm[:, dc * TPC + th * 512:dc * TPC + th * 512 + 512])
                                   for dc in range(NDC)]
                            mms.append((m128_r[0:1, :], trow[0:1, :]))
                            mu = bk.matmuls(n2, bk.t[n2][:, 0:512], mms,
                                            extra_after=[cvt[(1, hc4)], ttr, t_ones])
                            g_q[4 + hc4].rd(mu)
                            g_trow.rd(mu)
                            gut = bk.evac(n2, "vector",
                                          stt(gu_sb[:, hc * 512:(hc + 1) * 512],
                                              bk.t[n2][:, 0:512], su[:, hc:hc + 1],
                                              t_gs[:, :], ALU.mult, ALU.mult),
                                          after=[gs, t_scl] + g_gu.war())
                            g_gs.rd(gut)
                            gu_toks.append(gut)
                g_gu.wrote(gu_toks[-1])
                stage_drain = []
                for gq in g_q:
                    stage_drain += gq.readers + ([gq.writer] if gq.writer else [])
                n = bk.take("f")
                mms = [(ones_cb[:, :], gu_sb[:, hc * 512:(hc + 1) * 512])
                       for hc in range(NHC)]
                bk.matmuls(n, bk.t[n][0:1, 0:512], mms,
                           extra_after=gu_toks + [t_ones] + g_t2.war())
                tt2 = bk.evac(n, "vector", cp(t2row[0:1, :], bk.t[n][0:1, 0:512]))
                g_t2.wrote(tt2)
                with ExitStack() as c3:
                    sb3 = lambda n_, s_, d_: c3.enter_context(
                        nc.sbuf_tensor(n_, s_, d_, allow_name_mangling=True))
                    std2 = [sb3(f"std{i}", [128, NHC * 128], BF16) for i in range(2)]
                    wdi = sb3("wdi", [128, NHC * 128], I32)
                    g_sd = [Guard(), Guard()]
                    g_di = Guard()
                    for oc in range(NDC):
                        std_ = std2[oc % 2]
                        td = p.dma("sync", wdi[:, :].rearrange("p (hc o) -> p hc o", hc=NHC),
                                   wqd_t[:, oc * 128:(oc + 1) * 128]
                                   .rearrange("(hc p) o -> p hc o", p=128),
                                   after=g_di.war() + (stage_drain if oc == 0 else []),
                                   slot="wd0")
                        g_di.wrote(td)
                        cv = p.emit("scalar" if oc % 2 else "vector",
                                    (scp if oc % 2 else cp)(std_[:, :], wdi[:, :]),
                                    after=[td] + g_sd[oc % 2].war())
                        g_di.rd(cv)
                        g_sd[oc % 2].wrote(cv)
                        n = bk.take("f")
                        mms = [(std_[:, hc * 128:(hc + 1) * 128],
                                gu_sb[:, hc * 512:(hc + 1) * 512]) for hc in range(NHC)]
                        mms.append((m128_r[0:1, :], t2row[0:1, :]))
                        md = bk.matmuls(n, bk.t[n][:, 0:512], mms,
                                        extra_after=[cv, tt2, t_ones] + gu_toks)
                        g_sd[oc % 2].rd(md)
                        g_t2.rd(md)
                        g_gu.rd(md)
                        sl = resid[:, oc * TPC + th * 512: oc * TPC + th * 512 + 512]
                        ra = bk.evac(n, "vector",
                                     stt(sl, bk.t[n][:, 0:512], sd[:, oc:oc + 1],
                                         sl, ALU.mult, ALU.add), after=[t_scl])
                        resid_guard.rd(ra)
                        down_drain.append(md)

        if not fused:
            for dc in range(NDC):
                p.out_dma("sync", m2_out[dc * 128:(dc + 1) * 128, :],
                          resid[:, dc * TPC:(dc + 1) * TPC],
                          after=resid_guard.readers[-40:])



def build_fin():
    p = Prog()
    nc = p.nc
    io = {}
    io["o1"] = nc.dram_tensor("o1", [D, TPC], F32, kind="ExternalInput").ap()
    io["m2_in"] = nc.dram_tensor("m2_in", [D, TPC], F32, kind="ExternalInput").ap()
    io["w_out"] = nc.dram_tensor("w_out", [D, D], F32, kind="ExternalInput").ap()
    io["y_out"] = nc.dram_tensor("y_out", [D, TPC], F32, kind="ExternalOutput").ap()
    with ExitStack() as ctx:
        resid = ctx.enter_context(p.nc.sbuf_tensor("resid", [128, NDC * TPC], F32))
        bk = Banks(p, ctx, nf32=6, nbf=0)
        emit_fin(p, ctx, bk, io, resid, fused=False, o1_loader=None)
        p.render(ctx)
    return p.nc


def emit_fin(p, ctx, bk, io, resid, fused, o1_loader):
    nc = p.nc
    w_out = io["w_out"]
    y_out = io["y_out"]
    if True:
        sb = lambda n, s, d: ctx.enter_context(nc.sbuf_tensor(n, s, d, allow_name_mangling=True))
        o1_fm = sb("o1_fm", [128, NDC * TPC], BF16)
        t_wb = [sb(f"t_wb{i}", [128, NDC * 512], BF16) for i in range(2)]
        g_wb = [Guard(), Guard()]
        resid_guard = Guard()

        if not fused:
            t_m2 = p.dma("gpsimd", resid[:, :].rearrange("p (dc t) -> p dc t", dc=NDC),
                         io["m2_in"].rearrange("(dc p) t -> p dc t", p=128), slot="m2")
            resid_guard.wrote(t_m2)
        t_m2 = resid_guard.writer
        if o1_loader is None:
            t_o1 = p.dma("gpsimd", o1_fm[:, :].rearrange("p (dc t) -> p dc t", dc=NDC),
                         io["o1"].rearrange("(dc p) t -> p dc t", p=128), slot="o1")
        else:
            t_o1 = o1_loader(p, o1_fm)

        for blk in range(4):
            wb = t_wb[blk % 2]
            gw = g_wb[blk % 2]
            tok = p.dma("gpsimd", wb[:, :].rearrange("p (dc o) -> p dc o", dc=NDC),
                        w_out[:, blk * 512:(blk + 1) * 512]
                        .rearrange("(dc p) o -> p dc o", p=128),
                        after=gw.war(), slot=f"wb{blk % 2}")
            gw.wrote(tok)
            for ob in range(4):
                for cc_ in range(2):
                    n = bk.take("f")
                    mms = [(wb[:, dc * 512 + ob * 128:dc * 512 + (ob + 1) * 128],
                            o1_fm[:, dc * TPC + cc_ * 512:dc * TPC + cc_ * 512 + 512])
                           for dc in range(NDC)]
                    m = bk.matmuls(n, bk.t[n][:, 0:512], mms, extra_after=[tok, t_o1])
                    gw.rd(m)
                    oc = blk * 4 + ob
                    sl = resid[:, oc * TPC + cc_ * 512: oc * TPC + cc_ * 512 + 512]
                    ra = bk.evac(n, "vector", tt(sl, sl, bk.t[n][:, 0:512], ALU.add),
                                 after=[t_m2])
                    resid_guard.rd(ra)

        for dc in range(NDC):
            p.out_dma("sync", y_out[dc * 128:(dc + 1) * 128, :],
                      resid[:, dc * TPC:(dc + 1) * TPC],
                      after=resid_guard.readers)



GROUPS = [[0, 1, 2, 3], [4, 5, 6, 7]]


def build_fused(dbg=False):
    p = Prog()
    nc = p.nc
    io = pre0_dram(nc, inter=True)
    io.update(mid_dram(nc, inter=True))
    io["idf"] = io["idf"] if "idf" in io else None
    io["maskin"] = nc.dram_tensor("maskin", [512, 512], BF16, kind="ExternalInput").ap()
    io["w_out"] = nc.dram_tensor("w_out", [D, D], F32, kind="ExternalInput").ap()
    io["y_out"] = nc.dram_tensor("y_out", [D, TPC], F32, kind="ExternalOutput").ap()
    io["gidx"] = nc.dram_tensor("gidx", [128, 176], mybir.dt.uint32, kind="ExternalInput").ap()
    aqk = nc.dram_tensor("aqk", [4 * 4096, TPC], BF16).ap()
    av = nc.dram_tensor("av", [4 * 4096, 512], BF16).ap()
    o_c = nc.dram_tensor("o_c", [4096, 512], F32)
    ao = nc.dram_tensor("ao", [4 * 4096, 512], F32).ap()
    aqk1 = nc.dram_tensor("aqk1", [4 * 4096, TPC], BF16).ap()
    av1 = nc.dram_tensor("av1", [4 * 4096, 512], BF16).ap()
    o1_c = nc.dram_tensor("o1_c", [512, S], BF16)
    o1b = [nc.dram_tensor(f"o1b_{pr}", [128, S], BF16).ap() for pr in range(4)]
    ao1p = [nc.dram_tensor(f"ao1_{pr}", [4 * 128, S], BF16).ap() for pr in range(4)]
    o_c_ap = o_c.ap()
    o1_c_ap = o1_c.ap()

    with ExitStack() as ctx:
        resid = ctx.enter_context(nc.sbuf_tensor("resid", [128, NDC * TPC], F32))
        sb_gidx = ctx.enter_context(nc.sbuf_tensor("sb_gidx", [128, 176], mybir.dt.uint32))
        bk = Banks(p, ctx, nf32=6, nbf=2)
        t_gidx = p.dma("gpsimd", sb_gidx[:, :], io["gidx"], slot="gidx")

        def coll(name, cin, cout, after):
            return p.emit("gpsimd",
                          (lambda e, i=cin, o=cout:
                           e.collective_compute("AllGather", ALU.bypass,
                                                replica_groups=GROUPS,
                                                ins=[i.opt()], outs=[o.opt()])),
                          after=after, inc=1, sem="cc")

        ccd = {}

        def pre0_cb(p_):
            evtoks = [p_.slot_last.get(f"ds_ev{k}") for k in range(4)]
            ccd["qk"] = coll("qk", io["qk_out"], aqk, evtoks)
            ccd["v"] = coll("v", io["v_out"], av, [ccd["qk"]])

        with ExitStack() as stx:
            emit_pre0(p, stx, bk, io, resid, fused=True, qkv_done_cb=pre0_cb)
        cc_qk = ccd["qk"]
        cc_v = ccd["v"]
        p.barrier()

        aio = {"maskin": io["maskin"], "idf": io["idf"], "o_out": o_c_ap}

        def loader0(p_, sb_q, sb_k, sb_v, sb_m, c_idf):
            p_.dma("sync", sb_m[:, :].rearrange("p (ms q) -> p ms q", ms=4),
                   io["maskin"].rearrange("(ms p) q -> p ms q", p=128), slot="ldm")
            p_.dma("sync", c_idf[:, :], io["idf"], slot="ldm")
            qk_toks, v_toks = [], []
            for pr in range(4):
                for j in range(4):
                    p_.igather(sb_q[:, pr * S + j * 1024: pr * S + (j + 1) * 1024],
                               aqk, sb_gidx[:, pr * 4 + j: pr * 4 + j + 1],
                               after=[cc_qk, t_gidx], slot=f"ld{pr}")
                    t = p_.igather(sb_k[:, pr * S + j * 1024: pr * S + (j + 1) * 1024],
                                   aqk, sb_gidx[:, 16 + pr * 4 + j: 16 + pr * 4 + j + 1],
                                   after=[cc_qk], slot=f"ld{pr}")
                qk_toks.append(t)
            for j in range(4):
                for tcs in range(8):
                    tc = j * 8 + tcs
                    t = p_.igather(sb_v[:, tc * 512:(tc + 1) * 512], av,
                                   sb_gidx[:, 32 + tc: 32 + tc + 1],
                                   after=[cc_v], slot=f"lv{j}")
                v_toks.append(t)
            return (qk_toks, v_toks)

        with ExitStack() as stx:
            emit_attn(p, stx, bk, aio, True, loader=loader0, o_dt=F32)
        cc_o = coll("o", o_c_ap, ao, [p.slot_last.get("ds_od")])
        p.barrier()

        def o_loader(p_, xb, tt_, after):
            t = None
            for r in range(4):
                t = p_.igather(xb[:, r * 512:(r + 1) * 512], ao,
                               sb_gidx[:, 64 + tt_ * 4 + r: 64 + tt_ * 4 + r + 1],
                               after=[cc_o, t_gidx] + list(after), slot=f"ox{tt_ % 2}")
            return t

        def mid_cb(p_):
            evtoks = [p_.slot_last.get(f"ds_ev{k}") for k in range(4)]
            ccd["qk1"] = coll("qk1", io["qk1_out"], aqk1, evtoks)
            ccd["v1"] = coll("v1", io["v1_out"], av1, [ccd["qk1"]])

        with ExitStack() as stx:
            emit_mid(p, stx, bk, io, resid, fused=True, o_loader=o_loader,
                     ev_dt=BF16, qkv_done_cb=mid_cb)
        cc_qk1 = ccd["qk1"]
        cc_v1 = ccd["v1"]
        p.barrier()

        aio1 = {"maskin": io["maskin"], "idf": io["idf"], "o_out": o1_c_ap}

        def loader1(p_, sb_q, sb_k, sb_v, sb_m, c_idf):
            p_.dma("sync", sb_m[:, :].rearrange("p (ms q) -> p ms q", ms=4),
                   io["maskin"].rearrange("(ms p) q -> p ms q", p=128), slot="ldm")
            p_.dma("sync", c_idf[:, :], io["idf"], slot="ldm")
            qk_toks, v_toks = [], []
            for pr in range(4):
                for j in range(4):
                    p_.igather(sb_q[:, pr * S + j * 1024: pr * S + (j + 1) * 1024],
                               aqk1, sb_gidx[:, 96 + pr * 4 + j: 96 + pr * 4 + j + 1],
                               after=[cc_qk1], slot=f"ld{pr}")
                    t = p_.igather(sb_k[:, pr * S + j * 1024: pr * S + (j + 1) * 1024],
                                   aqk1, sb_gidx[:, 112 + pr * 4 + j: 112 + pr * 4 + j + 1],
                                   after=[cc_qk1], slot=f"ld{pr}")
                qk_toks.append(t)
            for j in range(4):
                for tcs in range(8):
                    tc = j * 8 + tcs
                    t = p_.igather(sb_v[:, tc * 512:(tc + 1) * 512], av1,
                                   sb_gidx[:, 128 + tc: 128 + tc + 1],
                                   after=[cc_v1], slot=f"lv{j}")
                v_toks.append(t)
            return (qk_toks, v_toks)

        cco1 = {}

        def pair_cb(pr):
            tb = p.dma("gpsimd", o1b[pr], o1_c_ap[pr * 128:(pr + 1) * 128, :],
                       after=[p.slot_last.get("ds_od")], slot="o1b")
            cco1[pr] = coll(f"o1_{pr}", o1b[pr], ao1p[pr], [tb])

        with ExitStack() as stx:
            emit_attn(p, stx, bk, aio1, False, loader=loader1, o_dt=BF16,
                      pair_done_cb=pair_cb)
        p.barrier()

        ao1v = [a.rearrange("r (j t) -> (r j) t", j=4) for a in ao1p]

        def o1_loader(p_, o1_fm):
            t = None
            for dc in range(NDC):
                t = p_.igather(o1_fm[:, dc * TPC:(dc + 1) * TPC], ao1v[dc % 4],
                               sb_gidx[:, 160 + dc: 160 + dc + 1],
                               after=[cco1[dc % 4]], slot="o1")
            return t

        fio = {"w_out": io["w_out"], "y_out": io["y_out"]}
        with ExitStack() as stx:
            emit_fin(p, stx, bk, fio, resid, fused=True, o1_loader=o1_loader)

        if dbg:
            p.barrier()
            for nm, ap in (("qk_out", io["qk_out"]), ("v_out", io["v_out"]),
                           ("aqk", aqk), ("o_c", o_c_ap), ("ao", ao),
                           ("qk1_out", io["qk1_out"]), ("v1_out", io["v1_out"]),
                           ("o1_c", o1_c_ap),):
                dt_ = ap.tensor.dtype if hasattr(ap, "tensor") else BF16
                dbg_t = nc.dram_tensor("dbg_" + nm, list(ap.shape), dt_,
                                       kind="ExternalOutput").ap()
                p.out_dma("gpsimd", dbg_t, ap, slot="dbg")
        p.render(ctx)
    return p.nc


def host_gidx(c):
    j0 = c % 4
    idx = np.zeros((176, 128), np.uint64)
    pp = np.arange(128, dtype=np.uint64)
    for pr in range(4):
        for j in range(4):
            idx[pr * 4 + j] = j * 4096 + j0 * 1024 + pr * 128 + pp
            idx[16 + pr * 4 + j] = j * 4096 + j0 * 1024 + 512 + pr * 128 + pp
            idx[96 + pr * 4 + j] = j * 4096 + j0 * 1024 + pr * 128 + pp
            idx[112 + pr * 4 + j] = j * 4096 + j0 * 1024 + 512 + pr * 128 + pp
    for tc in range(32):
        idx[32 + tc] = (tc // 8) * 4096 + j0 * 1024 + (tc % 8) * 128 + pp
        idx[128 + tc] = (tc // 8) * 4096 + j0 * 1024 + (tc % 8) * 128 + pp
    for tt_ in range(8):
        for r in range(4):
            idx[64 + tt_ * 4 + r] = r * 4096 + j0 * 1024 + tt_ * 128 + pp
    for dc in range(NDC):
        idx[160 + dc] = ((dc // 4) * 128 + pp) * 4 + j0
    return np.ascontiguousarray(idx.T.astype(np.uint32))


def host_inputs_fused(inputs):
    pre = host_inputs_pre0(inputs)
    mid = host_mid_inputs(inputs, [np.zeros((4096, 512), np.float32)] * 8,
                          [np.zeros((D, TPC), np.float32)] * 8)
    w_out = np.ascontiguousarray(np.asarray(inputs["L1_attn_w_out"], np.float32))
    mask = np.tril(np.ones((512, 512), np.float32)).T.astype(BF)
    maps = []
    for c in range(NCORE):
        m = dict(pre[c])
        for kk in ("wot", "wqkv", "wqg_t", "wqu_t", "wqd_t", "sg", "su", "sd"):
            m[kk] = mid[c][kk]
        m["w_out"] = w_out
        m["maskin"] = mask
        m["gidx"] = host_gidx(c)
        maps.append(m)
    return maps



def _esel():
    e = np.zeros((8, 512), np.float32)
    for ec in range(4):
        for pp in range(128):
            e[2 * ec + pp // 64, ec * 128 + pp] = 1.0
    return e.astype(BF)


def host_inputs_pre0(inputs):
    x = np.ascontiguousarray(np.asarray(inputs["L0_attn_x"], np.float32).reshape(B * S, D))
    shared = {
        "wqt": np.ascontiguousarray(np.asarray(inputs["L0_attn_w_q"], np.float32).T),
        "wkt": np.ascontiguousarray(np.asarray(inputs["L0_attn_w_k"], np.float32).T),
        "wvt": np.ascontiguousarray(np.asarray(inputs["L0_attn_w_v"], np.float32).T),
        "a_dt": np.ascontiguousarray(np.asarray(inputs["L0_ffn_A"], np.float32)
                                     .transpose(2, 0, 1).reshape(D, 512)),
        "b_allt": np.ascontiguousarray(np.asarray(inputs["L0_ffn_B"], np.float32)
                                       .transpose(0, 2, 1).reshape(512, D)),
        "wrt": np.ascontiguousarray(np.asarray(inputs["L0_ffn_Wr"], np.float32).T),
        "br": np.ascontiguousarray(np.asarray(inputs["L0_ffn_br"], np.float32).reshape(1, 8)),
        "esel": _esel(),
        "idf": np.eye(128, dtype=np.float32),
        "idb": np.eye(128).astype(BF),
    }
    return [dict(shared, x=np.ascontiguousarray(x[c * TPC:(c + 1) * TPC])) for c in range(NCORE)]




def host_attn_inputs(qk_list, v_list, l0):
    mask = np.tril(np.ones((512, 512), np.float32)).T.astype(BF)
    idf = np.eye(128, dtype=np.float32)
    maps = []
    for a in range(NCORE):
        b, hg = a // 4, a % 4
        qin = np.concatenate([qk_list[4 * b + j][hg * 1024: hg * 1024 + 512, :]
                              for j in range(4)], axis=1)
        kin = np.concatenate([qk_list[4 * b + j][hg * 1024 + 512: hg * 1024 + 1024, :]
                              for j in range(4)], axis=1)
        vin = np.concatenate([v_list[4 * b + j][hg * 1024:(hg + 1) * 1024, :]
                              for j in range(4)], axis=0)
        maps.append(dict(qin=np.ascontiguousarray(qin), kin=np.ascontiguousarray(kin),
                         vin=np.ascontiguousarray(vin), maskin=mask, idf=idf))
    return maps


def host_mid_inputs(inputs, att0_list, m1_list):
    shared = {
        "wot": np.ascontiguousarray(np.asarray(inputs["L0_attn_w_o"], np.float32).T),
        "wqkv": np.ascontiguousarray(np.asarray(inputs["L1_attn_w_qkv"], np.float32)),
        "wqg_t": np.ascontiguousarray(np.asarray(inputs["L1_ffn_wq_gate"], np.int32).T),
        "wqu_t": np.ascontiguousarray(np.asarray(inputs["L1_ffn_wq_up"], np.int32).T),
        "wqd_t": np.ascontiguousarray(np.asarray(inputs["L1_ffn_wq_down"], np.int32).T),
        "sg": np.ascontiguousarray(np.asarray(inputs["L1_ffn_s_gate"], np.float32)
                                   .reshape(NHC, 128).T),
        "su": np.ascontiguousarray(np.asarray(inputs["L1_ffn_s_up"], np.float32)
                                   .reshape(NHC, 128).T),
        "sd": np.ascontiguousarray(np.asarray(inputs["L1_ffn_s_down"], np.float32)
                                   .reshape(NDC, 128).T),
        "idf": np.eye(128, dtype=np.float32),
        "idb": np.eye(128).astype(BF),
    }
    maps = []
    for c in range(NCORE):
        b, j = c // 4, c % 4
        o_as = np.concatenate([att0_list[4 * b + hg][j * 1024:(j + 1) * 1024, :]
                               for hg in range(4)], axis=1)
        maps.append(dict(shared, o_as=np.ascontiguousarray(o_as),
                         m1_in=np.ascontiguousarray(m1_list[c])))
    return maps


def host_fin_inputs(inputs, att1_list, m2_list):
    w_out = np.ascontiguousarray(np.asarray(inputs["L1_attn_w_out"], np.float32))
    maps = []
    for c in range(NCORE):
        b, j = c // 4, c % 4
        o1 = np.concatenate([att1_list[4 * b + hg][:, j * 1024:(j + 1) * 1024]
                             for hg in range(4)], axis=0)
        maps.append(dict(w_out=w_out, o1=np.ascontiguousarray(o1),
                         m2_in=np.ascontiguousarray(m2_list[c])))
    return maps


def _run(nc, in_maps):
    res = run_bass_kernel_spmd(nc, in_maps, core_ids=list(range(NCORE)))
    return res.results


def kernel(**inputs):
    nc = build_fused()
    in_maps = host_inputs_fused(inputs)
    res = run_bass_kernel_spmd(nc, in_maps, core_ids=list(range(NCORE)))
    out = np.concatenate([np.asarray(r["y_out"]).T for r in res.results], axis=0)
    return np.ascontiguousarray(out.reshape(B, S, D).astype(np.float32))


def kernel_unfused(**inputs):
    r1 = _run(build_pre0(), host_inputs_pre0(inputs))
    qk_l = [np.asarray(r["qk_out"]) for r in r1]
    v_l = [np.asarray(r["v_out"]) for r in r1]
    m1_l = [np.asarray(r["m1_out"]) for r in r1]
    r2 = _run(build_attn(True), host_attn_inputs(qk_l, v_l, l0=True))
    att0 = [np.asarray(r["o_out"]) for r in r2]
    r3 = _run(build_mid(), host_mid_inputs(inputs, att0, m1_l))
    qk1 = [np.asarray(r["qk1_out"]) for r in r3]
    v1 = [np.asarray(r["v1_out"]) for r in r3]
    m2_l = [np.asarray(r["m2_out"]) for r in r3]
    r4 = _run(build_attn(False), host_attn_inputs(qk1, v1, l0=False))
    att1 = [np.asarray(r["o_out"]) for r in r4]
    r5 = _run(build_fin(), host_fin_inputs(inputs, att1, m2_l))
    out = np.concatenate([np.asarray(r["y_out"]).T for r in r5], axis=0)
    return np.ascontiguousarray(out.reshape(B, S, D).astype(np.float32))
